# version 08
# baseline (speedup 1.0000x reference)
import os
import numpy as np
import ml_dtypes
from contextlib import ExitStack

import concourse.bass as bass
import concourse.mybir as mybir
from concourse.alu_op_type import AluOpType as OP
from concourse.bass_utils import run_bass_kernel_spmd
from concourse.bass import IndirectOffsetOnAxis

F32 = mybir.dt.float32
BF16 = mybir.dt.bfloat16
I32 = mybir.dt.int32
U32 = mybir.dt.uint32
AF = mybir.ActivationFunctionType
AX = mybir.AxisListType
BFNP = ml_dtypes.bfloat16

NB = int(os.environ.get("P1NB", "2"))
P1ST = int(os.environ.get("P1ST", "5"))
P1SA = int(os.environ.get("P1SA", "4"))
T = 2048
D = 1024
DI = 2048
DS = 16
DTR = 64
NCHUNK = DI // 128
KCH = D // 128
TSUB = 1024
NSUB = T // TSUB
LN_EPS = 1e-5


class PB:

    ENGS = ("sync", "tensor", "vector", "scalar", "gpsimd")

    def __init__(self, nc, es, alias=False):
        self.nc = nc
        self.es = es
        self.recs = {e: [] for e in self.ENGS}
        self.sems = {}
        self.cnt = {}
        self.alias = alias
        self.lbl2phys = {}

    def _phys(self, label, default):
        if not self.alias:
            return label
        if label not in self.lbl2phys:
            self.lbl2phys[label] = default
        assert self.lbl2phys[label] == default, (label, default)
        return self.lbl2phys[label]

    def sem(self, name):
        if name not in self.sems:
            self.sems[name] = self.es.enter_context(self.nc.semaphore(name))
            self.cnt[name] = 0
        return self.sems[name]

    CHAIN_ENGS = ("vector", "scalar", "gpsimd")

    def _mark(self, label, sname, value):
        if not hasattr(self, "hist"):
            self.hist = {}
        self.hist.setdefault(label, []).append((sname, value))

    def wait(self, eng, label, val=None):
        if not hasattr(self, "hist"):
            self.hist = {}
        ents = self.hist.get(label, [])
        k = len(ents) if val is None else val
        if k <= 0 or not ents:
            return
        assert k <= len(ents), (label, k, len(ents))
        if hasattr(self, "qeng") and label in self.qeng:
            ename, ev = self._dma_cut(label)
            if eng != self.qeng[label]:
                e = self.sems[ename]
                self.recs[eng].append(
                    lambda E, e=e, ev=ev: E.wait_ge(e, ev))
            return
        sname, v = ents[k - 1]
        s = self.sem(sname)
        self.recs[eng].append(lambda E, s=s, v=v: E.wait_ge(s, v))

    def op(self, eng, build, sem=None, inc=1, dma16=False):
        if dma16:
            phys = self._phys(sem, "Q_" + eng)
            s = self.sem(phys)
            self.cnt[phys] += 16
            if not hasattr(self, "qeng"):
                self.qeng = {}
                self.echo_done = {}
                self.echo_cnt = {}
            self.qeng[sem] = eng
            self.recs[eng].append(
                lambda E, b=build, s=s: b(E).then_inc(s, 16))
            self._mark(sem, phys, self.cnt[phys])
            return
        if eng in self.CHAIN_ENGS:
            ch = "_ch_" + eng
            s = self.sem(ch)
            if not hasattr(self, "_last"):
                self._last = {}
            last = self._last.get(eng)
            if last is not None and last < self.cnt[ch]:
                pass
            if last is not None:
                self.recs[eng].append(
                    lambda E, s=s, v=last: E.wait_ge(s, v))
            self.cnt[ch] += 1
            v = self.cnt[ch]
            self.recs[eng].append(
                lambda E, b=build, s=s: b(E).then_inc(s, 1))
            self._last[eng] = v
            if sem is not None:
                self._mark(sem, ch, v)
        else:
            if sem is not None:
                phys = self._phys(sem, "T_" + eng)
                s = self.sem(phys)
                self.cnt[phys] += inc
                self.recs[eng].append(
                    lambda E, b=build, s=s, i=inc: b(E).then_inc(s, i))
                self._mark(sem, phys, self.cnt[phys])
            else:
                self.recs[eng].append(lambda E, b=build: b(E))

    def dma(self, eng, out, in_, sem, **kw):
        phys = self._phys(sem, "Q_" + eng)
        s = self.sem(phys)
        self.cnt[phys] += 16
        if not hasattr(self, "qeng"):
            self.qeng = {}
            self.echo_done = {}
            self.echo_cnt = {}
        self.qeng[sem] = eng
        self.recs[eng].append(
            lambda E, o=out, i=in_, s=s, kw=kw:
                E.dma_start(out=o, in_=i, **kw).then_inc(s, 16))
        self._mark(sem, phys, self.cnt[phys])

    def _dma_cut(self, label):
        qe = self.qeng[label]
        phys = self._phys(label, "Q_" + qe)
        total = self.cnt[phys]
        if self.echo_done.get(phys) != total:
            s = self.sem(phys)
            e = self.sem("e_" + phys)
            self.recs[qe].append(lambda E, s=s, v=total: E.wait_ge(s, v))
            self.cnt["e_" + phys] += 1
            ev = self.cnt["e_" + phys]
            self.recs[qe].append(
                lambda E, e=e, ev=ev: E.nop().then_inc(e, 1))
            self.echo_done[phys] = total
            self.echo_cnt[phys] = ev
        return ("e_" + phys, self.echo_cnt[phys])

    def barrier(self, tag):
        if hasattr(self, "qeng"):
            for lab in list(self.qeng):
                self._dma_cut(lab)
        s = self.sem("BAR")
        self.cnt["BAR"] += 5
        v = self.cnt["BAR"]
        for e in self.ENGS:
            self.recs[e].append(lambda E, s=s: E.nop().then_inc(s, 1))
        for e in self.ENGS:
            self.recs[e].append(lambda E, s=s, v=v: E.wait_ge(s, v))

    def c(self, label):
        if not hasattr(self, "hist"):
            self.hist = {}
        return len(self.hist.get(label, []))

    def n_instr(self):
        return sum(len(v) for v in self.recs.values())

    def finalize(self):
        with self.nc.Block() as block:
            for eng_name in self.ENGS:
                recs = self.recs[eng_name]

                def run(E, recs=recs):
                    for r in recs:
                        r(E)

                getattr(block, eng_name)(run)


def sbt(es, nc, name, shape, dtype):
    return es.enter_context(nc.sbuf_tensor(name, shape, dtype))


def pst(es, nc, name, shape, dtype):
    return es.enter_context(nc.psum_tensor(name, shape, dtype))



def emit_p1(pb, es, nc, io):
    U8 = mybir.dt.uint8
    ovl = sbt(es, nc, "p1_ovl", [128, 160 * 1024], U8)

    def view(off_kb, ncols, dtype):
        size = ncols * (4 if dtype == F32 else 2 if dtype in (BF16,) else 1)
        return ovl.ap()[:, off_kb * 1024:off_kb * 1024 + size].bitcast(dtype)

    xbc = view(0, 32768, BF16)
    w_half = view(64, 16384, BF16)
    ln0T = view(96, 16384, BF16)
    bbc_all = view(64, 16384, BF16)
    cbc_all = view(96, 16384, BF16)
    scano = view(128, 16384, BF16)
    wout = view(0, 16384, BF16)
    aT = view(32, 16384, BF16)
    ygld = view(64, 8192, BF16)
    atok = view(80, 1024, BF16)
    lntok = view(128, 1024, BF16)
    xbf = view(130, 2064, F32)

    def bbc(s):
        return bbc_all[:, s * TSUB:(s + 1) * TSUB]

    def cbc(s):
        return cbc_all[:, s * TSUB:(s + 1) * TSUB]

    cpool = (io["cpool_sb"] if "cpool_sb" in io else sbt(es, nc, "cpool_sb", [128, 8], F32))
    ident = sbt(es, nc, "p1_ident", [128, 128], BF16)
    sel16 = sbt(es, nc, "p1_sel16", [48, 16 * 128], BF16)
    xpw = sbt(es, nc, "p1_xpw", [128, 16 * 96], BF16)
    dtw = sbt(es, nc, "p1_dtw", [64, DI], BF16)
    conv_w = sbt(es, nc, "p1_convw", [128, 16 * 4], F32)
    conv_b = sbt(es, nc, "p1_convb", [128, 16], F32)
    dt_b = sbt(es, nc, "p1_dtb", [128, 16], F32)
    a_sb = sbt(es, nc, "p1_a", [128, 16 * 16], F32)
    d_sb = sbt(es, nc, "p1_d", [128, 16], F32)

    x0t = sbt(es, nc, "p1_x0t", [128, D], F32)
    stat = sbt(es, nc, "p1_stat", [128, 12], F32)
    mv = sbt(es, nc, "p1_mv", [128, 4], F32)
    rstd = sbt(es, nc, "p1_rstd", [128, 2], F32)
    zscr = sbt(es, nc, "p1_zscr", [128, TSUB], F32)
    projb = sbt(es, nc, "p1_projb", [64, T], BF16)
    bcprj = sbt(es, nc, "p1_bcprj", [64, T], BF16)
    dtbf = sbt(es, nc, "p1_dtbf", [128, TSUB], BF16)
    dtu = sbt(es, nc, "p1_dtu", [128, TSUB], BF16)
    dabf = sbt(es, nc, "p1_dabf", [128, 2 * TSUB], BF16)
    dbubf = sbt(es, nc, "p1_dbubf", [128, TSUB], BF16)
    ytmp = sbt(es, nc, "p1_ytmp", [128, TSUB], BF16)
    skeep = sbt(es, nc, "p1_skeep", [128, 16 * 16], F32)
    ygt = sbt(es, nc, "p1_ygt", [128, TSUB], F32)
    siluz = sbt(es, nc, "p1_siluz", [128, TSUB], BF16)
    x1t = zscr

    psA = pst(es, nc, "p1_psA", [128, 2048], F32)
    psB = pst(es, nc, "p1_psB", [128, 1024], F32)
    psT = pst(es, nc, "p1_psT", [128, 2048], BF16)

    CVALS = [1e-5, 1.0, -0.5, 0.0, 0.5, 2.0, -1.0, 8.0]
    for i, v in enumerate(CVALS):
        nc.const_aps.register(F32, v, cpool[:, i:i + 1])
    pb.dma("sync", cpool[:], io["cpool"], "qs")
    pb.dma("gpsimd", ident[:], io["ident"], "qgp")
    pb.dma("gpsimd", sel16[:], io["sel16"], "qgp")
    pb.dma("gpsimd", xpw[:], io["xpw_t"], "qgp")
    pb.dma("gpsimd", dtw[0:64, :], io["L0_attn_dt_proj_w"], "qgp")
    pb.dma("sync", conv_w[:], io["convw_t"], "qs")
    pb.dma("sync", conv_b[:], io["convb_t"], "qs")
    pb.dma("sync", dt_b[:], io["dtb_t"], "qs")
    pb.dma("sync", a_sb[:], io["a_t"], "qs")
    pb.dma("sync", d_sb[:], io["d_t"], "qs")
    pb.wait("scalar", "qs")
    pb.wait("tensor", "qgp")
    pb.wait("vector", "qs")
    pb.wait("vector", "qgp")
    pb.op("scalar", lambda E: E.activation(a_sb[:], a_sb[:], AF.Exp), sem="act0")
    pb.wait("vector", "act0")
    pb.op("vector", lambda E: E.tensor_scalar(
        a_sb[:], a_sb[:], -1.0, None, OP.mult), sem="va0")
    for bi in range(NB):
        if P1ST < 1:
            continue
        for tt in range(16):
            x0v = x0t[:, 0:D]
            pb.wait("sync", "va_ln", pb.c("va_ln"))
            pb.dma("sync", x0v, io["x_in"][bi, tt * 128:(tt + 1) * 128, :], "qx")
            pb.wait("vector", "qx")
            pb.wait("vector", "mtr", pb.c("mtr"))
            pb.op("vector", lambda E, x=x0v: E.bn_stats(stat[:, 0:6], x[:, 0:512]))
            pb.op("vector", lambda E, x=x0v: E.bn_stats(stat[:, 6:12], x[:, 512:1024]))
            pb.op("vector", lambda E: E.bn_aggr(mv[:, 0:2], stat[:, 0:12]), sem="vs")
            if P1SA < 2:
                continue
            pb.wait("scalar", "vs")
            pb.op("scalar", lambda E: E.activation(
                rstd[:, 0:1], mv[:, 1:2], AF.Ln, bias=LN_EPS), sem="act")
            pb.wait("scalar", "act")
            pb.op("scalar", lambda E: E.activation(
                rstd[:, 1:2], rstd[:, 0:1], AF.Exp, scale=-0.5), sem="act")
            if P1SA < 3:
                continue
            pb.wait("vector", "act")
            pb.op("vector", lambda E, x=x0v: E.tensor_scalar(
                lntok[:], x, mv[:, 0:1], rstd[:, 1:2], OP.subtract, OP.mult),
                sem="va_ln")
            if P1SA < 4:
                continue
            pb.wait("tensor", "va_ln")
            for c in range(KCH):
                pslot = psT[:, (c % 2) * 1024:(c % 2) * 1024 + 128]
                pb.wait("tensor", "vtr", pb.c("vtr") - 1)
                pb.op("tensor", lambda E, p=pslot, c=c: E.transpose(
                    p, lntok[:, c * 128:(c + 1) * 128], ident[:]), sem="mtr")
                pb.wait("vector", "mtr")
                pb.op("vector", lambda E, p=pslot, c=c, tt=tt: E.tensor_copy(
                    ln0T[:, c * 2048 + tt * 128: c * 2048 + (tt + 1) * 128], p),
                    sem="vtr")

        if P1ST < 2:
            continue
        pb.wait("tensor", "vtr")
        pb.op("vector", lambda E: E.memset(xbf[:, 13:16], 0.0), sem="va0")
        for pj in range(2):
            if bi > 0 or pj > 0:
                pb.wait("gpsimd", "vG", pb.c("vG"))
                pb.wait("gpsimd", "mmA", pb.c("mmA"))
                pb.wait("gpsimd", "mmO", pb.c("mmO"))
                pb.wait("gpsimd", "va_f", pb.c("va_f"))
                pb.wait("gpsimd", "vx1", pb.c("vx1"))
                pb.wait("scalar", "qx1", pb.c("qx1"))
            for kc in range(KCH):
                pb.dma("gpsimd", w_half[:, kc * 2048:(kc + 1) * 2048],
                       io["L0_attn_in_proj_w"].rearrange(
                           "(k p) n -> k p n", p=128)[kc][:, pj * 2048:(pj + 1) * 2048],
                       "qw")
            pb.wait("tensor", "qw")
            for jj in range(16):
                pb.wait("tensor", "actA", pb.c("actA"))
                pb.wait("tensor", "actZ", pb.c("actZ"))
                for ns in range(4):
                    for kc in range(KCH):
                        pb.op("tensor", lambda E, jj=jj, ns=ns, kc=kc: E.matmul(
                            psA[:, ns * 512:(ns + 1) * 512],
                            w_half[:, kc * 2048 + jj * 128: kc * 2048 + (jj + 1) * 128],
                            ln0T[:, kc * 2048 + ns * 512: kc * 2048 + (ns + 1) * 512],
                            start=(kc == 0), stop=(kc == KCH - 1)),
                            sem=("mmA" if (ns == 3 and kc == KCH - 1) else None))
                if pj == 0:
                    c = jj
                    pb.wait("scalar", "mmA")
                    pb.wait("scalar", "vcv", pb.c("vcv"))
                    pb.op("scalar", lambda E: E.activation(
                        xbf[:, 16:16 + T], psA[:], AF.Copy, bias=0.0), sem="actA")
                    pb.wait("vector", "actA")
                    for hh in range(2):
                        o = hh * 1024
                        pb.wait("vector", "actX", pb.c("actX"))
                        pb.op("vector", lambda E, c=c, o=o: E.tensor_scalar(
                            ygt[:], xbf[:, 13 + o:13 + o + 1024],
                            conv_w[:, c * 4:c * 4 + 1], None, OP.mult))
                        pb.op("vector", lambda E, c=c, o=o: E.scalar_tensor_tensor(
                            siluz[:], xbf[:, 14 + o:14 + o + 1024],
                            conv_w[:, c * 4 + 1:c * 4 + 2], ygt[:], OP.mult, OP.add))
                        pb.op("vector", lambda E, c=c, o=o: E.scalar_tensor_tensor(
                            ygt[:], xbf[:, 15 + o:15 + o + 1024],
                            conv_w[:, c * 4 + 2:c * 4 + 3], siluz[:], OP.mult, OP.add))
                        pb.op("vector", lambda E, c=c, o=o: E.scalar_tensor_tensor(
                            siluz[:], xbf[:, 16 + o:16 + o + 1024],
                            conv_w[:, c * 4 + 3:c * 4 + 4], ygt[:], OP.mult, OP.add),
                            sem="vcv")
                        pb.wait("scalar", "vcv")
                        pb.op("scalar", lambda E, c=c, o=o: E.activation(
                            xbc[:, c * 2048 + o:c * 2048 + o + 1024], siluz[:],
                            AF.Silu, bias=conv_b[:, c:c + 1]), sem="actX")
                else:
                    c = jj
                    pb.wait("scalar", "mmA")
                    for hh in range(2):
                        o = hh * 1024
                        pb.wait("scalar", "qz", pb.c("qz"))
                        pb.op("scalar", lambda E, o=o: E.activation(
                            zscr[:], psA[:, o:o + 1024], AF.Copy, bias=0.0),
                            sem="actZ")
                        pb.wait("sync", "actZ")
                        pb.dma("sync", io["z_dram"][bi, c][:, o:o + 1024],
                               zscr[:], "qz")

        if P1ST < 3:
            continue
        pb.wait("tensor", "actX")
        pb.wait("tensor", "actZ", pb.c("actZ"))
        for ns in range(4):
            for grp, (lo, m) in enumerate([(0, 64), (64, 16), (80, 16)]):
                tgt = [psB[0:64, 0:512], psB[0:16, 512:1024], psA[0:16, 0:512]][grp]
                for kc in range(NCHUNK):
                    pb.op("tensor", lambda E, ns=ns, kc=kc, lo=lo, m=m, tgt=tgt:
                          E.matmul(
                              tgt, xpw[:, kc * 96 + lo:kc * 96 + lo + m],
                              xbc[:, kc * 2048 + ns * 512:
                                  kc * 2048 + (ns + 1) * 512],
                              start=(kc == 0), stop=(kc == NCHUNK - 1)),
                          sem=("mmP" if kc == NCHUNK - 1 else None))
            pb.wait("scalar", "mmP")
            pb.op("scalar", lambda E, ns=ns: E.activation(
                projb[0:64, ns * 512:(ns + 1) * 512],
                psB[0:64, 0:512], AF.Copy, bias=0.0), sem="actP")
            pb.wait("vector", "mmP")
            pb.op("vector", lambda E, ns=ns: E.tensor_copy(
                bcprj[0:16, ns * 512:(ns + 1) * 512], psB[0:16, 512:1024]), sem="vP")
            pb.op("vector", lambda E, ns=ns: E.tensor_copy(
                bcprj[32:48, ns * 512:(ns + 1) * 512], psA[0:16, 0:512]), sem="vP")
            pb.wait("tensor", "actP")
            pb.wait("tensor", "vP")

        if P1ST < 4:
            continue
        pb.op("vector", lambda E: E.memset(skeep[:], 0.0), sem="vmem")
        for tsub in range(NSUB):
            t0 = tsub * TSUB
            pb.wait("tensor", "actD", pb.c("actD"))
            pb.wait("tensor", "aB", pb.c("aB"))
            pb.wait("tensor", "vB", pb.c("vB"))
            for s in range(16):
                for h in range(2):
                    pb.op("tensor", lambda E, s=s, h=h, t0=t0: E.matmul(
                        psB[:, h * 512:(h + 1) * 512],
                        sel16[0:16, s * 128:(s + 1) * 128],
                        bcprj[0:16, t0 + h * 512:t0 + (h + 1) * 512],
                        start=True, stop=True), sem=None)
                    pb.op("tensor", lambda E, s=s, h=h, t0=t0: E.matmul(
                        psA[:, h * 512:(h + 1) * 512],
                        sel16[32:48, s * 128:(s + 1) * 128],
                        bcprj[32:48, t0 + h * 512:t0 + (h + 1) * 512],
                        start=True, stop=True),
                        sem=("mmB" if h == 1 else None))
                pb.wait("vector", "mmB")
                pb.op("vector", lambda E, s=s: E.tensor_copy(
                    bbc(s), psB[:, 0:TSUB]), sem="vB")
                pb.wait("scalar", "mmB")
                pb.op("scalar", lambda E, s=s: E.activation(
                    cbc(s), psA[:, 0:TSUB], AF.Copy, bias=0.0), sem="aB")
                pb.wait("tensor", "vB")
                pb.wait("tensor", "aB")
            for c in range(NCHUNK):
                for h in range(2):
                    tgt = psB[:, 512:1024] if h == 0 else psB[:, 0:512]
                    pb.op("tensor", lambda E, c=c, h=h, t0=t0, tgt=tgt: E.matmul(
                        tgt, dtw[0:64, c * 128:(c + 1) * 128],
                        projb[0:64, t0 + h * 512:t0 + (h + 1) * 512],
                        start=True, stop=True), sem=("mmD" if h == 1 else None))
                pb.wait("scalar", "mmD")
                pb.op("scalar", lambda E, c=c: E.activation(
                    dtbf[:, 0:512], psB[:, 512:1024], AF.Exp,
                    bias=dt_b[:, c:c + 1]), sem="actD")
                pb.op("scalar", lambda E, c=c: E.activation(
                    dtbf[:, 512:1024], psB[:, 0:512], AF.Exp,
                    bias=dt_b[:, c:c + 1]), sem="actD")
                pb.op("scalar", lambda E: E.activation(
                    dtbf[:], dtbf[:], AF.Ln, bias=1.0), sem="actD2")
                pb.wait("vector", "actD2")
                pb.op("vector", lambda E, c=c, t0=t0: E.tensor_tensor(
                    dtu[:], dtbf[:], xbc[:, c * 2048 + t0:c * 2048 + t0 + TSUB],
                    OP.mult), sem="vD")
                for s in range(16):
                    da_s = dabf[:, (s % 2) * TSUB:(s % 2 + 1) * TSUB]
                    pb.wait("scalar", "vS2", pb.c("vS2") - 1)
                    pb.op("scalar", lambda E, c=c, s=s, da=da_s: E.activation(
                        da, dtbf[:], AF.Exp,
                        scale=a_sb[:, c * 16 + s:c * 16 + s + 1]), sem="actS")
                    pb.op("vector", lambda E, s=s: E.tensor_tensor(
                        dbubf[:], dtu[:], bbc(s), OP.mult), sem="vS1")
                    pb.wait("vector", "actS")
                    pb.op("vector", lambda E, c=c, s=s, da=da_s:
                          E.tensor_tensor_scan(
                              scano[:, s * TSUB:(s + 1) * TSUB], da, dbubf[:],
                              skeep[:, c * 16 + s:c * 16 + s + 1],
                              OP.mult, OP.add), sem="vS2")
                    pb.wait("vector", "mty", pb.c("mty"))
                    if s == 0:
                        pb.wait("vector", "qyg", pb.c("qyg"))
                    pb.op("vector", lambda E, s=s: E.tensor_tensor(
                        ytmp[:], scano[:, s * TSUB:(s + 1) * TSUB], cbc(s),
                        OP.mult), sem="vS3")
                    pb.wait("tensor", "vS3")
                    for h in range(2):
                        pb.op("tensor", lambda E, s=s, h=h: E.matmul(
                            psA[:, 1024 + h * 512:1024 + (h + 1) * 512],
                            ident[:], ytmp[:, h * 512:(h + 1) * 512],
                            start=(s == 0), stop=(s == 15)),
                            sem=("mty" if h == 1 else None))
                pb.wait("scalar", "vS2", pb.c("vS2") - 1)
                pb.op("vector", lambda E, c=c: E.tensor_copy(
                    skeep[:, c * 16:(c + 1) * 16],
                    scano.rearrange("p (s t) -> p s t", t=TSUB)[:, :, TSUB - 1]),
                    sem="vK")
                pb.wait("sync", "qz")
                pb.wait("sync", "actG", pb.c("actG"))
                pb.dma("sync", zscr[:, 0:TSUB],
                       io["z_dram"][bi, c][:, t0:t0 + TSUB], "qzl")
                pb.wait("scalar", "qzl")
                pb.wait("scalar", "vG", pb.c("vG"))
                pb.op("scalar", lambda E: E.activation(
                    siluz[:], zscr[:, 0:TSUB], AF.Silu), sem="actG")
                pb.wait("vector", "mty")
                pb.wait("vector", "actG")
                pb.wait("vector", "qyg", pb.c("qyg"))
                pb.op("vector", lambda E, c=c, t0=t0: E.scalar_tensor_tensor(
                    ygt[:], xbc[:, c * 2048 + t0:c * 2048 + t0 + TSUB],
                    d_sb[:, c:c + 1], psA[:, 1024:2048], OP.mult, OP.add),
                    sem="vG")
                pb.op("vector", lambda E: E.tensor_tensor(
                    ytmp[:], ygt[:], siluz[:], OP.mult), sem="vG")
                pb.wait("sync", "vG")
                pb.dma("sync", io["yg_dram"][bi, c][:, t0:t0 + TSUB],
                       ytmp[:], "qyg")
                pb.wait("tensor", "vG")

        if P1ST < 5:
            continue
        pb.wait("gpsimd", "vG")
        for kc in range(NCHUNK):
            pb.dma("gpsimd", wout[:, kc * 1024:(kc + 1) * 1024],
                   io["L0_attn_out_proj_w"].rearrange(
                       "(k p) n -> k p n", p=128)[kc], "qwo")
        pb.wait("tensor", "qwo")
        pb.wait("sync", "vG")
        for ts4 in range(4):
            pb.wait("sync", "qyg")
            pb.wait("sync", "mmO", pb.c("mmO"))
            for c in range(NCHUNK):
                pb.dma("sync", ygld[:, c * 512:(c + 1) * 512],
                       io["yg_dram"][bi, c][:, ts4 * 512:(ts4 + 1) * 512],
                       "qygl")
            pb.wait("tensor", "qygl")
            for oc in range(8):
                pbank = psA[:, (oc % 2) * 512:(oc % 2) * 512 + 512]
                pb.wait("tensor", "vE", pb.c("vE") - 1)
                for kc in range(NCHUNK):
                    pb.op("tensor",
                          lambda E, oc=oc, kc=kc, p=pbank: E.matmul(
                              p, wout[:, kc * 1024 + oc * 128:
                                      kc * 1024 + (oc + 1) * 128],
                              ygld[:, kc * 512:(kc + 1) * 512],
                              start=(kc == 0), stop=(kc == NCHUNK - 1)),
                          sem=("mmO" if kc == NCHUNK - 1 else None))
                pb.wait("vector", "mmO")
                pb.op("vector",
                      lambda E, oc=oc, ts4=ts4, p=pbank: E.tensor_copy(
                          aT[:, oc * 2048 + ts4 * 512:
                             oc * 2048 + (ts4 + 1) * 512],
                          p), sem="vE")
        pb.wait("tensor", "vE")
        for tt in range(16):
            pb.wait("vector", "va_f", pb.c("va_f"))
            for c in range(8):
                pslot = psT[:, (c % 2) * 1024:(c % 2) * 1024 + 128]
                pb.wait("tensor", "vF", pb.c("vF") - 1)
                pb.op("tensor", lambda E, c=c, tt=tt, p=pslot: E.transpose(
                    p, aT[:, c * 2048 + tt * 128: c * 2048 + (tt + 1) * 128],
                    ident[:]), sem="mtr2")
                pb.wait("vector", "mtr2")
                pb.op("vector", lambda E, c=c, p=pslot: E.tensor_copy(
                    atok[:, c * 128:(c + 1) * 128], p), sem="vF")
            x0v = x0t[:, 0:D]
            pb.wait("sync", "vx1", pb.c("vx1"))
            pb.dma("sync", x0v, io["x_in"][bi, tt * 128:(tt + 1) * 128, :], "qx2")
            pb.op("vector", lambda E: E.bn_stats(stat[:, 0:6], atok[:, 0:512]))
            pb.op("vector", lambda E: E.bn_stats(stat[:, 6:12], atok[:, 512:1024]))
            pb.op("vector", lambda E: E.bn_aggr(mv[:, 2:4], stat[:, 0:12]), sem="vs2")
            pb.wait("scalar", "vs2")
            pb.op("scalar", lambda E: E.activation(
                rstd[:, 0:1], mv[:, 3:4], AF.Ln, bias=LN_EPS), sem="act2")
            pb.wait("scalar", "act2")
            pb.op("scalar", lambda E: E.activation(
                rstd[:, 1:2], rstd[:, 0:1], AF.Exp, scale=-0.5), sem="act2")
            pb.wait("vector", "act2")
            pb.wait("vector", "qx1", pb.c("qx1"))
            pb.op("vector", lambda E: E.tensor_scalar(
                x1t[:], atok[:], mv[:, 2:3], rstd[:, 1:2],
                OP.subtract, OP.mult), sem="va_f")
            pb.wait("vector", "qx2")
            pb.op("vector", lambda E, x=x0v: E.tensor_tensor(
                x1t[:], x1t[:], x, OP.add), sem="vx1")
            pb.wait("sync", "vx1")
            pb.dma("sync", io["x1_out"][bi, tt * 128:(tt + 1) * 128, :],
                   x1t[:], "qx1")

    pb.wait("sync", "qx1")
    pb.wait("sync", "qyg")


def prep_p1_weights(d):
    out = {}
    out["xpw_t"] = np.ascontiguousarray(
        d["L0_attn_x_proj_w"].reshape(16, 128, 96).transpose(1, 0, 2)
        .reshape(128, 16 * 96))
    out["convw_t"] = np.ascontiguousarray(
        d["L0_attn_conv_w"].reshape(16, 128, 4).transpose(1, 0, 2)
        .reshape(128, 64))
    out["convb_t"] = np.ascontiguousarray(
        d["L0_attn_conv_b"].reshape(16, 128).T)
    out["dtb_t"] = np.ascontiguousarray(
        d["L0_attn_dt_proj_b"].reshape(16, 128).T)
    out["a_t"] = np.ascontiguousarray(
        d["L0_attn_A_log"].reshape(16, 128, 16).transpose(1, 0, 2)
        .reshape(128, 256))
    out["d_t"] = np.ascontiguousarray(
        d["L0_attn_D_param"].reshape(16, 128).T)
    return out


def make_consts():
    ident = np.eye(128, dtype=np.float32)
    sel16 = np.zeros((48, 16 * 128), np.float32)
    for s in range(16):
        sel16[s, s * 128:(s + 1) * 128] = 1.0
        sel16[32 + s, s * 128:(s + 1) * 128] = 1.0
    cpool = np.zeros((128, 8), np.float32)
    for i, v in enumerate([1e-5, 1.0, -0.5, 0.0, 0.5, 2.0, -1.0, 8.0]):
        cpool[:, i] = v
    return {"ident": ident, "sel16": sel16, "cpool": cpool}


P1_WNAMES = [
    ("L0_attn_in_proj_w", [D, 2 * DI]), ("L0_attn_conv_w", [DI, 4]),
    ("L0_attn_conv_b", [DI]), ("L0_attn_x_proj_w", [DI, 96]),
    ("L0_attn_dt_proj_w", [DTR, DI]), ("L0_attn_dt_proj_b", [DI]),
    ("L0_attn_A_log", [DI, DS]), ("L0_attn_D_param", [DI]),
    ("L0_attn_out_proj_w", [DI, D]),
]


def build_p1():
    nc = bass.Bass("TRN2", target_bir_lowering=False)
    io = {}
    io["x_in"] = nc.dram_tensor("x_in", [NB, T, D], F32, kind="ExternalInput").ap()
    for nm, shp in P1_WNAMES:
        io[nm] = nc.dram_tensor(nm, shp, F32, kind="ExternalInput").ap()
    io["ident"] = nc.dram_tensor("ident", [128, 128], F32, kind="ExternalInput").ap()
    io["sel16"] = nc.dram_tensor("sel16", [48, 16 * 128], F32,
                                 kind="ExternalInput").ap()
    io["cpool"] = nc.dram_tensor("cpool", [128, 8], F32,
                                 kind="ExternalInput").ap()
    for nm, shp in [("xpw_t", [128, 16 * 96]), ("convw_t", [128, 64]),
                    ("convb_t", [128, 16]), ("dtb_t", [128, 16]),
                    ("a_t", [128, 256]), ("d_t", [128, 16])]:
        io[nm] = nc.dram_tensor(nm, shp, F32, kind="ExternalInput").ap()
    io["z_dram"] = nc.dram_tensor("z_dram", [NB, NCHUNK, 128, T], F32).ap()
    io["yg_dram"] = nc.dram_tensor("yg_dram", [NB, NCHUNK, 128, T], BF16).ap()
    io["x1_out"] = nc.dram_tensor("x1_out", [NB, T, D], F32,
                                  kind="ExternalOutput").ap()
    es = ExitStack()
    with es:
        pb = PB(nc, es)
        emit_p1(pb, es, nc, io)
        print("P1 instructions:", pb.n_instr())
        pb.finalize()
    return nc




KCAP = 1536
CGLOB = 8192.0


def emit_ln_tile(pb, mv, stat, rstd, src):
    pb.op("vector", lambda E: E.bn_stats(stat[:, 0:6], src[:, 0:512]))
    pb.op("vector", lambda E: E.bn_stats(stat[:, 6:12], src[:, 512:1024]))
    pb.op("vector", lambda E: E.bn_aggr(mv[:, 0:2], stat[:, 0:12]), sem="vs")
    pb.wait("scalar", "vs")
    pb.op("scalar", lambda E: E.activation(
        rstd[:, 0:1], mv[:, 1:2], AF.Ln, bias=LN_EPS), sem="act")
    pb.wait("scalar", "act")
    pb.op("scalar", lambda E: E.activation(
        rstd[:, 1:2], rstd[:, 0:1], AF.Exp, scale=-0.5), sem="act")


def emit_p2pre(pb, es, nc, io):
    U8 = mybir.dt.uint8
    ovl = sbt(es, nc, "p2_ovl", [128, 161 * 1024], U8)

    def view(off_kb, ncols, dtype):
        size = ncols * (4 if dtype == F32 else 2)
        return ovl.ap()[:, off_kb * 1024:off_kb * 1024 + size].bitcast(dtype)

    hT = view(0, 16384, BF16)
    qT = view(32, 16384, BF16)
    aT2 = view(32, 16384, BF16)
    kT = view(64, 16384, BF16)
    vg = view(96, 16640, BF16)
    pTa = view(129, 8192, BF16)
    pTb = view(145, 8192, BF16)
    ttmp = view(131, 1024, F32)
    x1tl = view(145, 1024, F32)
    x1btok = view(149, 1024, F32)
    htok = view(153, 1024, BF16)
    h2tok = view(155, 1024, BF16)
    h2Tp = view(157, 1024, BF16)
    atok2 = view(159, 1024, BF16)
    rowin = view(0, 1024, BF16)[0:1, :]
    m1T = view(64, 4096, F32)
    m2T = view(80, 4096, F32)
    lcum = view(96, 4096, F32)

    cpool = (io["cpool_sb"] if "cpool_sb" in io else sbt(es, nc, "p2_cpool", [128, 8], F32))
    ident = sbt(es, nc, "p2_ident", [128, 128], BF16)
    ones1 = sbt(es, nc, "p2_ones1", [1, 128], BF16)
    g1rep = sbt(es, nc, "p2_g1rep", [128, 1024], BF16)
    b1rep = sbt(es, nc, "p2_b1rep", [128, 1024], BF16)
    g2rep = sbt(es, nc, "p2_g2rep", [128, 1024], BF16)
    b2rep = sbt(es, nc, "p2_b2rep", [128, 1024], BF16)
    bvrep = sbt(es, nc, "p2_bvrep", [128, 1024], BF16)
    wqk = sbt(es, nc, "p2_wqk", [128, 2 * 8192], BF16)
    bqk = sbt(es, nc, "p2_bqk", [128, 16], F32)
    bo_t = sbt(es, nc, "p2_bo", [128, 8], F32)
    wr_sb = sbt(es, nc, "p2_wr", [128, 64], BF16)
    stat = sbt(es, nc, "p2_stat", [128, 12], F32)
    mv = sbt(es, nc, "p2_mv", [128, 4], F32)
    rstd = sbt(es, nc, "p2_rstd", [128, 4], F32)
    obf = sbt(es, nc, "p2_obf", [128, 512], BF16)
    recip = sbt(es, nc, "p2_recip", [1, 512], BF16)
    logit_sb = sbt(es, nc, "p2_logit", [128, 256], F32)
    mrow = sbt(es, nc, "p2_mrow", [128, 2], F32)
    mwork = sbt(es, nc, "p2_mwork", [128, 32], F32)
    mbf = sbt(es, nc, "p2_mbf", [128, 16], BF16)
    iota8 = sbt(es, nc, "p2_iota8", [128, 8], F32)
    cnt_sb = sbt(es, nc, "p2_cnt", [8, 2], F32)

    psA = pst(es, nc, "p2_psA", [128, 2048], F32)
    psB = pst(es, nc, "p2_psB", [128, 1024], F32)
    psT = pst(es, nc, "p2_psT", [128, 2048], BF16)

    CVALS = [1e-5, 1.0, -0.5, 0.0, 0.5, 2.0, -1.0, 8.0]
    for i, v in enumerate(CVALS):
        nc.const_aps.register(F32, v, cpool[:, i:i + 1])
    pb.dma("sync", cpool[:], io["cpool"], "qs")
    pb.wait("scalar", "qs")
    pb.wait("vector", "qs")
    pb.wait("tensor", "qs")
    pb.dma("gpsimd", ident[:], io["ident"], "qgp")
    pb.dma("gpsimd", bqk[:, 0:8], io["bq_t"], "qgp")
    pb.dma("gpsimd", bqk[:, 8:16], io["bk_t"], "qgp")
    pb.dma("gpsimd", bo_t[:], io["bo_t"], "qgp")
    pb.dma("gpsimd", wr_sb[:], io["wr_t"], "qgp")
    pb.wait("gpsimd", "qgp")
    pb.op("gpsimd", lambda E: E.iota(
        iota8[:], [[1, 8]], base=0, channel_multiplier=0,
        allow_small_or_imprecise_dtypes=True), sem="pio")
    pb.op("vector", lambda E: E.memset(ones1[:], 1.0), sem="vone")
    for nm, dst in [("L0_ffn_ln1_g", g1rep), ("L0_ffn_ln1_b", b1rep),
                    ("L0_ffn_ln2_g", g2rep), ("L0_ffn_ln2_b", b2rep),
                    ("L0_ffn_bv", bvrep)]:
        pb.wait("gpsimd", "mbc", pb.c("mbc"))
        pb.dma("gpsimd", rowin[:], io[nm], "qrow")
        pb.wait("tensor", "qrow")
        pb.wait("tensor", "vone")
        pb.wait("tensor", "vbc", pb.c("vbc"))
        for h in range(2):
            pb.op("tensor", lambda E, h=h: E.matmul(
                psB[:, h * 512:(h + 1) * 512], ones1[0:1, 0:128],
                rowin[0:1, h * 512:(h + 1) * 512], start=True, stop=True),
                sem=("mbc" if h else None))
        pb.wait("vector", "mbc")
        pb.op("vector", lambda E, dst=dst: E.tensor_copy(dst[:], psB[:, 0:1024]),
              sem="vbc")

    for bi in range(NB):
        for tt in range(16):
            pb.wait("sync", "vh", pb.c("vh"))
            pb.wait("sync", "vx1b", pb.c("vx1b"))
            pb.dma("sync", x1tl[:], io["x1_in"][bi, tt * 128:(tt + 1) * 128, :],
                   "qx")
            pb.wait("vector", "qx")
            pb.wait("vector", "vhcp", pb.c("vhcp"))
            emit_ln_tile(pb, mv, stat, rstd, x1tl[:])
            pb.wait("vector", "act")
            pb.op("vector", lambda E: E.tensor_tensor(
                rstd[:, 2:3], mv[:, 1:2], rstd[:, 1:2], OP.mult))
            pb.op("vector", lambda E: E.tensor_tensor(
                rstd[:, 2:3], rstd[:, 2:3], rstd[:, 1:2], OP.mult), sem="vv2")
            pb.wait("scalar", "vv2")
            pb.op("scalar", lambda E: E.activation(
                rstd[:, 3:4], rstd[:, 2:3], AF.Ln, bias=LN_EPS), sem="acA")
            pb.wait("scalar", "acA")
            pb.op("scalar", lambda E: E.activation(
                rstd[:, 3:4], rstd[:, 3:4], AF.Exp, scale=-0.5), sem="acB")
            pb.wait("vector", "acB")
            pb.op("vector", lambda E: E.tensor_tensor(
                rstd[:, 3:4], rstd[:, 3:4], rstd[:, 1:2], OP.mult))
            pb.op("vector", lambda E: E.tensor_scalar(
                ttmp[:], x1tl[:], mv[:, 0:1], rstd[:, 3:4],
                OP.subtract, OP.mult))
            pb.op("vector", lambda E: E.tensor_tensor(
                ttmp[:], ttmp[:], g1rep[:], OP.mult))
            pb.op("vector", lambda E: E.tensor_tensor(
                htok[:], ttmp[:], b1rep[:], OP.add), sem="vh")
            pb.wait("tensor", "vh")
            for c in range(KCH):
                pslot = psT[:, (c % 2) * 1024:(c % 2) * 1024 + 128]
                pb.wait("tensor", "vhcp", pb.c("vhcp") - 1)
                pb.op("tensor", lambda E, p=pslot, c=c: E.transpose(
                    p, htok[:, c * 128:(c + 1) * 128], ident[:]), sem="mtA")
                pb.wait("vector", "mtA")
                pb.op("vector", lambda E, p=pslot, c=c, tt=tt: E.tensor_copy(
                    hT[:, c * 2048 + tt * 128: c * 2048 + (tt + 1) * 128], p),
                    sem="vhcp")

        pb.wait("gpsimd", "mq", pb.c("mq"))
        pb.wait("gpsimd", "mvg", pb.c("mvg"))
        pb.wait("gpsimd", "mat", pb.c("mat"))
        pb.wait("gpsimd", "mlg", pb.c("mlg"))
        for w_i, wnm in [(0, "L0_ffn_wq"), (1, "L0_ffn_wk")]:
            for kc in range(KCH):
                pb.dma("gpsimd", wqk[:, w_i * 8192 + kc * 1024:
                                     w_i * 8192 + (kc + 1) * 1024],
                       io[wnm].rearrange("(k p) n -> k p n", p=128)[kc], "qwqk")
        pb.wait("tensor", "qwqk")
        pb.wait("tensor", "vhcp")
        for w_i in range(2):
            dstT = qT if w_i == 0 else kT
            sc = 0.125 if w_i == 0 else 1.0
            for oc in range(KCH):
                for ts4 in range(4):
                    pbank = psA[:, (ts4 % 4) * 512:(ts4 % 4) * 512 + 512]
                    pb.wait("tensor", "vqk", pb.c("vqk") - 3)
                    for kc in range(KCH):
                        pb.op("tensor", lambda E, w_i=w_i, oc=oc, ts4=ts4,
                              kc=kc, p=pbank: E.matmul(
                                  p, wqk[:, w_i * 8192 + kc * 1024 + oc * 128:
                                         w_i * 8192 + kc * 1024 + (oc + 1) * 128],
                                  hT[:, kc * 2048 + ts4 * 512:
                                     kc * 2048 + (ts4 + 1) * 512],
                                  start=(kc == 0), stop=(kc == KCH - 1)),
                              sem=("mq" if kc == KCH - 1 else None))
                    pb.wait("vector", "mq")
                    pb.op("vector", lambda E, w_i=w_i, oc=oc, ts4=ts4, sc=sc,
                          p=pbank, dT=dstT: E.tensor_scalar(
                              dT[:, oc * 2048 + ts4 * 512:
                                 oc * 2048 + (ts4 + 1) * 512],
                              p, bqk[:, w_i * 8 + oc:w_i * 8 + oc + 1], sc,
                              OP.add, OP.mult), sem="vqk")
        pb.wait("gpsimd", "mq", pb.c("mq"))
        pb.wait("gpsimd", "mat", pb.c("mat"))
        for kc in range(KCH):
            pb.dma("gpsimd", wqk[:, kc * 1024:(kc + 1) * 1024],
                   io["L0_ffn_wv"].rearrange("(k p) n -> k p n", p=128)[kc],
                   "qwv")
        vg4 = vg.rearrange("p (t h d) -> p t h d", h=16, d=65)
        pb.op("vector", lambda E: E.memset(vg4[:, :, :, 64:65], 1.0),
              sem="vgone")
        pb.wait("tensor", "qwv")
        pb.wait("tensor", "vgone")
        for tt in range(16):
            for half in range(2):
                pbank = psB[:, (half % 2) * 512:(half % 2) * 512 + 512]
                pb.wait("tensor", "vvg", pb.c("vvg") - 1)
                for kc in range(KCH):
                    pb.op("tensor", lambda E, tt=tt, half=half, kc=kc,
                          p=pbank: E.matmul(
                              p, hT[:, kc * 2048 + tt * 128:
                                    kc * 2048 + (tt + 1) * 128],
                              wqk[:, kc * 1024 + half * 512:
                                  kc * 1024 + (half + 1) * 512],
                              start=(kc == 0), stop=(kc == KCH - 1)),
                          sem=("mvg" if kc == KCH - 1 else None))
                pb.wait("vector", "mvg")
                pb.op("vector", lambda E, half=half, p=pbank: E.tensor_tensor(
                    obf[:, 0:512], p, bvrep[:, half * 512:(half + 1) * 512],
                    OP.add))
                pb.op("vector", lambda E, tt=tt, half=half: E.tensor_copy(
                    vg4[:, tt, half * 8:(half + 1) * 8, 0:64],
                    obf.ap().rearrange("p (h d) -> p h d", d=64)),
                    sem="vvg")

        if bi == 0 and "dbg_hT" in io:
            pb.wait("sync", "vqk")
            pb.dma("sync", io["dbg_hT"], hT, "qdbg")
            pb.dma("sync", io["dbg_qT"], qT, "qdbg")
            pb.dma("sync", io["dbg_kT"], kT, "qdbg")
            pb.wait("sync", "vvg")
            pb.dma("sync", io["dbg_vg"], vg, "qdbg")
            pb.wait("sync", "qdbg")
            for e2 in ("vector", "scalar", "tensor", "gpsimd"):
                pb.wait(e2, "qdbg")
        pb.wait("scalar", "vvg", pb.c("vvg"))
        pb.wait("scalar", "vh", pb.c("vh"))
        pb.wait("scalar", "qx", pb.c("qx"))
        pb.wait("scalar", "qx2", pb.c("qx2"))
        pb.wait("scalar", "vh2", pb.c("vh2"))
        pb.wait("scalar", "vx1b", pb.c("vx1b"))
        pb.wait("scalar", "vat2", pb.c("vat2"))
        pb.wait("scalar", "vlg", pb.c("vlg"))
        pb.wait("scalar", "vtp", pb.c("vtp"))
        pb.wait("scalar", "qx1b", pb.c("qx1b"))
        pb.wait("scalar", "qh2", pb.c("qh2"))
        pb.wait("scalar", "qlg", pb.c("qlg"))
        pb.wait("scalar", "mtp", pb.c("mtp"))
        pb.wait("scalar", "mtt", pb.c("mtt"))
        pb.wait("scalar", "mlg", pb.c("mlg"))
        for hp in range(KCH):
            for qq in range(4):
                for kc in range(16):
                    bA = psA[:, (kc % 2) * 512:(kc % 2) * 512 + 512]
                    bB = psA[:, 1024 + (kc % 2) * 512:
                             1024 + (kc % 2) * 512 + 512]
                    pb.wait("tensor", "aexp", pb.c("aexp") - 2)
                    pb.op("tensor", lambda E, hp=hp, qq=qq, kc=kc, b=bA:
                          E.matmul(b, kT[0:64, hp * 2048 + kc * 128:
                                         hp * 2048 + (kc + 1) * 128],
                                   qT[0:64, hp * 2048 + qq * 512:
                                      hp * 2048 + (qq + 1) * 512],
                                   start=True, stop=True), sem="msc")
                    pb.op("tensor", lambda E, hp=hp, qq=qq, kc=kc, b=bB:
                          E.matmul(b, kT[64:128, hp * 2048 + kc * 128:
                                         hp * 2048 + (kc + 1) * 128],
                                   qT[64:128, hp * 2048 + qq * 512:
                                      hp * 2048 + (qq + 1) * 512],
                                   start=True, stop=True), sem="msc")
                    pb.wait("scalar", "msc")
                    pb.op("scalar", lambda E, kc=kc, b=bA: E.activation(
                        pTa[:, kc * 512:(kc + 1) * 512], b, AF.Exp),
                        sem="aexp")
                    pb.op("scalar", lambda E, kc=kc, b=bB: E.activation(
                        pTb[:, kc * 512:(kc + 1) * 512], b, AF.Exp),
                        sem="aexp")
                for hi in range(2):
                    pT_ = pTa if hi == 0 else pTb
                    h = hp * 2 + hi
                    pob = psB[:, 0:512]
                    pb.wait("tensor", "aexp")
                    pb.wait("tensor", "vsc", pb.c("vsc"))
                    pb.wait("tensor", "vob", pb.c("vob"))
                    pb.wait("tensor", "arc", pb.c("arc"))
                    for kc in range(16):
                        pb.op("tensor", lambda E, h=h, kc=kc, pT_=pT_,
                              pob=pob: E.matmul(
                                  pob[0:65, :], vg4[:, kc, h, :],
                                  pT_[:, kc * 512:(kc + 1) * 512],
                                  start=(kc == 0), stop=(kc == 15)),
                              sem=("mpv" if kc == 15 else None))
                    pb.wait("vector", "mpv")
                    pb.wait("scalar", "mpv")
                    pb.op("scalar", lambda E, pob=pob: E.activation(
                        psB[64:65, 512:1024], pob[64:65, :], AF.Ln))
                    pb.op("scalar", lambda E: E.activation(
                        recip[0:1, :], psB[64:65, 512:1024], AF.Exp,
                        scale=-1.0), sem="arc")
                    pb.op("vector", lambda E, pob=pob: E.tensor_copy(
                        obf[0:64, :], pob[0:64, :]), sem="vob")
                    pb.wait("tensor", "arc")
                    pb.op("tensor", lambda E: E.matmul(
                        psB[0:64, 512:1024], ones1[0:1, 0:64], recip[0:1, :],
                        start=True, stop=True), sem="mrc")
                    pb.wait("vector", "mrc")
                    hof = hi * 64
                    pb.op("vector", lambda E, hof=hof, qq=qq, hp=hp:
                          E.scalar_tensor_tensor(
                              hT[hof:hof + 64,
                                 hp * 2048 + qq * 512:hp * 2048 + (qq + 1) * 512],
                              obf[0:64, :], 1.0, psB[0:64, 512:1024],
                              OP.mult, OP.mult), sem="vsc")

        if bi == 0 and "dbg_hT" in io:
            pb.wait("sync", "vsc")
            pb.dma("sync", io["dbg_oT"], hT, "qdbg")
            pb.wait("sync", "qdbg")
            for e2 in ("vector", "scalar", "tensor", "gpsimd"):
                pb.wait(e2, "qdbg")
        pb.wait("gpsimd", "msc", pb.c("msc"))
        pb.wait("gpsimd", "mpv", pb.c("mpv"))
        pb.wait("sync", "mpv", pb.c("mpv"))
        pb.wait("sync", "vsc", pb.c("vsc"))
        pb.wait("vector", "mpv", pb.c("mpv"))
        pb.wait("vector", "msc", pb.c("msc"))
        pb.wait("scalar", "mpv", pb.c("mpv"))
        for kc in range(KCH):
            pb.dma("gpsimd", wqk[:, 8192 + kc * 1024:8192 + (kc + 1) * 1024],
                   io["L0_ffn_wo"].rearrange("(k p) n -> k p n", p=128)[kc],
                   "qwo")
        pb.wait("tensor", "qwo")
        pb.wait("tensor", "vsc")
        for oc in range(8):
            for ts4 in range(4):
                pbank = psB[:, (ts4 % 2) * 512:(ts4 % 2) * 512 + 512]
                pb.wait("tensor", "vao", pb.c("vao") - 1)
                for kc in range(KCH):
                    pb.op("tensor", lambda E, oc=oc, ts4=ts4, kc=kc, p=pbank:
                          E.matmul(
                              p, wqk[:, 8192 + kc * 1024 + oc * 128:
                                     8192 + kc * 1024 + (oc + 1) * 128],
                              hT[:, kc * 2048 + ts4 * 512:
                                 kc * 2048 + (ts4 + 1) * 512],
                              start=(kc == 0), stop=(kc == KCH - 1)),
                          sem=("mat" if kc == KCH - 1 else None))
                pb.wait("vector", "mat")
                pb.op("vector", lambda E, oc=oc, ts4=ts4, p=pbank:
                      E.tensor_scalar(
                          aT2[:, oc * 2048 + ts4 * 512:
                              oc * 2048 + (ts4 + 1) * 512],
                          p, bo_t[:, oc:oc + 1], None, OP.add), sem="vao")
        if bi == 0 and "dbg_hT" in io:
            pb.wait("sync", "vao")
            pb.dma("sync", io["dbg_aT2"], aT2, "qdbg")
            pb.wait("sync", "qdbg")
            for e2 in ("vector", "scalar", "tensor", "gpsimd"):
                pb.wait(e2, "qdbg")
        pb.wait("tensor", "vao")
        for tt in range(16):
            pb.wait("vector", "vat2", pb.c("vat2"))
            pb.wait("vector", "mtp", pb.c("mtp"))
            pb.wait("tensor", "vtp", pb.c("vtp"))
            for c in range(8):
                pslot = psT[:, (c % 2) * 1024:(c % 2) * 1024 + 128]
                pb.wait("tensor", "vatk", pb.c("vatk") - 1)
                pb.op("tensor", lambda E, c=c, tt=tt, p=pslot: E.transpose(
                    p, aT2[:, c * 2048 + tt * 128:c * 2048 + (tt + 1) * 128],
                    ident[:]), sem="mtt")
                pb.wait("vector", "mtt")
                pb.op("vector", lambda E, c=c, p=pslot: E.tensor_copy(
                    atok2[:, c * 128:(c + 1) * 128], p), sem="vatk")
            pb.wait("sync", "vh2", pb.c("vh2"))
            pb.wait("sync", "vh", pb.c("vh"))
            pb.wait("sync", "vx1b", pb.c("vx1b"))
            pb.dma("sync", x1tl[:], io["x1_in"][bi, tt * 128:(tt + 1) * 128, :],
                   "qx2")
            pb.wait("vector", "qx2")
            pb.wait("vector", "vatk")
            emit_ln_tile(pb, mv, stat, rstd, x1tl[:])
            pb.wait("vector", "act")
            pb.op("vector", lambda E: E.tensor_scalar(
                ttmp[:], x1tl[:], mv[:, 0:1], rstd[:, 1:2],
                OP.subtract, OP.mult))
            pb.wait("vector", "qx1b", pb.c("qx1b"))
            pb.op("vector", lambda E: E.tensor_tensor(
                x1btok[:], ttmp[:], atok2[:], OP.add), sem="vx1b")
            pb.wait("sync", "vx1b")
            pb.dma("sync", io["x1b_out"][bi, tt * 128:(tt + 1) * 128, :],
                   x1btok[:], "qx1b")
            emit_ln_tile(pb, mv, stat, rstd, x1btok[:])
            pb.wait("vector", "act")
            pb.op("vector", lambda E: E.tensor_scalar(
                ttmp[:], x1btok[:], mv[:, 0:1], rstd[:, 1:2],
                OP.subtract, OP.mult))
            pb.op("vector", lambda E: E.tensor_tensor(
                ttmp[:], ttmp[:], g2rep[:], OP.mult))
            pb.wait("vector", "qh2", pb.c("qh2"))
            pb.op("vector", lambda E: E.tensor_tensor(
                h2tok[:], ttmp[:], b2rep[:], OP.add), sem="vh2")
            pb.wait("sync", "vh2")
            pb.dma("sync", io["h2_out"][bi, tt * 128:(tt + 1) * 128, :],
                   h2tok[:], "qh2")
            pb.wait("tensor", "vatk", pb.c("vatk"))
            pb.wait("tensor", "vh2", pb.c("vh2"))
            for c in range(KCH):
                pslot = psT[:, (c % 2) * 1024:(c % 2) * 1024 + 128]
                pb.wait("tensor", "vtp", pb.c("vtp") - 1)
                pb.op("tensor", lambda E, c=c, p=pslot: E.transpose(
                    p, h2tok[:, c * 128:(c + 1) * 128], ident[:]), sem="mtp")
                pb.wait("vector", "mtp")
                pb.op("vector", lambda E, c=c, p=pslot: E.tensor_copy(
                    h2Tp[:, c * 128:(c + 1) * 128], p), sem="vtp")
            pb.wait("tensor", "vtp")
            pb.wait("tensor", "vlg", pb.c("vlg"))
            for c in range(KCH):
                pb.op("tensor", lambda E, c=c: E.matmul(
                    psA[:, 0:8], h2Tp[:, c * 128:(c + 1) * 128],
                    wr_sb[:, c * 8:(c + 1) * 8],
                    start=(c == 0), stop=(c == KCH - 1)),
                    sem=("mlg" if c == KCH - 1 else None))
            pb.wait("vector", "mlg")
            pb.op("vector", lambda E, bi=bi, tt=tt: E.tensor_copy(
                logit_sb[:, (bi * 16 + tt) * 8:(bi * 16 + tt + 1) * 8],
                psA[:, 0:8]), sem="vlg")
            pb.wait("sync", "vlg")
            pb.dma("sync",
                   io["logits_out"][bi, tt * 128:(tt + 1) * 128, :],
                   logit_sb[:, (bi * 16 + tt) * 8:(bi * 16 + tt + 1) * 8],
                   "qlg")

    for lbl in ["msc", "mpv", "mat", "mlg", "vlg", "qlg", "vsc", "vtp"]:
        pb.wait("vector", lbl, pb.c(lbl))
        pb.wait("tensor", lbl, pb.c(lbl))
        pb.wait("scalar", lbl, pb.c(lbl))
    emit_masks_and_cums(pb, es, nc, io, logit_sb, mrow, mwork, mbf,
                        ident, psT, m1T, m2T, lcum, cnt_sb, want_counts=True)
    pb.wait("sync", "qx1b")
    pb.wait("sync", "qh2")
    pb.wait("sync", "qlg")
    pb.wait("sync", "qcnt")


def emit_masks_and_cums(pb, es, nc, io, logit_sb, mrow, mwork, mbf,
                        ident, psT, m1T, m2T, lcum, cnt_sb, want_counts):
    for g in range(32):
        L = logit_sb[:, g * 8:(g + 1) * 8]
        m1 = mwork[:, 0:8]
        m2 = mwork[:, 8:16]
        l2 = mwork[:, 16:24]
        cs = mwork[:, 24:32]
        pb.op("vector", lambda E, L=L: E.tensor_reduce(
            mrow[:, 0:1], L, AX.X, OP.max))
        pb.op("vector", lambda E, L=L, m1=m1: E.tensor_scalar(
            m1, L, mrow[:, 0:1], None, OP.is_ge))
        pb.op("vector", lambda E, m1=m1, cs=cs: E.tensor_tensor_scan(
            cs, m1, m1, 0.0, OP.add, OP.bypass))
        pb.op("vector", lambda E, cs=cs: E.tensor_scalar(
            cs, cs, 1.0, None, OP.is_le))
        pb.op("vector", lambda E, m1=m1, cs=cs: E.tensor_tensor(
            m1, m1, cs, OP.mult))
        pb.op("vector", lambda E, m1=m1, l2=l2, L=L: E.scalar_tensor_tensor(
            l2, m1, -1e30, L, OP.mult, OP.add))
        pb.op("vector", lambda E, l2=l2: E.tensor_reduce(
            mrow[:, 1:2], l2, AX.X, OP.max))
        pb.op("vector", lambda E, l2=l2, m2=m2: E.tensor_scalar(
            m2, l2, mrow[:, 1:2], None, OP.is_ge))
        pb.op("vector", lambda E, m2=m2, cs=cs: E.tensor_tensor_scan(
            cs, m2, m2, 0.0, OP.add, OP.bypass))
        pb.op("vector", lambda E, cs=cs: E.tensor_scalar(
            cs, cs, 1.0, None, OP.is_le))
        pb.op("vector", lambda E, m2=m2, cs=cs: E.tensor_tensor(
            m2, m2, cs, OP.mult))
        pb.op("vector", lambda E: E.tensor_copy(mbf[:], mwork[:, 0:16]),
              sem="vmk")
        for mi in range(2):
            mT = m1T if mi == 0 else m2T
            pslot = psT[:, (mi % 2) * 1024:(mi % 2) * 1024 + 128]
            pb.wait("tensor", "vmk")
            pb.wait("tensor", "vmT", pb.c("vmT") - 1)
            pb.op("tensor", lambda E, mi=mi, p=pslot: E.transpose(
                p[0:8, 0:128], mbf[:, mi * 8:mi * 8 + 8], ident[:]),
                sem="mmT")
            pb.wait("vector", "mmT")
            pb.op("vector", lambda E, g=g, mT=mT, p=pslot: E.tensor_copy(
                mT[0:8, g * 128:(g + 1) * 128], p[0:8, 0:128]), sem="vmT")
    pb.op("vector", lambda E: E.tensor_tensor_scan(
        lcum[0:8, :], m1T[0:8, :], m1T[0:8, :], 0.0, OP.add, OP.bypass),
        sem="vcum")
    if want_counts:
        pb.op("vector", lambda E: E.tensor_copy(
            cnt_sb[0:8, 0:1], lcum[0:8, 4095:4096]), sem="vcnt")
        pb.op("vector", lambda E: E.tensor_tensor_scan(
            lcum[0:8, :], m2T[0:8, :], m2T[0:8, :], 0.0, OP.add, OP.bypass),
            sem="vcum")
        pb.op("vector", lambda E: E.tensor_copy(
            cnt_sb[0:8, 1:2], lcum[0:8, 4095:4096]), sem="vcnt")
        pb.wait("sync", "vcnt")
        pb.dma("sync", io["cnt_out"], cnt_sb[0:8, 0:2], "qcnt")


P2_WNAMES = [
    ("L0_ffn_wq", [D, D]), ("L0_ffn_wk", [D, D]), ("L0_ffn_wv", [D, D]),
    ("L0_ffn_wo", [D, D]), ("L0_ffn_ln1_g", [1, D]), ("L0_ffn_ln1_b", [1, D]),
    ("L0_ffn_ln2_g", [1, D]), ("L0_ffn_ln2_b", [1, D]), ("L0_ffn_bv", [1, D]),
]


def prep_p2_weights(d):
    out = {}
    for nm in ["L0_ffn_ln1_g", "L0_ffn_ln1_b", "L0_ffn_ln2_g", "L0_ffn_ln2_b"]:
        out[nm] = d[nm].reshape(1, D)
    out["L0_ffn_bv"] = d["L0_ffn_bv"].reshape(1, D)
    out["bq_t"] = np.ascontiguousarray(d["L0_ffn_bq"].reshape(8, 128).T)
    out["bk_t"] = np.ascontiguousarray(d["L0_ffn_bk"].reshape(8, 128).T)
    out["bo_t"] = np.ascontiguousarray(d["L0_ffn_bo"].reshape(8, 128).T)
    out["wr_t"] = np.ascontiguousarray(
        d["L0_ffn_wr"].reshape(8, 128, 8).transpose(1, 0, 2).reshape(128, 64))
    return out


def build_p2pre():
    nc = bass.Bass("TRN2", target_bir_lowering=False)
    io = {}
    io["x1_in"] = nc.dram_tensor("x1_in", [NB, T, D], F32,
                                 kind="ExternalInput").ap()
    for nm in ["L0_ffn_wq", "L0_ffn_wk", "L0_ffn_wv", "L0_ffn_wo"]:
        io[nm] = nc.dram_tensor(nm, [D, D], F32, kind="ExternalInput").ap()
    for nm in ["L0_ffn_ln1_g", "L0_ffn_ln1_b", "L0_ffn_ln2_g", "L0_ffn_ln2_b",
               "L0_ffn_bv"]:
        io[nm] = nc.dram_tensor(nm, [1, D], F32, kind="ExternalInput").ap()
    for nm, shp in [("bq_t", [128, 8]), ("bk_t", [128, 8]), ("bo_t", [128, 8]),
                    ("wr_t", [128, 64]), ("ident", [128, 128]),
                    ("cpool", [128, 8])]:
        io[nm] = nc.dram_tensor(nm, shp, F32, kind="ExternalInput").ap()
    io["x1b_out"] = nc.dram_tensor("x1b_out", [NB, T, D], F32,
                                   kind="ExternalOutput").ap()
    io["h2_out"] = nc.dram_tensor("h2_out", [NB, T, D], BF16,
                                  kind="ExternalOutput").ap()
    io["logits_out"] = nc.dram_tensor("logits_out", [NB, T, 8], F32,
                                      kind="ExternalOutput").ap()
    io["cnt_out"] = nc.dram_tensor("cnt_out", [8, 2], F32,
                                   kind="ExternalOutput").ap()
    if os.environ.get("P2DBG", "0") == "1":
        for nm in ["dbg_hT", "dbg_qT", "dbg_kT", "dbg_oT"]:
            io[nm] = nc.dram_tensor(nm, [128, 16384], BF16,
                                    kind="ExternalOutput").ap()
        io["dbg_vg"] = nc.dram_tensor("dbg_vg", [128, 16640], BF16,
                                      kind="ExternalOutput").ap()
        io["dbg_aT2"] = nc.dram_tensor("dbg_aT2", [128, 16384], BF16,
                                       kind="ExternalOutput").ap()
    es = ExitStack()
    with es:
        pb = PB(nc, es)
        emit_p2pre(pb, es, nc, io)
        print("P2pre instructions:", pb.n_instr())
        pb.finalize()
    return nc




def emit_p2post(pb, es, nc, io):
    U8 = mybir.dt.uint8
    ovl = sbt(es, nc, "p3_ovl", [128, 176 * 1024], U8)

    def view(off_kb, ncols, dtype):
        size = ncols * (4 if dtype == F32 else 2)
        return ovl.ap()[:, off_kb * 1024:off_kb * 1024 + size].bitcast(dtype)

    m1T = view(0, 4096, F32)
    m2T = view(16, 4096, F32)
    lcum1 = view(32, 4096, F32)
    lcum2 = view(48, 4096, F32)
    ew1 = view(64, 4096, F32)
    ew2 = view(80, 4096, F32)
    wstgA = view(96, 4096, BF16)
    wstgB = view(104, 4096, BF16)
    w1e = view(0, 32768, BF16)
    w2e = view(64, 32768, BF16)
    h1sb = view(128, 16384, BF16)
    xg = view(160, 4096, BF16)
    xeT = view(168, 4096, BF16)
    yeT = view(168, 4096, BF16)
    yetok = view(160, 4096, BF16)
    ygA = view(0, 1024, BF16)
    ygB = view(2, 1024, BF16)
    cmbf = view(4, 1024, F32)
    x1btl = view(8, 1024, F32)
    x2tl = view(12, 1024, F32)

    cpool = (io["cpool_sb"] if "cpool_sb" in io else sbt(es, nc, "p3_cpool", [128, 8], F32))
    ident = sbt(es, nc, "p3_ident", [128, 128], BF16)
    identf = sbt(es, nc, "p3_identf", [128, 128], F32)
    logit_sb = sbt(es, nc, "p3_logit", [128, 256], F32)
    mrow = sbt(es, nc, "p3_mrow", [128, 2], F32)
    mwork = sbt(es, nc, "p3_mwork", [128, 32], F32)
    mbf = sbt(es, nc, "p3_mbf", [128, 16], BF16)
    cnt_sb = sbt(es, nc, "p3_cnt", [8, 2], F32)
    cntf = sbt(es, nc, "p3_cntf", [128, 1], F32)
    mm1 = sbt(es, nc, "p3_mm1", [128, 8], BF16)
    mm2 = sbt(es, nc, "p3_mm2", [128, 8], BF16)
    off_sb = sbt(es, nc, "p3_off", [8, 4], F32)
    ecol = sbt(es, nc, "p3_ecol", [8, 2], F32)
    pp = sbt(es, nc, "p3_pp", [128, 64], F32)
    vv = sbt(es, nc, "p3_vv", [128, 64], F32)
    ss = sbt(es, nc, "p3_ss", [128, 64], F32)
    su32 = sbt(es, nc, "p3_su32", [128, 64], U32)
    pu32 = sbt(es, nc, "p3_pu32", [128, 64], U32)
    gg = sbt(es, nc, "p3_gg", [128, 64], F32)
    iotat = sbt(es, nc, "p3_iotat", [128, 64], U32)
    tokmap_sb = sbt(es, nc, "p3_tokmap", [128, 192], U32)
    bigv = sbt(es, nc, "p3_bigv", [128, 192], U32)
    b1sb = sbt(es, nc, "p3_b1sb", [128, 32], F32)
    b2sb = sbt(es, nc, "p3_b2sb", [128, 8], F32)
    stat = sbt(es, nc, "p3_stat", [128, 12], F32)
    mv = sbt(es, nc, "p3_mv", [128, 4], F32)
    rstd = sbt(es, nc, "p3_rstd", [128, 4], F32)

    psA = pst(es, nc, "p3_psA", [128, 2048], F32)
    psB = pst(es, nc, "p3_psB", [128, 1024], F32)
    psT = pst(es, nc, "p3_psT", [128, 2048], BF16)

    CVALS = [1e-5, 1.0, -0.5, 0.0, 0.5, 2.0, -1.0, 8.0]
    for i, v in enumerate(CVALS):
        nc.const_aps.register(F32, v, cpool[:, i:i + 1])
    pb.dma("sync", cpool[:], io["cpool"], "qs")
    pb.wait("scalar", "qs")
    pb.wait("vector", "qs")
    pb.wait("tensor", "qs")
    REGS = {}

    def _mkregs(E):
        for nm, v in [("tokmap", 128 * 96 - 1), ("xg", NB * T - 1),
                      ("ye", 8 * KCAP - 1)]:
            r = E.alloc_register()
            E.reg_mov(r, v)
            REGS[nm] = r

    pb.recs["gpsimd"].append(_mkregs)
    pb.dma("gpsimd", ident[:], io["ident"], "qgp")
    pb.dma("sync", identf[:], io["ident"], "qs")
    pb.dma("sync", cntf[:], io["cnt_all"], "qs")
    pb.wait("gpsimd", "qgp")
    pb.op("gpsimd", lambda E: E.iota(
        iotat.ap().rearrange("p (g o) -> p g o", o=2), [[128, 32], [0, 2]],
        base=0, channel_multiplier=1), sem="pio")
    pb.op("gpsimd", lambda E: E.iota(
        bigv[:], [[0, 192]], base=1 << 30, channel_multiplier=0), sem="pio")
    pb.op("gpsimd", lambda E: E.iota(
        ecol[0:8, 0:1], [[1, 1]], base=0, channel_multiplier=KCAP,
        allow_small_or_imprecise_dtypes=True), sem="pio")
    pb.wait("sync", "pio")
    pb.dma("sync", io["tokmap"].rearrange("(p g) o -> p (g o)", g=96),
           bigv[:], "qtm0")

    for bi in range(NB):
        for tt in range(16):
            pb.dma("sync",
                   logit_sb[:, (bi * 16 + tt) * 8:(bi * 16 + tt + 1) * 8],
                   io["logits_in"][bi, tt * 128:(tt + 1) * 128, :], "qlg")
    pb.wait("vector", "qlg")
    pb.wait("tensor", "qlg")
    pb.wait("scalar", "qlg")

    if True:
        emit_masks_and_cums(pb, es, nc, io, logit_sb, mrow, mwork, mbf,
                        ident, psT, m1T, m2T, lcum1, cnt_sb,
                        want_counts=False)
    pb.op("vector", lambda E: E.tensor_tensor_scan(
        lcum2[0:8, :], m2T[0:8, :], m2T[0:8, :], 0.0, OP.add, OP.bypass),
        sem="vcum")
    pb.wait("scalar", "vmk", pb.c("vmk"))
    pb.wait("scalar", "vcum", pb.c("vcum"))
    pb.wait("scalar", "vmT", pb.c("vmT"))
    for g in range(32):
        L = logit_sb[:, g * 8:(g + 1) * 8]
        pb.wait("scalar", "vpp", pb.c("vpp"))
        pb.op("scalar", lambda E, L=L: E.activation(
            mwork[:, 0:8], L, AF.Exp, accum_out=mrow[:, 0:1]), sem="aex")
        pb.wait("vector", "aex")
        pb.op("vector", lambda E: E.reciprocal(mrow[:, 1:2], mrow[:, 0:1]))
        pb.op("vector", lambda E, L=L: E.tensor_reduce(
            mwork[:, 8:9], L, AX.X, OP.max))
        pb.op("vector", lambda E, L=L: E.tensor_scalar(
            mwork[:, 16:24], L, mwork[:, 8:9], None, OP.is_ge))
        pb.op("vector", lambda E: E.tensor_tensor_scan(
            mwork[:, 24:32], mwork[:, 16:24], mwork[:, 16:24], 0.0,
            OP.add, OP.bypass))
        pb.op("vector", lambda E: E.tensor_scalar(
            mwork[:, 24:32], mwork[:, 24:32], 1.0, None, OP.is_le))
        pb.op("vector", lambda E: E.tensor_tensor(
            mwork[:, 16:24], mwork[:, 16:24], mwork[:, 24:32], OP.mult))
        pb.op("vector", lambda E: E.tensor_tensor(
            mwork[:, 8:16], mwork[:, 0:8], mwork[:, 16:24], OP.mult))
        pb.op("vector", lambda E, g=g: E.tensor_scalar(
            mwork[:, 8:16], mwork[:, 8:16], 0.0, 0.0, OP.add, OP.add,
            accum_out=pp[:, 2 * g:2 * g + 1]))
        pb.op("vector", lambda E, L=L: E.scalar_tensor_tensor(
            mwork[:, 8:16], mwork[:, 16:24], -1e30, L, OP.mult, OP.add))
        pb.op("vector", lambda E: E.tensor_reduce(
            mwork[:, 24:25], mwork[:, 8:16], AX.X, OP.max))
        pb.op("vector", lambda E: E.tensor_scalar(
            mwork[:, 8:16], mwork[:, 8:16], mwork[:, 24:25], None, OP.is_ge))
        pb.op("vector", lambda E: E.tensor_tensor_scan(
            mwork[:, 24:32], mwork[:, 8:16], mwork[:, 8:16], 0.0,
            OP.add, OP.bypass))
        pb.op("vector", lambda E: E.tensor_scalar(
            mwork[:, 24:32], mwork[:, 24:32], 1.0, None, OP.is_le))
        pb.op("vector", lambda E: E.tensor_tensor(
            mwork[:, 8:16], mwork[:, 8:16], mwork[:, 24:32], OP.mult))
        pb.op("vector", lambda E: E.tensor_tensor(
            mwork[:, 24:32], mwork[:, 0:8], mwork[:, 8:16], OP.mult))
        pb.op("vector", lambda E, g=g: E.tensor_scalar(
            mwork[:, 24:32], mwork[:, 24:32], 0.0, 0.0, OP.add, OP.add,
            accum_out=pp[:, 2 * g + 1:2 * g + 2]))
        pb.op("vector", lambda E, g=g: E.tensor_scalar(
            pp[:, 2 * g:2 * g + 2], pp[:, 2 * g:2 * g + 2], mrow[:, 1:2],
            None, OP.mult), sem="vpp")

    if True:
        pb.wait("tensor", "qgp2")
        pb.wait("tensor", "qs")
        pb.wait("gpsimd", "mmT", pb.c("mmT"))
        pb.dma("gpsimd", mbf[:, 0:8], io["Mmask"], "qgp2")
        pb.dma("gpsimd", mbf[:, 8:16], io["Mmask2"], "qgp2")
        pb.wait("tensor", "qgp2")
        cntbf = ecol
        pb.wait("vector", "qs")
        pb.op("vector", lambda E: E.tensor_copy(mm1[:, 0:1], cntf[:]), sem="vcf")
        pb.wait("tensor", "vcf")
        pb.op("tensor", lambda E: E.matmul(
            psB[0:8, 0:1], mbf[:, 0:8], mm1[:, 0:1], start=True, stop=True),
            sem=None)
        pb.op("tensor", lambda E: E.matmul(
            psB[0:8, 8:9], mbf[:, 8:16], mm1[:, 0:1], start=True, stop=True),
            sem="moff")
        pb.wait("vector", "moff")
        pb.op("vector", lambda E: E.tensor_copy(off_sb[0:8, 0:2],
                                                psB[0:8, 0:9:8]), sem="voff")

        pb.wait("vector", "vcum")
        pb.op("vector", lambda E: E.tensor_scalar(
            ew1[0:8, :], lcum1[0:8, :], off_sb[0:8, 0:1], -1.0, OP.add, OP.add))
        pb.op("vector", lambda E: E.tensor_scalar(
            ew1[0:8, :], ew1[0:8, :], CGLOB, None, OP.is_lt))
        pb.op("vector", lambda E: E.tensor_tensor(
            ew1[0:8, :], ew1[0:8, :], m1T[0:8, :], OP.mult), sem="vk1")
        pb.op("vector", lambda E: E.tensor_scalar(
            ew2[0:8, :], lcum2[0:8, :], off_sb[0:8, 1:2], -1.0, OP.add, OP.add))
        pb.op("vector", lambda E: E.tensor_scalar(
            ew2[0:8, :], ew2[0:8, :], CGLOB, None, OP.is_lt))
        pb.op("vector", lambda E: E.tensor_tensor(
            ew2[0:8, :], ew2[0:8, :], m2T[0:8, :], OP.mult), sem="vk2")
        pb.wait("tensor", "vk1", pb.c("vk1"))
        pb.wait("tensor", "vk2", pb.c("vk2"))
        for g in range(32):
            for mi in range(2):
                src = (ew1 if mi == 0 else ew2)[0:8, g * 128:(g + 1) * 128]
                pslot = psT[:, (mi % 2) * 1024:(mi % 2) * 1024 + 256].bitcast(F32)
                pb.wait("tensor", "vkv", pb.c("vkv") - 1)
                pb.op("tensor", lambda E, src=src, p=pslot: E.transpose(
                    p[0:128, 0:8], src, identf[0:8, 0:8]), sem="mkv")
                pb.wait("vector", "mkv")
                pb.op("vector", lambda E, g=g, mi=mi, p=pslot: E.tensor_reduce(
                    vv[:, 2 * g + mi:2 * g + mi + 1], p[0:128, 0:8], AX.X,
                    OP.add), sem="vkv")
        pb.wait("vector", "mkv", pb.c("mkv"))
        pb.op("vector", lambda E: E.tensor_scalar(
            ew1[0:8, :], lcum1[0:8, :], ecol[0:8, 0:1], -1.0, OP.add, OP.add))
        pb.op("vector", lambda E: E.tensor_tensor(
            ew1[0:8, :], ew1[0:8, :], m1T[0:8, :], OP.mult), sem="vs1")
        pb.op("vector", lambda E: E.tensor_tensor(
            ecol[0:8, 1:2], ecol[0:8, 0:1], lcum1[0:8, 4095:4096], OP.add))
        pb.op("vector", lambda E: E.tensor_scalar(
            ew2[0:8, :], lcum2[0:8, :], ecol[0:8, 1:2], -1.0, OP.add, OP.add))
        pb.op("vector", lambda E: E.tensor_tensor(
            ew2[0:8, :], ew2[0:8, :], m2T[0:8, :], OP.mult), sem="vs2")
        pb.wait("tensor", "vs1", pb.c("vs1"))
        pb.wait("tensor", "vs2", pb.c("vs2"))
        for g in range(32):
            for mi in range(2):
                src = (ew1 if mi == 0 else ew2)[0:8, g * 128:(g + 1) * 128]
                pslot = psT[:, (mi % 2) * 1024:(mi % 2) * 1024 + 256].bitcast(F32)
                pb.wait("tensor", "vsv", pb.c("vsv") - 1)
                pb.op("tensor", lambda E, src=src, p=pslot: E.transpose(
                    p[0:128, 0:8], src, identf[0:8, 0:8]), sem="msv")
                pb.wait("vector", "msv")
                pb.op("vector", lambda E, g=g, mi=mi, p=pslot: E.tensor_reduce(
                    ss[:, 2 * g + mi:2 * g + mi + 1], p[0:128, 0:8], AX.X,
                    OP.add), sem="vsv")
        pb.op("vector", lambda E: E.tensor_tensor(gg[:], pp[:], vv[:], OP.mult))
        for g in range(32):
            pb.op("vector", lambda E, g=g: E.tensor_tensor(
                mrow[:, 0:1], gg[:, 2 * g:2 * g + 1], gg[:, 2 * g + 1:2 * g + 2],
                OP.add))
            pb.op("vector", lambda E: E.tensor_scalar(
                mrow[:, 0:1], mrow[:, 0:1], 1.1920929e-07, None, OP.max))
            pb.op("vector", lambda E: E.reciprocal(mrow[:, 1:2], mrow[:, 0:1]))
            pb.op("vector", lambda E, g=g: E.tensor_scalar(
                gg[:, 2 * g:2 * g + 2], gg[:, 2 * g:2 * g + 2], mrow[:, 1:2],
                None, OP.mult), sem="vgg")
        pb.op("vector", lambda E: E.tensor_scalar(
            vv[:], ss[:], float(8 * KCAP), None, OP.is_ge))
        pb.op("vector", lambda E: E.scalar_tensor_tensor(
            ss[:], vv[:], 1e9, ss[:], OP.mult, OP.add), sem="vsl")
        pb.op("vector", lambda E: E.tensor_copy(su32[:], ss[:]), sem="vsu")
        pb.op("vector", lambda E: E.tensor_scalar(
            pp[:], ss[:], 1.0 / 128.0, None, OP.mult))
        pb.op("vector", lambda E: E.tensor_copy(pu32[:], pp[:]))
        pb.op("vector", lambda E: E.tensor_copy(pp[:], pu32[:]))
        pb.op("vector", lambda E: E.scalar_tensor_tensor(
            vv[:], pp[:], -128.0, ss[:], OP.mult, OP.add))
        pb.op("vector", lambda E: E.tensor_scalar(
            vv[:], vv[:], 0.0, None, OP.is_lt))
        pb.op("vector", lambda E: E.tensor_tensor(
            pp[:], pp[:], vv[:], OP.subtract))
        pb.op("vector", lambda E: E.tensor_scalar(
            vv[:], ss[:], 96.0, None, OP.mult))
        pb.op("vector", lambda E: E.scalar_tensor_tensor(
            pp[:], pp[:], -(128.0 * 96.0 - 1.0), vv[:], OP.mult, OP.add),
            sem="vpm")
        pb.op("vector", lambda E: E.tensor_copy(pu32[:], pp[:]), sem="vpu")
    pb.wait("gpsimd", "vpu")
    pb.wait("gpsimd", "qtm0")
    for g in range(32):
        for mi in range(2):
            pb.op("gpsimd", lambda E, g=g, mi=mi: E.indirect_dma_start(
                out=io["tokmap"],
                out_offset=IndirectOffsetOnAxis(
                    ap=pu32[:, 2 * g + mi:2 * g + mi + 1], axis=0),
                in_=iotat[:, 2 * g:2 * g + 2], in_offset=None,
                bounds_check=REGS["tokmap"], oob_is_err=False), sem="qsc",
                dma16=True)
    pb.wait("sync", "qsc")
    pb.dma("sync", tokmap_sb[:],
           io["tokmap"].rearrange("(p g) o -> p (g o)", g=96), "qtm")
    pb.wait("gpsimd", "qtm")

    pb.wait("sync", "qstg")
    pb.wait("gpsimd", "qstg")
    pb.wait("vector", "qstg")
    pb.wait("tensor", "qstg")
    pb.op("gpsimd", lambda E: E.memset(xg[:], 0.0), sem="pmz")
    for e in range(8):
        pb.wait("gpsimd", "mh1", pb.c("mh1"))
        pb.wait("gpsimd", "my2", pb.c("my2"))
        if e == 0:
            pb.wait("gpsimd", "vkv", pb.c("vkv"))
            pb.wait("gpsimd", "vsv", pb.c("vsv"))
            pb.wait("gpsimd", "vs1", pb.c("vs1"))
            pb.wait("gpsimd", "vs2", pb.c("vs2"))
            pb.wait("gpsimd", "vcum", pb.c("vcum"))
        for kc in range(8):
            pb.dma("gpsimd", w1e[:, kc * 4096:(kc + 1) * 4096],
                   io["L0_ffn_w1"][e].rearrange(
                       "(k p) n -> k p n", p=128)[kc], "qw1")
        for fc in range(32):
            pb.dma("gpsimd", w2e[:, fc * 1024:(fc + 1) * 1024],
                   io["L0_ffn_w2"][e].rearrange(
                       "(f p) n -> f p n", p=128)[fc], "qw2")
        pb.dma("sync", b1sb[:], io["b1_t"][e], "qb")
        pb.dma("sync", b2sb[:], io["b2_t"][e], "qb")
        pb.wait("tensor", "qw1")
        pb.wait("tensor", "qw2")
        pb.wait("scalar", "qb")
        for grp in range(3):
            pb.wait("gpsimd", "mxT", pb.c("mxT"))
            pb.wait("gpsimd", "vyT", pb.c("vyT"))
            pb.wait("gpsimd", "qye", pb.c("qye"))
            for i4 in range(4):
                gch = e * 12 + grp * 4 + i4
                pb.op("gpsimd", lambda E, gch=gch, i4=i4: E.indirect_dma_start(
                    out=xg[:, i4 * 1024:(i4 + 1) * 1024], out_offset=None,
                    in_=io["h2_in"].rearrange("b t d -> (b t) d"),
                    in_offset=IndirectOffsetOnAxis(
                        ap=tokmap_sb[:, 2 * gch:2 * gch + 1], axis=0),
                    bounds_check=REGS["xg"], oob_is_err=False), sem="qxg",
                    dma16=True)
            pb.wait("tensor", "qxg")
            for i4 in range(4):
                for c in range(KCH):
                    pslot = psT[:, ((i4 * 8 + c) % 2) * 1024:
                                ((i4 * 8 + c) % 2) * 1024 + 128]
                    pb.wait("tensor", "vxT", pb.c("vxT") - 1)
                    pb.op("tensor", lambda E, i4=i4, c=c, p=pslot: E.transpose(
                        p, xg[:, i4 * 1024 + c * 128:i4 * 1024 + (c + 1) * 128],
                        ident[:]), sem="mxT")
                    pb.wait("vector", "mxT")
                    pb.op("vector", lambda E, i4=i4, c=c, p=pslot:
                          E.tensor_copy(
                              xeT[:, c * 512 + i4 * 128:c * 512 + (i4 + 1) * 128],
                              p), sem="vxT")
            pb.wait("tensor", "vxT")
            for fc in range(32):
                pbank = psA[:, (fc % 4) * 512:(fc % 4) * 512 + 512]
                pb.wait("tensor", "ah1", pb.c("ah1") - 3)
                for kc in range(8):
                    pb.op("tensor", lambda E, fc=fc, kc=kc, p=pbank: E.matmul(
                        p, w1e[:, kc * 4096 + fc * 128:kc * 4096 + (fc + 1) * 128],
                        xeT[:, kc * 512:(kc + 1) * 512],
                        start=(kc == 0), stop=(kc == 7)),
                        sem=("mh1" if kc == 7 else None))
                pb.wait("scalar", "mh1")
                pb.op("scalar", lambda E, fc=fc, p=pbank: E.activation(
                    h1sb[:, fc * 512:(fc + 1) * 512], p, AF.Relu,
                    bias=b1sb[:, fc:fc + 1]), sem="ah1")
            pb.wait("tensor", "ah1")
            for oc in range(8):
                pbank = psB[:, (oc % 2) * 512:(oc % 2) * 512 + 512]
                pb.wait("tensor", "vy2", pb.c("vy2") - 1)
                for fc in range(32):
                    pb.op("tensor", lambda E, oc=oc, fc=fc, p=pbank: E.matmul(
                        p, w2e[:, fc * 1024 + oc * 128:fc * 1024 + (oc + 1) * 128],
                        h1sb[:, fc * 512:(fc + 1) * 512],
                        start=(fc == 0), stop=(fc == 31)),
                        sem=("my2" if fc == 31 else None))
                pb.wait("vector", "my2")
                pb.op("vector", lambda E, oc=oc, p=pbank: E.tensor_scalar(
                    yeT[:, oc * 512:(oc + 1) * 512], p, b2sb[:, oc:oc + 1],
                    0.8, OP.add, OP.mult), sem="vy2")
            pb.wait("tensor", "vy2")
            for i4 in range(4):
                pb.wait("vector", "qye", pb.c("qye") - 2)
                for c in range(KCH):
                    pslot = psT[:, ((i4 * 8 + c) % 2) * 1024:
                                ((i4 * 8 + c) % 2) * 1024 + 128]
                    pb.wait("tensor", "vyT", pb.c("vyT") - 1)
                    pb.op("tensor", lambda E, i4=i4, c=c, p=pslot: E.transpose(
                        p, yeT[:, c * 512 + i4 * 128:c * 512 + (i4 + 1) * 128],
                        ident[:]), sem="myT")
                    pb.wait("vector", "myT")
                    pb.op("vector", lambda E, i4=i4, c=c, p=pslot:
                          E.tensor_copy(
                              yetok[:, i4 * 1024 + c * 128:
                                    i4 * 1024 + (c + 1) * 128], p), sem="vyT")
                pb.wait("sync", "vyT")
                row0 = e * KCAP + grp * 512 + i4 * 128
                pb.dma("sync", io["yedram"][row0:row0 + 128, :],
                       yetok[:, i4 * 1024:(i4 + 1) * 1024], "qye")

    pb.wait("gpsimd", "my2", pb.c("my2"))
    pb.wait("gpsimd", "mh1", pb.c("mh1"))
    pb.wait("gpsimd", "qye", pb.c("qye"))
    for g in range(32):
        bi, tt = divmod(g, 16)
        pb.wait("gpsimd", "vcb", pb.c("vcb"))
        pb.op("gpsimd", lambda E: E.memset(ygA[:], 0.0))
        pb.op("gpsimd", lambda E: E.memset(ygB[:], 0.0), sem="pz")
        pb.wait("gpsimd", "pz")
        pb.op("gpsimd", lambda E, g=g: E.indirect_dma_start(
            out=ygA[:], out_offset=None, in_=io["yedram"],
            in_offset=IndirectOffsetOnAxis(ap=su32[:, 2 * g:2 * g + 1], axis=0),
            bounds_check=REGS["ye"], oob_is_err=False), sem="qyg",
            dma16=True)
        pb.op("gpsimd", lambda E, g=g: E.indirect_dma_start(
            out=ygB[:], out_offset=None, in_=io["yedram"],
            in_offset=IndirectOffsetOnAxis(ap=su32[:, 2 * g + 1:2 * g + 2],
                                           axis=0),
            bounds_check=REGS["ye"], oob_is_err=False), sem="qyg",
            dma16=True)
        pb.wait("sync", "vcb", pb.c("vcb"))
        pb.wait("sync", "qx2", pb.c("qx2"))
        pb.dma("sync", x1btl[:], io["x1b_in"][bi, tt * 128:(tt + 1) * 128, :],
               "qxb")
        pb.dma("sync", x2tl[:], io["x1_in"][bi, tt * 128:(tt + 1) * 128, :],
               "qx1")
        pb.wait("vector", "qyg")
        pb.wait("vector", "qxb")
        pb.wait("vector", "qx1")
        pb.op("vector", lambda E, g=g: E.tensor_scalar(
            cmbf[:], ygA[:], gg[:, 2 * g:2 * g + 1], None, OP.mult))
        pb.op("vector", lambda E, g=g: E.scalar_tensor_tensor(
            cmbf[:], ygB[:], gg[:, 2 * g + 1:2 * g + 2], cmbf[:],
            OP.mult, OP.add))
        pb.op("vector", lambda E: E.tensor_tensor(
            cmbf[:], cmbf[:], x1btl[:], OP.add), sem="vfo")
        emit_ln_tile(pb, mv, stat, rstd, cmbf[:])
        pb.wait("vector", "act")
        pb.op("vector", lambda E: E.tensor_scalar(
            cmbf[:], cmbf[:], mv[:, 0:1], rstd[:, 1:2], OP.subtract, OP.mult))
        pb.op("vector", lambda E: E.tensor_tensor(
            x2tl[:], x2tl[:], cmbf[:], OP.add), sem="vcb")
        pb.wait("sync", "vcb")
        pb.dma("sync", io["x2_out"][bi, tt * 128:(tt + 1) * 128, :],
               x2tl[:], "qx2")
    pb.wait("sync", "qx2")


def prep_p2post_weights(d, core):
    out = {}
    out["b1_t"] = np.ascontiguousarray(
        d["L0_ffn_b1"].reshape(8, 32, 128).transpose(0, 2, 1))
    out["b2_t"] = np.ascontiguousarray(
        d["L0_ffn_b2"].reshape(8, 8, 128).transpose(0, 2, 1))
    M1 = np.zeros((128, 8), np.float32)
    M2 = np.zeros((128, 8), np.float32)
    for r in range(8):
        for e in range(8):
            if r < core:
                M1[r * 16 + e, e] = 1.0
                M2[r * 16 + 8 + e, e] = 1.0
            M2[r * 16 + e, e] += 1.0
    out["Mmask"] = M1
    out["Mmask2"] = M2
    return out


def build_p2post():
    nc = bass.Bass("TRN2", target_bir_lowering=False,
                   dynamic_dma_scratch_size=32768)
    io = {}
    io["x1_in"] = nc.dram_tensor("x1_in", [NB, T, D], F32,
                                 kind="ExternalInput").ap()
    io["x1b_in"] = nc.dram_tensor("x1b_in", [NB, T, D], F32,
                                  kind="ExternalInput").ap()
    io["h2_in"] = nc.dram_tensor("h2_in", [NB, T, D], BF16,
                                 kind="ExternalInput").ap()
    io["logits_in"] = nc.dram_tensor("logits_in", [NB, T, 8], F32,
                                     kind="ExternalInput").ap()
    io["cnt_all"] = nc.dram_tensor("cnt_all", [128, 1], F32,
                                   kind="ExternalInput").ap()
    io["L0_ffn_w1"] = nc.dram_tensor("L0_ffn_w1", [8, D, 4096], F32,
                                     kind="ExternalInput").ap()
    io["L0_ffn_w2"] = nc.dram_tensor("L0_ffn_w2", [8, 4096, D], F32,
                                     kind="ExternalInput").ap()
    for nm, shp in [("b1_t", [8, 128, 32]), ("b2_t", [8, 128, 8]),
                    ("Mmask", [128, 8]), ("Mmask2", [128, 8]),
                    ("ident", [128, 128]), ("cpool", [128, 8])]:
        io[nm] = nc.dram_tensor(nm, shp, F32, kind="ExternalInput").ap()
    io["tokmap"] = nc.dram_tensor("tokmap", [128 * 96, 2], U32).ap()
    io["w1bf"] = nc.dram_tensor("w1bf", [8, 8, 128, 4096], BF16).ap()
    io["w2bf"] = nc.dram_tensor("w2bf", [8, 32, 128, 1024], BF16).ap()
    io["yedram"] = nc.dram_tensor("yedram", [8 * KCAP, D], BF16).ap()
    io["x2_out"] = nc.dram_tensor("x2_out", [NB, T, D], F32,
                                  kind="ExternalOutput").ap()
    es = ExitStack()
    with es:
        pb = PB(nc, es)
        emit_p2post(pb, es, nc, io)
        print("P2post instructions:", pb.n_instr())
        pb.finalize()
    return nc




def emit_p3(pb, es, nc, io):
    U8 = mybir.dt.uint8
    ovl = sbt(es, nc, "p4_ovl", [128, 132 * 1024], U8)

    def view(off_kb, ncols, dtype):
        size = ncols * (4 if dtype == F32 else 2)
        return ovl.ap()[:, off_kb * 1024:off_kb * 1024 + size].bitcast(dtype)

    lnT = view(0, 16384, BF16)
    OT = view(0, 16384, BF16)
    Qtok = view(32, 16384, BF16)
    aT = view(32, 16384, BF16)
    Ktok = view(64, 16384, BF16)
    VT = view(96, 16384, BF16)
    wq3 = view(128, 2048, BF16)

    cpool = (io["cpool_sb"] if "cpool_sb" in io else sbt(es, nc, "p4_cpool", [128, 8], F32))
    ident = sbt(es, nc, "p4_ident", [128, 128], BF16)
    ones1 = sbt(es, nc, "p4_ones1", [1, 128], BF16)
    rowbf = sbt(es, nc, "p4_rowbf", [1, 1024], BF16)
    bqrep = sbt(es, nc, "p4_bqrep", [128, 1024], BF16)
    bkrep = sbt(es, nc, "p4_bkrep", [128, 1024], BF16)
    bv_t = sbt(es, nc, "p4_bvt", [128, 8], F32)
    bo_t = sbt(es, nc, "p4_bot", [128, 8], F32)
    wbuf = sbt(es, nc, "p4_wbuf", [128, 8192], BF16)
    x2tl = sbt(es, nc, "p4_x2tl", [128, D], F32)
    stat = sbt(es, nc, "p4_stat", [128, 12], F32)
    mv = sbt(es, nc, "p4_mv", [128, 4], F32)
    rstd = sbt(es, nc, "p4_rstd", [128, 4], F32)
    lntok = sbt(es, nc, "p4_lntok", [128, D], BF16)
    esb = sbt(es, nc, "p4_esb", [64, 64], BF16)
    psums = sbt(es, nc, "p4_psums", [64, 2], F32)
    ptb = sbt(es, nc, "p4_ptb", [128, 64], BF16)
    atok = sbt(es, nc, "p4_atok", [128, D], BF16)
    x3t = sbt(es, nc, "p4_x3t", [128, D], F32)

    psA = pst(es, nc, "p4_psA", [128, 2048], F32)
    psB = pst(es, nc, "p4_psB", [128, 1024], F32)
    psT = pst(es, nc, "p4_psT", [128, 2048], BF16)

    CVALS = [1e-5, 1.0, -0.5, 0.0, 0.5, 2.0, -1.0, 8.0]
    for i, v in enumerate(CVALS):
        nc.const_aps.register(F32, v, cpool[:, i:i + 1])
    pb.dma("sync", cpool[:], io["cpool"], "qs")
    pb.wait("scalar", "qs")
    pb.wait("vector", "qs")
    pb.wait("tensor", "qs")
    pb.dma("gpsimd", ident[:], io["ident"], "qgp")
    pb.dma("gpsimd", bv_t[:], io["bv_t"], "qgp")
    pb.dma("gpsimd", bo_t[:], io["bo_t"], "qgp")
    pb.wait("tensor", "qgp")
    pb.wait("vector", "qgp")
    pb.wait("scalar", "qgp")
    pb.op("vector", lambda E: E.memset(ones1[:], 1.0), sem="vone")
    for nm, dst in [("bq_row", bqrep), ("bk_row", bkrep)]:
        pb.wait("gpsimd", "mbc3", pb.c("mbc3"))
        pb.dma("gpsimd", rowbf[:], io[nm], "qrow")
        pb.wait("tensor", "qrow")
        pb.wait("tensor", "vone")
        pb.wait("tensor", "vbc3", pb.c("vbc3"))
        for h in range(2):
            pb.op("tensor", lambda E, h=h: E.matmul(
                psB[:, h * 512:(h + 1) * 512], ones1[0:1, 0:128],
                rowbf[0:1, h * 512:(h + 1) * 512], start=True, stop=True),
                sem=("mbc3" if h else None))
        pb.wait("vector", "mbc3")
        pb.op("vector", lambda E, dst=dst: E.tensor_copy(dst[:], psB[:, 0:1024]),
              sem="vbc3")

    for bi in range(NB):
        for tt in range(16):
            pb.wait("sync", "vln3", pb.c("vln3"))
            pb.wait("sync", "v3y", pb.c("v3y"))
            pb.dma("sync", x2tl[:], io["x2_in"][bi, tt * 128:(tt + 1) * 128, :],
                   "qx")
            pb.wait("vector", "qx")
            pb.wait("vector", "m3A", pb.c("m3A"))
            emit_ln_tile(pb, mv, stat, rstd, x2tl[:])
            pb.wait("vector", "act")
            pb.op("vector", lambda E: E.tensor_scalar(
                lntok[:], x2tl[:], mv[:, 0:1], rstd[:, 1:2],
                OP.subtract, OP.mult), sem="vln3")
            pb.wait("tensor", "vln3")
            for c in range(KCH):
                pslot = psT[:, (c % 2) * 1024:(c % 2) * 1024 + 128]
                pb.wait("tensor", "v3A", pb.c("v3A") - 1)
                pb.op("tensor", lambda E, p=pslot, c=c: E.transpose(
                    p, lntok[:, c * 128:(c + 1) * 128], ident[:]), sem="m3A")
                pb.wait("vector", "m3A")
                pb.op("vector", lambda E, p=pslot, c=c, tt=tt: E.tensor_copy(
                    lnT[:, c * 2048 + tt * 128: c * 2048 + (tt + 1) * 128], p),
                    sem="v3A")

        for w_i, dstT in ((0, Qtok), (1, Ktok)):
            pb.wait("gpsimd", "v3B", pb.c("v3B"))
            pb.wait("gpsimd", "v3V", pb.c("v3V"))
            pb.wait("gpsimd", "v3D", pb.c("v3D"))
            for kc in range(KCH):
                pb.dma("gpsimd", wbuf[:, kc * 1024:(kc + 1) * 1024],
                       io["L1_attn_W_qkv"].rearrange(
                           "(k p) n -> k p n", p=128)[kc][:, w_i * 1024:
                                                          (w_i + 1) * 1024],
                       "qw3")
            pb.wait("tensor", "qw3")
            pb.wait("tensor", "v3A")
            brep = bqrep if w_i == 0 else bkrep
            for tt in range(16):
                for half in range(2):
                    pbank = psB[:, (half % 2) * 512:(half % 2) * 512 + 512]
                    pb.wait("tensor", "v3B", pb.c("v3B") - 1)
                    for kc in range(KCH):
                        pb.op("tensor", lambda E, tt=tt, half=half, kc=kc,
                              p=pbank: E.matmul(
                                  p, lnT[:, kc * 2048 + tt * 128:
                                        kc * 2048 + (tt + 1) * 128],
                                  wbuf[:, kc * 1024 + half * 512:
                                      kc * 1024 + (half + 1) * 512],
                                  start=(kc == 0), stop=(kc == KCH - 1)),
                              sem=("m3B" if kc == KCH - 1 else None))
                    pb.wait("vector", "m3B")
                    pb.op("vector", lambda E, tt=tt, half=half, p=pbank,
                          dT=dstT, br=brep: E.tensor_tensor(
                              dT[:, tt * 1024 + half * 512:
                                 tt * 1024 + (half + 1) * 512],
                              p, br[:, half * 512:(half + 1) * 512], OP.add),
                          sem="v3B")
        pb.wait("gpsimd", "v3B", pb.c("v3B"))
        for kc in range(KCH):
            pb.dma("gpsimd", wbuf[:, kc * 1024:(kc + 1) * 1024],
                   io["L1_attn_W_qkv"].rearrange(
                       "(k p) n -> k p n", p=128)[kc][:, 2048:3072], "qwv3")
        pb.wait("tensor", "qwv3")
        pb.wait("tensor", "v3B")
        for oc in range(8):
            for ts4 in range(4):
                pbank = psB[:, (ts4 % 2) * 512:(ts4 % 2) * 512 + 512]
                pb.wait("tensor", "v3V", pb.c("v3V") - 1)
                for kc in range(KCH):
                    pb.op("tensor", lambda E, oc=oc, ts4=ts4, kc=kc, p=pbank:
                          E.matmul(
                              p, wbuf[:, kc * 1024 + oc * 128:
                                      kc * 1024 + (oc + 1) * 128],
                              lnT[:, kc * 2048 + ts4 * 512:
                                  kc * 2048 + (ts4 + 1) * 512],
                              start=(kc == 0), stop=(kc == KCH - 1)),
                          sem=("m3V" if kc == KCH - 1 else None))
                pb.wait("vector", "m3V")
                pb.op("vector", lambda E, oc=oc, ts4=ts4, p=pbank:
                      E.tensor_scalar(
                          VT[:, oc * 2048 + ts4 * 512:
                             oc * 2048 + (ts4 + 1) * 512],
                          p, bv_t[:, oc:oc + 1], None, OP.add), sem="v3V")

        pb.wait("tensor", "v3V")
        pb.wait("scalar", "v3B", pb.c("v3B"))
        pb.wait("scalar", "v3V", pb.c("v3V"))
        pb.wait("vector", "v3B", pb.c("v3B"))
        for h in range(16):
            hp = h // 2
            hrow = (h % 2) * 64
            for n in range(8):
                bA = psA[0:64, (n % 2) * 512:(n % 2) * 512 + 64]
                pb.wait("tensor", "a3e", pb.c("a3e") - 1)
                for c2 in range(2):
                    pb.op("tensor", lambda E, h=h, n=n, c2=c2, b=bA: E.matmul(
                        b, Qtok[:, (n + c2) * 1024 + h * 64:
                                (n + c2) * 1024 + (h + 1) * 64],
                        Ktok[:, (n + c2) * 1024 + h * 64:
                             (n + c2) * 1024 + (h + 1) * 64],
                        start=(c2 == 0), stop=(c2 == 1)),
                        sem=("m3w" if c2 else None))
                pb.wait("scalar", "m3w")
                pb.wait("scalar", "v3p", pb.c("v3p"))
                pb.op("scalar", lambda E, b=bA: E.activation(
                    esb[0:64, :], b, AF.Exp, scale=0.125,
                    accum_out=psums[0:64, 0:1]), sem="a3e")
                pb.wait("vector", "a3e")
                pb.op("vector", lambda E: E.reciprocal(
                    psums[0:64, 1:2], psums[0:64, 0:1]))
                pb.op("vector", lambda E: E.tensor_scalar(
                    esb[0:64, :], esb[0:64, :], psums[0:64, 1:2], None,
                    OP.mult), sem="v3p")
                pslot = psT[0:64, (n % 2) * 1024:(n % 2) * 1024 + 64]
                pb.wait("tensor", "v3p")
                pb.wait("tensor", "v3pt", pb.c("v3pt") - 1)
                pb.op("tensor", lambda E, p=pslot: E.transpose(
                    p, esb[0:64, :], ident[0:64, 0:64]), sem="m3pt")
                pb.wait("vector", "m3pt")
                pb.op("vector", lambda E, p=pslot, hrow=hrow: E.tensor_copy(
                    ptb[hrow:hrow + 64, :], p), sem="v3pt")
                pb.wait("tensor", "v3pt")
                for c2 in range(2):
                    ob = psB[:, (c2 % 2) * 512:(c2 % 2) * 512 + 64]
                    pb.wait("tensor", "v3o", pb.c("v3o") - 1)
                    pb.op("tensor", lambda E, h=h, n=n, c2=c2, hp=hp,
                          hrow=hrow, ob=ob: E.matmul(
                              ob, VT[hrow:hrow + 64,
                                     hp * 2048 + n * 128 + c2 * 128:
                                     hp * 2048 + n * 128 + (c2 + 1) * 128],
                              ptb[hrow:hrow + 64, :], start=True, stop=True),
                          sem="m3o")
                    pb.wait("vector", "m3o")
                    for wh in range(2):
                        w_hi = c2 * 2 + wh
                        pb.op("vector", lambda E, hp=hp, hrow=hrow, n=n,
                              w_hi=w_hi, wh=wh, ob=ob: E.tensor_copy(
                                  OT[hrow:hrow + 64,
                                     hp * 2048 + n * 256 + w_hi:
                                     hp * 2048 + n * 256 + w_hi + 253:4],
                                  ob[wh * 64:(wh + 1) * 64, 0:64],
                                  ), sem="v3o")

        pb.wait("gpsimd", "v3V", pb.c("v3V"))
        pb.wait("gpsimd", "v3o", pb.c("v3o"))
        pb.wait("gpsimd", "a3e", pb.c("a3e"))
        for kc in range(KCH):
            pb.dma("gpsimd", wbuf[:, kc * 1024:(kc + 1) * 1024],
                   io["L1_attn_W_out"].rearrange(
                       "(k p) n -> k p n", p=128)[kc], "qwo3")
        pb.wait("tensor", "qwo3")
        pb.wait("tensor", "v3o")
        for oc in range(8):
            for ts4 in range(4):
                pbank = psB[:, (ts4 % 2) * 512:(ts4 % 2) * 512 + 512]
                pb.wait("tensor", "v3D", pb.c("v3D") - 1)
                for kc in range(KCH):
                    pb.op("tensor", lambda E, oc=oc, ts4=ts4, kc=kc, p=pbank:
                          E.matmul(
                              p, wbuf[:, kc * 1024 + oc * 128:
                                      kc * 1024 + (oc + 1) * 128],
                              OT[:, kc * 2048 + ts4 * 512:
                                 kc * 2048 + (ts4 + 1) * 512],
                              start=(kc == 0), stop=(kc == KCH - 1)),
                          sem=("m3D" if kc == KCH - 1 else None))
                pb.wait("vector", "m3D")
                pb.op("vector", lambda E, oc=oc, ts4=ts4, p=pbank:
                      E.tensor_scalar(
                          aT[:, oc * 2048 + ts4 * 512:
                             oc * 2048 + (ts4 + 1) * 512],
                          p, bo_t[:, oc:oc + 1], None, OP.add), sem="v3D")
        pb.wait("tensor", "v3D")
        for tt in range(16):
            pb.wait("vector", "v3x", pb.c("v3x"))
            for c in range(8):
                pslot = psT[:, (c % 2) * 1024:(c % 2) * 1024 + 128]
                pb.wait("tensor", "v3F", pb.c("v3F") - 1)
                pb.op("tensor", lambda E, c=c, tt=tt, p=pslot: E.transpose(
                    p, aT[:, c * 2048 + tt * 128:c * 2048 + (tt + 1) * 128],
                    ident[:]), sem="m3F")
                pb.wait("vector", "m3F")
                pb.op("vector", lambda E, c=c, p=pslot: E.tensor_copy(
                    atok[:, c * 128:(c + 1) * 128], p), sem="v3F")
            pb.wait("sync", "v3y", pb.c("v3y"))
            pb.wait("sync", "vln3", pb.c("vln3"))
            pb.dma("sync", x2tl[:], io["x2_in"][bi, tt * 128:(tt + 1) * 128, :],
                   "qx2")
            pb.wait("vector", "qx2")
            pb.wait("vector", "v3F")
            emit_ln_tile(pb, mv, stat, rstd, atok[:])
            pb.wait("vector", "act")
            pb.wait("vector", "qx3", pb.c("qx3"))
            pb.op("vector", lambda E: E.tensor_scalar(
                x3t[:], atok[:], mv[:, 0:1], rstd[:, 1:2],
                OP.subtract, OP.mult), sem="v3x")
            pb.op("vector", lambda E: E.tensor_tensor(
                x3t[:], x3t[:], x2tl[:], OP.add), sem="v3y")
            pb.wait("sync", "v3y")
            pb.dma("sync", io["x3_out"][bi, tt * 128:(tt + 1) * 128, :],
                   x3t[:], "qx3")
    pb.wait("sync", "qx3")


def prep_p3_weights(d):
    out = {}
    out["bq_row"] = np.ascontiguousarray(d["L1_attn_b_qkv"][0:1024]).reshape(1, 1024)
    out["bk_row"] = np.ascontiguousarray(d["L1_attn_b_qkv"][1024:2048]).reshape(1, 1024)
    out["bv_t"] = np.ascontiguousarray(d["L1_attn_b_qkv"][2048:3072].reshape(8, 128).T)
    out["bo_t"] = np.ascontiguousarray(d["L1_attn_b_out"].reshape(8, 128).T)
    return out


def build_p3():
    nc = bass.Bass("TRN2", target_bir_lowering=False)
    io = {}
    io["x2_in"] = nc.dram_tensor("x2_in", [NB, T, D], F32,
                                 kind="ExternalInput").ap()
    io["L1_attn_W_qkv"] = nc.dram_tensor("L1_attn_W_qkv", [D, 3072], F32,
                                         kind="ExternalInput").ap()
    io["L1_attn_W_out"] = nc.dram_tensor("L1_attn_W_out", [D, D], F32,
                                         kind="ExternalInput").ap()
    for nm, shp in [("bq_row", [1, 1024]), ("bk_row", [1, 1024]),
                    ("bv_t", [128, 8]), ("bo_t", [128, 8]),
                    ("ident", [128, 128]), ("cpool", [128, 8])]:
        io[nm] = nc.dram_tensor(nm, shp, F32, kind="ExternalInput").ap()
    io["x3_out"] = nc.dram_tensor("x3_out", [NB, T, D], F32,
                                  kind="ExternalOutput").ap()
    es = ExitStack()
    with es:
        pb = PB(nc, es)
        emit_p3(pb, es, nc, io)
        print("P3 instructions:", pb.n_instr())
        pb.finalize()
    return nc



P4NSUB = int(os.environ.get("P4NSUB", "2"))
P4OFF = [1, 1, 1, 1, 0, 0, 0, 0]


def emit_p4(pb, es, nc, io):
    U8 = mybir.dt.uint8
    ovl = sbt(es, nc, "p5_ovl", [128, 192768], U8)

    def view(off_b, ncols, dtype):
        size = ncols * (4 if dtype == F32 else 2)
        return ovl.ap()[:, off_b:off_b + size].bitcast(dtype)

    xcT = view(0, 16392, F32)
    wqkvsb = view(67584, 12288, BF16)
    wosb = view(67584, 4096, BF16)
    w1sb = view(67584, 22528, BF16)
    w2sb = view(67584, 22528, BF16)
    lnbf = view(174336, 4096, BF16)
    x3sb = view(135168, 1024, F32)
    x4sb = view(139264, 1024, F32)
    obf_ = view(75776, 512, BF16)
    recipb_ = view(76800, 512, BF16)
    Qt = view(92160, 8192, BF16)
    OT = view(92160, 8192, BF16)
    x3t = view(92160, 1024, F32)
    lntok = view(96256, 1024, BF16)
    Kt = view(108544, 8192, BF16)
    vg = view(124928, 8320, BF16)
    QT = view(141568, 8192, BF16)
    KT = view(157952, 8192, BF16)
    tqC = view(174336, 1024, BF16)
    tqS = view(176384, 1024, BF16)
    tkC = view(178432, 1024, BF16)
    tkS = view(180480, 1024, BF16)
    pTa = view(174336, 4096, BF16)
    masksb = view(182528, 4096, BF16)
    hbf = view(190720, 1024, BF16)
    w1ffin = view(112640, 16384, BF16)
    vsb = view(112640, 11264, BF16)
    sqbf = view(145408, 1024, BF16)
    hbf2 = view(147456, 4096, BF16)
    murow_ = view(155648, 512, BF16)
    rstdrow_ = view(156672, 512, BF16)
    r1_ = view(157696, 512, F32)
    r2_ = view(159744, 512, F32)
    gelb = view(161792, 1024, BF16)
    vbuf = view(163840, 1024, BF16)
    fqn = view(165888, 512, F32)
    murow = murow_[0:1, :]
    rstdrow = rstdrow_[0:1, :]
    r1 = r1_[0:1, :]
    r2 = r2_[0:1, :]

    cpool = (io["cpool_sb"] if "cpool_sb" in io else sbt(es, nc, "p5_cpool", [128, 8], F32))
    ident = sbt(es, nc, "p5_ident", [128, 128], BF16)
    ones1 = sbt(es, nc, "p5_ones1", [1, 128], BF16)
    onesc = sbt(es, nc, "p5_onesc", [128, 1], BF16)
    qsr = sbt(es, nc, "p5_qsr", [128, 256], F32)
    qksbf = sbt(es, nc, "p5_qksbf", [1, 256], BF16)
    ssb = sbt(es, nc, "p5_ssb", [128, 8], F32)
    rstds = sbt(es, nc, "p5_rstds", [128, 8], F32)
    junk = sbt(es, nc, "p5_junk", [128, 64], F32)
    b1t = sbt(es, nc, "p5_b1t", [128, 88], F32)
    b2t = sbt(es, nc, "p5_b2t", [128, 16], F32)
    lngb = sbt(es, nc, "p5_lngb", [128, 32], F32)
    stat = sbt(es, nc, "p5_stat", [128, 12], F32)
    mv = sbt(es, nc, "p5_mv", [128, 4], F32)
    rstd = sbt(es, nc, "p5_rstd", [128, 4], F32)
    recipb = recipb_
    obf = obf_

    psA = pst(es, nc, "p5_psA", [128, 2048], F32)
    psB = pst(es, nc, "p5_psB", [128, 1024], F32)
    psT = pst(es, nc, "p5_psT", [128, 2048], BF16)

    CVALS = [1e-5, 1.0, -0.5, 0.0, 0.5, 2.0, -1.0, 8.0]
    for i, v in enumerate(CVALS):
        nc.const_aps.register(F32, v, cpool[:, i:i + 1])

    vg4 = vg.rearrange("p (t h e) -> p t h e", h=8, e=65)

    pb.dma("sync", cpool[:], io["cpool"], "q5c")
    for e in ("scalar", "vector", "tensor"):
        pb.wait(e, "q5c")
    pb.dma("gpsimd", ident[:], io["ident"], "q5g")
    pb.dma("gpsimd", masksb[:], io["p4_masks"], "q5g")
    pb.dma("gpsimd", b1t[:], io["p4_b1t"], "q5g")
    pb.dma("gpsimd", b2t[:], io["p4_b2t"], "q5g")
    pb.dma("gpsimd", lngb[:], io["p4_lngb"], "q5g")
    pb.dma("gpsimd", qksbf[:], io["p4_qks"], "q5g")
    for e in ("scalar", "vector", "tensor"):
        pb.wait(e, "q5g")
    pb.op("vector", lambda E: E.memset(ones1[:], 1.0))
    pb.op("vector", lambda E: E.memset(onesc[:], 1.0), sem="v5one")
    pb.wait("tensor", "v5one")

    for bi in range(NB):
        pb.wait("sync", "v5vb", pb.c("v5vb"))
        pb.wait("sync", "v5w2", pb.c("v5w2"))
        pb.wait("sync", "v5wo", pb.c("v5wo"))
        pb.wait("tensor", "v5tp", pb.c("v5tp"))
        if pb.c("q5x4"):
            pb.wait("vector", "q5x4")
        def epi_ts(ts):
            if ts == 0:
                pb.wait("tensor", "a5gel", pb.c("a5gel"))
                pb.wait("tensor", "v5vb", pb.c("v5vb"))
                pb.wait("sync", "a5e", pb.c("a5e"))
            pb.wait("vector", "a5q2", pb.c("a5q2"))
            for c in range(8):
                pb.op("vector", lambda E, c=c, ts=ts: E.tensor_copy(
                    hbf2[:, c * 512:(c + 1) * 512],
                    xcT[:, c * 2049 + 1 + ts * 512:
                        c * 2049 + 1 + (ts + 1) * 512]),
                    sem=("v5h2" if c == 7 else None))
            pb.wait("tensor", "v5h2")
            pb.wait("scalar", "v5h2")
            pb.wait("tensor", "v5mu", pb.c("v5mu"))
            for c in range(8):
                sqs = sqbf[:, (c % 2) * 512:(c % 2) * 512 + 512]
                pb.wait("scalar", "m5s2", pb.c("m5s2") - 1)
                pb.op("scalar", lambda E, c=c, s=sqs: E.activation(
                    s, hbf2[:, c * 512:(c + 1) * 512], AF.Square),
                    sem="a5q2")
                pb.op("tensor", lambda E, c=c: E.matmul(
                    psA[0:1, 0:512], onesc[:, 0:1],
                    hbf2[:, c * 512:(c + 1) * 512],
                    start=(c == 0), stop=(c == 7)),
                    sem=("m5s1" if c == 7 else None))
                pb.wait("tensor", "a5q2")
                pb.op("tensor", lambda E, c=c, s=sqs: E.matmul(
                    psA[0:1, 512:1024], onesc[:, 0:1], s,
                    start=(c == 0), stop=(c == 7)), sem="m5s2")
            pb.wait("vector", "m5s1")
            pb.wait("vector", "m5s2")
            pb.op("vector", lambda E: E.tensor_scalar(
                r1[:], psA[0:1, 0:512], 1.0 / 1024.0, None, OP.mult))
            pb.op("vector", lambda E: E.tensor_copy(murow[:], r1[:]),
                  sem="v5mu")
            pb.op("vector", lambda E: E.tensor_scalar(
                r2[:], psA[0:1, 512:1024], 1.0 / 1024.0, None, OP.mult))
            pb.op("vector", lambda E: E.tensor_tensor(
                r1[:], r1[:], r1[:], OP.mult))
            pb.op("vector", lambda E: E.tensor_tensor(
                r2[:], r2[:], r1[:], OP.subtract), sem="v5var")
            pb.wait("scalar", "v5var")
            pb.op("scalar", lambda E: E.activation(
                r1[:], r2[:], AF.Ln, bias=LN_EPS))
            pb.op("scalar", lambda E: E.activation(
                r1[:], r1[:], AF.Exp, scale=-0.5), sem="a5sd")
            pb.wait("vector", "a5sd")
            pb.op("vector", lambda E: E.tensor_copy(rstdrow[:], r1[:]),
                  sem="v5mu")
            pb.wait("tensor", "v5mu")
            pb.wait("tensor", "v5ep", pb.c("v5ep"))
            pb.op("tensor", lambda E: E.matmul(
                psA[:, 1024:1536], ones1[0:1, 0:128], murow[:],
                start=True, stop=True))
            pb.op("tensor", lambda E: E.matmul(
                psA[:, 1536:2048], ones1[0:1, 0:128], rstdrow[:],
                start=True, stop=True), sem="m5bc")
            pb.wait("vector", "m5bc")
            for c in range(8):
                pb.op("vector", lambda E, c=c: E.tensor_tensor(
                    fqn[:], hbf2[:, c * 512:(c + 1) * 512],
                    psA[:, 1024:1536], OP.subtract))
                pb.op("vector", lambda E, c=c: E.tensor_tensor(
                    lnbf[:, c * 512:(c + 1) * 512], fqn[:],
                    psA[:, 1536:2048], OP.mult), sem="v5ep")
            pb.wait("tensor", "v5ep")
            for tl in range(4):
                tg = ts * 4 + tl
                pb.wait("sync", "v5tp", pb.c("v5tp"))
                pb.dma("sync", x3sb[:],
                       io["x3_in"][bi, tg * 128:(tg + 1) * 128, :], "q5x3")
                pb.wait("vector", "q5x3")
                if pb.c("q5x4"):
                    pb.wait("vector", "q5x4")
                for c in range(8):
                    pslot = psT[:, (c % 2) * 1024:(c % 2) * 1024 + 128]
                    pb.wait("tensor", "v5tp", pb.c("v5tp") - 1)
                    pb.op("tensor", lambda E, p=pslot, c=c, tl=tl:
                          E.transpose(
                              p, lnbf[:, c * 512 + tl * 128:
                                      c * 512 + (tl + 1) * 128],
                              ident[:]), sem="m5tp")
                    pb.wait("vector", "m5tp")
                    pb.op("vector", lambda E, p=pslot, c=c: E.tensor_tensor(
                        x4sb[:, c * 128:(c + 1) * 128],
                        x3sb[:, c * 128:(c + 1) * 128], p, OP.add),
                        sem="v5tp")
                pb.wait("sync", "v5tp")
                pb.dma("sync", io["x4_out"][bi, tg * 128:(tg + 1) * 128, :],
                       x4sb[:], "q5x4")

        xcg = xcT.rearrange("p (c n) -> p c n", n=2049)
        pb.op("vector", lambda E: E.memset(xcg[:, :, 0:1], 0.0), sem="v5pro")
        for tt in range(16):
            pb.wait("sync", "v5ln", pb.c("v5ln"))
            pb.dma("sync", x3t[:], io["x3_in"][bi, tt * 128:(tt + 1) * 128, :],
                   "q5x")
            pb.wait("vector", "q5x")
            emit_ln_tile(pb, mv, stat, rstd, x3t[:])
            pb.wait("vector", "act")
            pb.op("vector", lambda E: E.tensor_scalar(
                lntok[:], x3t[:], mv[:, 0:1], rstd[:, 1:2],
                OP.subtract, OP.mult), sem="v5ln")
            pb.wait("tensor", "v5ln")
            for c in range(8):
                pslot = psT[:, (c % 2) * 1024:(c % 2) * 1024 + 128]
                pb.wait("tensor", "v5pro", pb.c("v5pro") - 1)
                pb.op("tensor", lambda E, p=pslot, c=c: E.transpose(
                    p, lntok[:, c * 128:(c + 1) * 128], ident[:]), sem="m5pro")
                pb.wait("vector", "m5pro")
                pb.op("vector", lambda E, p=pslot, c=c, tt=tt: E.tensor_copy(
                    xcT[:, c * 2049 + 1 + tt * 128:c * 2049 + 1 + (tt + 1) * 128],
                    p), sem="v5pro")

        for sub in range(P4NSUB):
            pb.wait("gpsimd", "v5rot", pb.c("v5rot"))
            pb.wait("gpsimd", "v5w2", pb.c("v5w2"))
            pb.wait("gpsimd", "v5vb", pb.c("v5vb"))
            pb.wait("gpsimd", "v5qkv", pb.c("v5qkv"))
            pb.wait("gpsimd", "v5tp", pb.c("v5tp"))
            if pb.c("q5x4"):
                pb.wait("gpsimd", "q5x4")
            for kc in range(8):
                pb.dma("gpsimd", wqkvsb[:, kc * 1536:(kc + 1) * 1536],
                       io["L1_ffn_wqkv"][sub].rearrange(
                           "(k p) n -> k p n", p=128)[kc], "q5w")
            for i, tb in enumerate((tqC, tqS, tkC, tkS)):
                pb.dma("gpsimd", tb.rearrange("p (a d) -> p a d", d=64),
                       io["p4_rot"][i].rearrange("(a p) d -> p a d", p=128),
                       "q5rot")
            pb.wait("tensor", "v5w2", pb.c("v5w2"))
            pb.wait("tensor", "v5qsr", pb.c("v5qsr"))
            pb.op("tensor", lambda E, sub=sub: E.matmul(
                psB[:, 0:64], ones1[0:1, 0:128],
                qksbf[0:1, sub * 128:sub * 128 + 64], start=True, stop=True),
                sem="m5qs")
            pb.op("tensor", lambda E, sub=sub: E.matmul(
                psB[:, 64:128], ones1[0:1, 0:128],
                qksbf[0:1, sub * 128 + 64:sub * 128 + 128],
                start=True, stop=True), sem="m5qs")
            pb.wait("vector", "m5qs")
            for dst, src in [((0, 64), (0, 64)), ((64, 96), (32, 64)),
                             ((96, 128), (0, 32)), ((128, 192), (64, 128)),
                             ((192, 224), (96, 128)), ((224, 256), (64, 96))]:
                pb.op("vector", lambda E, dst=dst, src=src: E.tensor_copy(
                    qsr[:, dst[0]:dst[1]], psB[:, src[0]:src[1]]),
                    sem="v5qsr")
            pb.wait("vector", "q5rot")
            for tb, qof in ((tqC, 0), (tqS, 64), (tkC, 128), (tkS, 192)):
                t3 = tb.rearrange("p (a d) -> p a d", d=64)
                pb.op("vector", lambda E, t3=t3, qof=qof: E.tensor_tensor(
                    t3, t3,
                    qsr[:, qof:qof + 64].unsqueeze(1).broadcast_to(
                        (128, 16, 64)), OP.mult), sem="v5rot")

            pb.wait("vector", "v5pv", pb.c("v5pv"))
            pb.op("vector", lambda E: E.memset(vg[:], 1.0), sem="v5vg")
            pb.wait("tensor", "q5w")
            pb.wait("tensor", "a5e", pb.c("a5e"))
            pb.wait("tensor", "v5ffin", pb.c("v5ffin"))
            pb.wait("tensor", "v5ep", pb.c("v5ep"))
            pb.wait("tensor", "v5pro", pb.c("v5pro"))
            pb.wait("tensor", "v5vg")
            pb.wait("scalar", "v5rot", pb.c("v5rot"))

            def p4_transposes(tt):
                for src_, dstT in ((Qt, QT), (Kt, KT)):
                    for cp in range(4):
                        pslot = psT[:, (cp % 2) * 1024:(cp % 2) * 1024 + 128]
                        pb.wait("tensor", "v5T", pb.c("v5T") - 1)
                        pb.op("tensor", lambda E, p=pslot, src_=src_, tt=tt,
                              cp=cp: E.transpose(
                                  p, src_[:, tt * 512 + cp * 128:
                                          tt * 512 + (cp + 1) * 128],
                                  ident[:]), sem="m5T")
                        pb.wait("vector", "m5T")
                        pb.op("vector", lambda E, p=pslot, dstT=dstT, tt=tt,
                              cp=cp: E.tensor_copy(
                                  dstT[:, cp * 2048 + tt * 128:
                                       cp * 2048 + (tt + 1) * 128], p),
                              sem="v5T")

            for tt in range(16):
                for c in range(8):
                    pb.op("vector", lambda E, c=c, tt=tt: E.tensor_copy(
                        hbf[:, c * 128:(c + 1) * 128],
                        xcT[:, c * 2049 + P4OFF[c] + tt * 128:
                            c * 2049 + P4OFF[c] + (tt + 1) * 128]),
                        sem=("v5hbf" if c == 7 else None))
                pb.wait("tensor", "v5hbf")
                for j in range(3):
                    slot = ((tt * 3 + j) % 4) * 512
                    pbank = psA[:, slot:slot + 512]
                    pb.wait("tensor", "v5qkv", pb.c("v5qkv") - 3)
                    for kc in range(8):
                        pb.op("tensor", lambda E, j=j, kc=kc, p=pbank:
                              E.matmul(
                                  p, hbf[:, kc * 128:(kc + 1) * 128],
                                  wqkvsb[:, kc * 1536 + j * 512:
                                         kc * 1536 + (j + 1) * 512],
                                  start=(kc == 0), stop=(kc == 7)),
                              sem=("m5qkv" if kc == 7 else None))
                    pb.wait("vector", "m5qkv")
                    if j == 0:
                        pb.op("vector", lambda E, tt=tt, p=pbank:
                              E.tensor_copy(
                                  Qt[:, tt * 512:(tt + 1) * 512], p),
                              sem="v5qkv")
                    elif j == 1:
                        pb.op("vector", lambda E, tt=tt, p=pbank:
                              E.tensor_copy(
                                  Kt[:, tt * 512:(tt + 1) * 512], p),
                              sem="v5qkv")
                    else:
                        pb.op("vector", lambda E, tt=tt, p=pbank:
                              E.tensor_copy(
                                  vg4[:, tt, :, 0:64],
                                  p.rearrange("p (h d) -> p h d", d=64)),
                              sem="v5qkv")

                pb.wait("scalar", "v5qkv", pb.c("v5qkv") - 1)
                for src_, tC, tS in ((Qt, tqC, tqS), (Kt, tkC, tkS)):
                    tile = src_[:, tt * 512:(tt + 1) * 512]
                    t3 = tile.rearrange("p (h d) -> p h d", d=64)
                    pb.wait("scalar", "v5rt2", pb.c("v5rt2"))
                    for h in range(8):
                        pb.op("scalar", lambda E, tile=tile, h=h:
                              E.activation(
                                  junk[:], tile[:, h * 64:(h + 1) * 64],
                                  AF.Square, accum_out=ssb[:, h:h + 1]),
                              sem=("a5sq" if h == 7 else None))
                    pb.wait("vector", "a5sq")
                    pb.op("vector", lambda E: E.tensor_scalar(
                        ssb[:], ssb[:], 1e-24, None, OP.max), sem="v5mx")
                    pb.wait("scalar", "v5mx")
                    pb.op("scalar", lambda E: E.activation(
                        rstds[:], ssb[:], AF.Ln))
                    pb.op("scalar", lambda E: E.activation(
                        rstds[:], rstds[:], AF.Exp, scale=-0.5), sem="a5rs")
                    pb.wait("vector", "a5rs")
                    qn3 = psB[:, 0:512].rearrange("p (h d) -> p h d", d=64)
                    qrt23 = psB[:, 512:1024].rearrange(
                        "p (h d) -> p h d", d=64)
                    pb.op("vector", lambda E, t3=t3, qn3=qn3: E.tensor_tensor(
                        qn3, t3, rstds.ap().unsqueeze(2).broadcast_to(
                            (128, 8, 64)), OP.mult))
                    pb.op("vector", lambda E, qn3=qn3, qrt23=qrt23, tS=tS,
                          tt=tt: E.tensor_tensor(
                              qrt23[:, :, 0:32], qn3[:, :, 32:64],
                              tS[:, tt * 64:tt * 64 + 32].unsqueeze(1)
                              .broadcast_to((128, 8, 32)), OP.mult))
                    pb.op("vector", lambda E, qn3=qn3, qrt23=qrt23, tS=tS,
                          tt=tt: E.tensor_tensor(
                              qrt23[:, :, 32:64], qn3[:, :, 0:32],
                              tS[:, tt * 64 + 32:tt * 64 + 64].unsqueeze(1)
                              .broadcast_to((128, 8, 32)), OP.mult))
                    pb.op("vector", lambda E, qn3=qn3, tC=tC,
                          tt=tt: E.tensor_tensor(
                              qn3, qn3,
                              tC[:, tt * 64:(tt + 1) * 64].unsqueeze(1)
                              .broadcast_to((128, 8, 64)), OP.mult),
                          sem="v5rt2")
                    pb.op("vector", lambda E, tile=tile: E.tensor_copy(
                        tile, psB[:, 512:1024]))
                    pb.op("vector", lambda E, tile=tile: E.tensor_tensor(
                        tile, tile, psB[:, 0:512], OP.add), sem="v5rot")

                if tt >= 1:
                    pb.wait("tensor", "v5rot", pb.c("v5rot") - 2)
                    p4_transposes(tt - 1)
            pb.wait("tensor", "v5rot")
            p4_transposes(15)

            if "dbg_q" in io and sub == 0 and bi == 0:
                pb.wait("sync", "v5rot", pb.c("v5rot"))
                pb.dma("sync", io["dbg_q"], Qt, "q5dbg")
                pb.dma("sync", io["dbg_k"], Kt, "q5dbg")
                pb.wait("sync", "q5dbg")
                for e2 in ("vector", "scalar", "tensor", "gpsimd"):
                    pb.wait(e2, "q5dbg")

            pb.wait("tensor", "v5T")
            pb.wait("scalar", "v5T", pb.c("v5T"))
            pb.wait("scalar", "v5msk", pb.c("v5msk"))
            pb.wait("vector", "v5vg", pb.c("v5vg"))
            for h in range(8):
                cp = h // 2
                hrow = (h % 2) * 64
                for w in range(4):
                    kts = list(range(4, 8)) if w == 0 else list(range(8))
                    pb.wait("scalar", "v5o", pb.c("v5o"))
                    for kt in kts:
                        ktg = 4 * (w - 1) + kt
                        slot = psA[:, (kt % 4) * 512:(kt % 4) * 512 + 512]
                        pb.wait("tensor", "a5e", pb.c("a5e") - 3)
                        pb.op("tensor", lambda E, hrow=hrow, cp=cp, ktg=ktg,
                              w=w, s=slot: E.matmul(
                                  s, KT[hrow:hrow + 64,
                                        cp * 2048 + ktg * 128:
                                        cp * 2048 + (ktg + 1) * 128],
                                  QT[hrow:hrow + 64,
                                     cp * 2048 + w * 512:
                                     cp * 2048 + (w + 1) * 512],
                                  start=True, stop=True), sem="m5sc")
                        pb.wait("scalar", "m5sc")
                        pb.op("scalar", lambda E, kt=kt, s=slot:
                              E.activation(
                                  pTa[:, kt * 512:(kt + 1) * 512], s,
                                  AF.Exp, scale=8.0), sem="a5e")
                        pb.wait("vector", "a5e")
                        pb.op("vector", lambda E, kt=kt: E.tensor_tensor(
                            pTa[:, kt * 512:(kt + 1) * 512],
                            pTa[:, kt * 512:(kt + 1) * 512],
                            masksb[:, kt * 512:(kt + 1) * 512], OP.mult),
                            sem="v5msk")
                    pb.wait("tensor", "v5msk")
                    pb.wait("tensor", "v5ob", pb.c("v5ob"))
                    pb.wait("tensor", "a5rc", pb.c("a5rc"))
                    for i, kt in enumerate(kts):
                        ktg = 4 * (w - 1) + kt
                        pb.op("tensor", lambda E, ktg=ktg, h=h, kt=kt,
                              last=(i == len(kts) - 1), first=(i == 0):
                              E.matmul(
                                  psB[0:65, 0:512], vg4[:, ktg, h, :],
                                  pTa[:, kt * 512:(kt + 1) * 512],
                                  start=first, stop=last),
                              sem=("m5pv" if i == len(kts) - 1 else None))
                    pb.wait("vector", "m5pv")
                    pb.wait("scalar", "m5pv")
                    pb.op("scalar", lambda E: E.activation(
                        psB[64:65, 512:1024], psB[64:65, 0:512], AF.Ln))
                    pb.op("scalar", lambda E: E.activation(
                        recipb[0:1, :], psB[64:65, 512:1024], AF.Exp,
                        scale=-1.0), sem="a5rc")
                    pb.op("vector", lambda E: E.tensor_copy(
                        obf[0:64, :], psB[0:64, 0:512]), sem="v5ob")
                    pb.wait("tensor", "a5rc")
                    pb.wait("tensor", "v5o", pb.c("v5o") - 1)
                    pb.op("tensor", lambda E: E.matmul(
                        psB[0:64, 512:1024], ones1[0:1, 0:64], recipb[0:1, :],
                        start=True, stop=True), sem="m5rc")
                    pb.wait("vector", "m5rc")
                    pb.op("vector", lambda E, hrow=hrow, cp=cp, w=w:
                          E.scalar_tensor_tensor(
                              OT[hrow:hrow + 64,
                                 cp * 2048 + w * 512:cp * 2048 + (w + 1) * 512],
                              obf[0:64, :], 1.0, psB[0:64, 512:1024],
                              OP.mult, OP.mult), sem="v5o")

            pb.wait("gpsimd", "v5qkv", pb.c("v5qkv"))
            pb.wait("gpsimd", "v5rot", pb.c("v5rot"))
            for ic in range(4):
                pb.dma("gpsimd", wosb[:, ic * 1024:(ic + 1) * 1024],
                       io["L1_ffn_wo"][sub].rearrange(
                           "(k p) n -> k p n", p=128)[ic], "q5wo")
            pb.wait("tensor", "q5wo")
            pb.wait("tensor", "v5o")
            for oc in range(8):
                for ts4 in range(4):
                    pbank = psB[:, (ts4 % 2) * 512:(ts4 % 2) * 512 + 512]
                    pb.wait("tensor", "v5wo", pb.c("v5wo") - 1)
                    for ic in range(4):
                        pb.op("tensor", lambda E, oc=oc, ts4=ts4, ic=ic,
                              p=pbank: E.matmul(
                                  p, wosb[:, ic * 1024 + oc * 128:
                                          ic * 1024 + (oc + 1) * 128],
                                  OT[:, ic * 2048 + ts4 * 512:
                                     ic * 2048 + (ts4 + 1) * 512],
                                  start=(ic == 0), stop=(ic == 3)),
                              sem=("m5wo" if ic == 3 else None))
                    pb.wait("vector", "m5wo")
                    pb.op("vector", lambda E, oc=oc, ts4=ts4, p=pbank:
                          E.tensor_tensor(
                              xcT[:, oc * 2049 + 1 + ts4 * 512:
                                  oc * 2049 + 1 + (ts4 + 1) * 512],
                              xcT[:, oc * 2049 + 1 + ts4 * 512:
                                  oc * 2049 + 1 + (ts4 + 1) * 512],
                              p, OP.add), sem="v5wo")

            if "dbg_xc" in io and sub == 0 and bi == 0:
                pb.wait("sync", "v5wo", pb.c("v5wo"))
                pb.dma("sync", io["dbg_xc"], xcT, "q5dbg2")
                pb.wait("sync", "q5dbg2")
                for e2 in ("vector", "scalar", "tensor", "gpsimd"):
                    pb.wait(e2, "q5dbg2")

            pb.wait("gpsimd", "v5wo", pb.c("v5wo"))
            for kc in range(8):
                pb.dma("gpsimd", w1sb[:, kc * 2816:(kc + 1) * 2816],
                       io["p4_w1t"][sub].rearrange(
                           "(k p) n -> k p n", p=128)[kc][:, 0:2816], "q5w1")
            pb.wait("tensor", "v5wo")
            pb.wait("tensor", "a5e", pb.c("a5e"))
            pb.wait("scalar", "v5wo", pb.c("v5wo"))
            gfc_box = [0]

            def ffn_stats_ts(ts):
                pb.wait("vector", "a5q2", pb.c("a5q2"))
                for c in range(8):
                    pb.op("vector", lambda E, c=c, ts=ts: E.tensor_copy(
                        hbf2[:, c * 512:(c + 1) * 512],
                        xcT[:, c * 2049 + P4OFF[c] + ts * 512:
                            c * 2049 + P4OFF[c] + (ts + 1) * 512]),
                        sem=("v5h2" if c == 7 else None))
                pb.wait("tensor", "v5h2")
                pb.wait("scalar", "v5h2")
                pb.wait("tensor", "v5mu", pb.c("v5mu"))
                pb.wait("tensor", "v5ffin", pb.c("v5ffin"))
                for c in range(8):
                    sqs = sqbf[:, (c % 2) * 512:(c % 2) * 512 + 512]
                    pb.wait("scalar", "m5s2", pb.c("m5s2") - 1)
                    pb.op("scalar", lambda E, c=c, ts=ts, s=sqs:
                          E.activation(
                              s, hbf2[:, c * 512:(c + 1) * 512], AF.Square),
                          sem="a5q2")
                    pb.op("tensor", lambda E, c=c: E.matmul(
                        psB[0:1, 0:512], onesc[:, 0:1],
                        hbf2[:, c * 512:(c + 1) * 512],
                        start=(c == 0), stop=(c == 7)),
                        sem=("m5s1" if c == 7 else None))
                    pb.wait("tensor", "a5q2")
                    pb.op("tensor", lambda E, c=c, s=sqs: E.matmul(
                        psB[0:1, 512:1024], onesc[:, 0:1], s,
                        start=(c == 0), stop=(c == 7)), sem="m5s2")
                pb.wait("vector", "m5s1")
                pb.wait("vector", "m5s2")
                pb.op("vector", lambda E: E.tensor_scalar(
                    r1[:], psB[0:1, 0:512], 1.0 / 1024.0, None, OP.mult))
                pb.op("vector", lambda E: E.tensor_copy(murow[:], r1[:]),
                      sem="v5mu")
                pb.op("vector", lambda E: E.tensor_scalar(
                    r2[:], psB[0:1, 512:1024], 1.0 / 1024.0, None, OP.mult))
                pb.op("vector", lambda E: E.tensor_tensor(
                    r1[:], r1[:], r1[:], OP.mult))
                pb.op("vector", lambda E: E.tensor_tensor(
                    r2[:], r2[:], r1[:], OP.subtract), sem="v5var")
                pb.wait("scalar", "v5var")
                pb.op("scalar", lambda E: E.activation(
                    r1[:], r2[:], AF.Ln, bias=LN_EPS))
                pb.op("scalar", lambda E: E.activation(
                    r1[:], r1[:], AF.Exp, scale=-0.5), sem="a5sd")
                pb.wait("vector", "a5sd")
                pb.op("vector", lambda E: E.tensor_copy(rstdrow[:], r1[:]),
                      sem="v5mu")
                pb.wait("tensor", "v5mu")
                pb.wait("tensor", "v5ffin", pb.c("v5ffin"))
                pb.op("tensor", lambda E: E.matmul(
                    psB[:, 0:512], ones1[0:1, 0:128], murow[:],
                    start=True, stop=True))
                pb.op("tensor", lambda E: E.matmul(
                    psB[:, 512:1024], ones1[0:1, 0:128], rstdrow[:],
                    start=True, stop=True), sem="m5bc")
                pb.wait("vector", "m5bc")
                for c in range(8):
                    pb.op("vector", lambda E, c=c: E.tensor_tensor(
                        fqn[:], hbf2[:, c * 512:(c + 1) * 512],
                        psB[:, 0:512], OP.subtract))
                    pb.op("vector", lambda E: E.tensor_tensor(
                        fqn[:], fqn[:], psB[:, 512:1024], OP.mult))
                    pb.op("vector", lambda E, c=c, ts=ts, sub=sub:
                          E.tensor_scalar(
                              w1ffin[:, c * 2048 + ts * 512:
                                     c * 2048 + (ts + 1) * 512],
                              fqn[:], lngb[:, sub * 16 + c:sub * 16 + c + 1],
                              lngb[:, sub * 16 + 8 + c:sub * 16 + 8 + c + 1],
                              OP.mult, OP.add), sem="v5ffin")


            def w1_ts(fcs, ts):
                for fc in fcs:
                    fcl = gfc_box[0]
                    gfc_box[0] += 1
                    m = 42 if fc == 21 else 128
                    pcol = (fc - fcs[0]) * 256
                    sA = psA[0:m, (fcl % 2) * 512:(fcl % 2) * 512 + 512]
                    sG = psA[0:m, 1024 + (fcl % 2) * 512:
                             1024 + (fcl % 2) * 512 + 512]
                    pb.wait("tensor", "v5vb", pb.c("v5vb") - 1)
                    pb.wait("tensor", "a5gel", pb.c("a5gel") - 1)
                    for kc in range(8):
                        pb.op("tensor", lambda E, kc=kc, pcol=pcol, m=m,
                              ts=ts, s=sA: E.matmul(
                                  s, w1sb[:, kc * 2816 + pcol:
                                          kc * 2816 + pcol + m],
                                  w1ffin[:, kc * 2048 + ts * 512:
                                         kc * 2048 + (ts + 1) * 512],
                                  start=(kc == 0), stop=(kc == 7)),
                              sem=("m5w1a" if kc == 7 else None))
                    for kc in range(8):
                        pb.op("tensor", lambda E, kc=kc, pcol=pcol, m=m,
                              ts=ts, s=sG: E.matmul(
                                  s, w1sb[:, kc * 2816 + pcol + 128:
                                          kc * 2816 + pcol + 128 + m],
                                  w1ffin[:, kc * 2048 + ts * 512:
                                         kc * 2048 + (ts + 1) * 512],
                                  start=(kc == 0), stop=(kc == 7)),
                              sem=("m5w1g" if kc == 7 else None))
                    gs = gelb[:, (fcl % 2) * 512:(fcl % 2) * 512 + 512]
                    pb.wait("scalar", "m5w1g")
                    pb.wait("scalar", "v5vb", pb.c("v5vb") - 1)
                    pb.op("scalar", lambda E, m=m, fc=fc, sub=sub, s=sG,
                          g=gs: E.activation(
                              g[0:m, :], s, AF.Gelu,
                              bias=b1t[0:m, sub * 44 + 2 * fc + 1:
                                       sub * 44 + 2 * fc + 2]),
                          sem="a5gel")
                    vs = vbuf[:, (fcl % 2) * 512:(fcl % 2) * 512 + 512]
                    pb.wait("vector", "a5gel")
                    pb.wait("vector", "m5w1a")
                    if pb.c("v5vb") >= 2:
                        pb.wait("vector", "q5vT")
                    pb.op("vector", lambda E, m=m, fc=fc, sub=sub, s=sA,
                          g=gs, v=vs: E.scalar_tensor_tensor(
                              v[0:m, :], s,
                              b1t[0:m, sub * 44 + 2 * fc:
                                  sub * 44 + 2 * fc + 1],
                              g[0:m, :], OP.add, OP.mult), sem="v5vb")
                    pb.wait("sync", "v5vb")
                    pb.dma("sync",
                           io["vT_dram"][fc * 128:fc * 128 + m,
                                         ts * 512:(ts + 1) * 512],
                           vs[0:m, :], "q5vT")


            fcs0 = list(range(11))
            for ts in range(4):
                ffn_stats_ts(ts)
                pb.wait("tensor", "v5ffin")
                if ts == 0:
                    pb.wait("tensor", "q5w1")
                w1_ts(fcs0, ts)
            pb.wait("gpsimd", "v5vb", pb.c("v5vb"))
            for kc in range(8):
                pb.dma("gpsimd", w1sb[:, kc * 2816:(kc + 1) * 2816],
                       io["p4_w1t"][sub].rearrange(
                           "(k p) n -> k p n", p=128)[kc][:, 2816:5632],
                       "q5w1")
            pb.wait("tensor", "q5w1")
            fcs1 = list(range(11, 22))
            for ts in range(4):
                w1_ts(fcs1, ts)

            pb.wait("gpsimd", "v5vb", pb.c("v5vb"))
            for fc in range(22):
                pb.dma("gpsimd", w2sb[:, fc * 1024:(fc + 1) * 1024],
                       io["p4_w2t"][sub].rearrange(
                           "(k p) n -> k p n", p=128)[fc], "q5w2")
            pb.wait("sync", "q5vT")
            pb.wait("tensor", "v5ffin", pb.c("v5ffin"))
            for ts in range(4):
                if ts > 0:
                    pb.wait("sync", "v5w2", pb.c("v5w2"))
                for fc in range(22):
                    m = 42 if fc == 21 else 128
                    pb.dma("sync", vsb[0:m, fc * 512:fc * 512 + 512],
                           io["vT_dram"][fc * 128:fc * 128 + m,
                                         ts * 512:(ts + 1) * 512], "q5v")
                pb.wait("tensor", "q5v")
                if ts == 0:
                    pb.wait("tensor", "q5w2")
                for oc in range(8):
                    pbank = psB[:, (oc % 2) * 512:(oc % 2) * 512 + 512]
                    pb.wait("tensor", "v5w2", pb.c("v5w2") - 1)
                    for fc in range(22):
                        m = 42 if fc == 21 else 128
                        pb.op("tensor", lambda E, fc=fc, oc=oc, m=m,
                              p=pbank: E.matmul(
                                  p, w2sb[0:m, fc * 1024 + oc * 128:
                                          fc * 1024 + (oc + 1) * 128],
                                  vsb[0:m, fc * 512:(fc + 1) * 512],
                                  start=(fc == 0), stop=(fc == 21)),
                              sem=("m5w2" if fc == 21 else None))
                    pb.wait("vector", "m5w2")
                    pb.op("vector", lambda E, oc=oc, ts=ts, sub=sub,
                          p=pbank: E.scalar_tensor_tensor(
                              xcT[:, oc * 2049 + 1 + ts * 512:
                                  oc * 2049 + 1 + (ts + 1) * 512],
                              p, b2t[:, sub * 8 + oc:sub * 8 + oc + 1],
                              xcT[:, oc * 2049 + 1 + ts * 512:
                                  oc * 2049 + 1 + (ts + 1) * 512],
                              OP.add, OP.add), sem="v5w2")
                if sub == P4NSUB - 1:
                    epi_ts(ts)

    pb.wait("sync", "q5x4")


def prep_p4_weights(d):
    out = {}
    w1 = d["L1_ffn_w1"]
    w1t = np.zeros((2, 1024, 5632), np.float32)
    b1t = np.zeros((128, 88), np.float32)
    for s in range(2):
        for fc in range(22):
            m = min(128, 2730 - fc * 128)
            w1t[s, :, fc * 256:fc * 256 + m] = w1[s][:, fc * 128:fc * 128 + m]
            w1t[s, :, fc * 256 + 128:fc * 256 + 128 + m] = \
                w1[s][:, 2730 + fc * 128:2730 + fc * 128 + m]
            b1t[0:m, s * 44 + 2 * fc] = d["L1_ffn_b1"][s][fc * 128:
                                                          fc * 128 + m]
            b1t[0:m, s * 44 + 2 * fc + 1] = d["L1_ffn_b1"][s][
                2730 + fc * 128:2730 + fc * 128 + m]
    out["p4_w1t"] = w1t
    out["p4_b1t"] = b1t
    w2t = np.zeros((2, 2816, 1024), np.float32)
    w2t[:, 0:2730] = d["L1_ffn_w2"]
    out["p4_w2t"] = w2t
    b2t = np.zeros((128, 16), np.float32)
    lngb = np.zeros((128, 32), np.float32)
    for s in range(2):
        b2t[:, s * 8:(s + 1) * 8] = d["L1_ffn_b2"][s].reshape(8, 128).T
        lngb[:, s * 16:s * 16 + 8] = d["L1_ffn_ln_g"][s].reshape(8, 128).T
        lngb[:, s * 16 + 8:s * 16 + 16] = \
            d["L1_ffn_ln_b"][s].reshape(8, 128).T
    out["p4_b2t"] = b2t
    out["p4_lngb"] = lngb
    qks = np.zeros((1, 256), np.float32)
    for s in range(2):
        qks[0, s * 128:s * 128 + 64] = d["L1_ffn_q_scale"][s]
        qks[0, s * 128 + 64:s * 128 + 128] = d["L1_ffn_k_scale"][s]
    out["p4_qks"] = qks
    return out


def make_p4_consts():
    half_idx = np.arange(0, 64, 2, dtype=np.float64)
    inv_freq = 1.0 / 10000.0 ** (half_idx / 64.0)
    t = np.arange(2048, dtype=np.float64)
    freqs = t[:, None] * inv_freq[None, :]
    freqs = np.concatenate([freqs, freqs], -1)
    base = (half_idx + 0.4 * 64.0) / (1.4 * 64.0)
    power = (t - 1024.0) / 256.0
    scale = base[None, :] ** power[:, None]
    scale = np.concatenate([scale, scale], -1)
    sign = np.concatenate([-np.ones(32), np.ones(32)])[None, :]
    rot = np.stack([np.cos(freqs) * scale, np.sin(freqs) * scale * sign,
                    np.cos(freqs) / scale,
                    np.sin(freqs) / scale * sign]).astype(np.float32)
    mk = np.zeros((128, 8, 512), np.float32)
    p = np.arange(128)[:, None]
    il = np.arange(512)[None, :]
    for kt in range(4):
        mk[:, kt] = (il <= kt * 128 + p)
    for kt in range(4, 8):
        mk[:, kt] = (il >= (kt - 4) * 128 + p)
    return {"p4_rot": rot, "p4_masks": mk.reshape(128, 4096)}


def build_p4():
    nc = bass.Bass("TRN2", target_bir_lowering=False)
    io = {}
    io["x3_in"] = nc.dram_tensor("x3_in", [NB, T, D], F32,
                                 kind="ExternalInput").ap()
    io["L1_ffn_wqkv"] = nc.dram_tensor("L1_ffn_wqkv", [2, D, 1536], F32,
                                       kind="ExternalInput").ap()
    io["L1_ffn_wo"] = nc.dram_tensor("L1_ffn_wo", [2, 512, D], F32,
                                     kind="ExternalInput").ap()
    for nm, shp in [("p4_w1t", [2, 1024, 5632]), ("p4_w2t", [2, 2816, 1024]),
                    ("p4_b1t", [128, 88]), ("p4_b2t", [128, 16]),
                    ("p4_lngb", [128, 32]), ("p4_qks", [1, 256]),
                    ("p4_rot", [4, 2048, 64]), ("p4_masks", [128, 4096]),
                    ("ident", [128, 128]), ("cpool", [128, 8])]:
        io[nm] = nc.dram_tensor(nm, shp, F32, kind="ExternalInput").ap()
    io["vT_dram"] = nc.dram_tensor("vT_dram", [2816, 2048], BF16).ap()
    io["x4_out"] = nc.dram_tensor("x4_out", [NB, T, D], F32,
                                  kind="ExternalOutput").ap()
    if os.environ.get("P4DBG", "0") == "1":
        io["dbg_q"] = nc.dram_tensor("dbg_q", [128, 8192], BF16,
                                     kind="ExternalOutput").ap()
        io["dbg_k"] = nc.dram_tensor("dbg_k", [128, 8192], BF16,
                                     kind="ExternalOutput").ap()
        io["dbg_xc"] = nc.dram_tensor("dbg_xc", [128, 16392], F32,
                                      kind="ExternalOutput").ap()
    es = ExitStack()
    with es:
        pb = PB(nc, es)
        emit_p4(pb, es, nc, io)
        print("P4 instructions:", pb.n_instr())
        pb.finalize()
    return nc


def build_launchA():
    nc = bass.Bass("TRN2", target_bir_lowering=False)
    io = {}
    io["x_in"] = nc.dram_tensor("x_in", [NB, T, D], F32,
                                kind="ExternalInput").ap()
    for nm, shp in P1_WNAMES:
        io[nm] = nc.dram_tensor(nm, shp, F32, kind="ExternalInput").ap()
    io["ident"] = nc.dram_tensor("ident", [128, 128], F32,
                                 kind="ExternalInput").ap()
    io["sel16"] = nc.dram_tensor("sel16", [48, 16 * 128], F32,
                                 kind="ExternalInput").ap()
    io["cpool"] = nc.dram_tensor("cpool", [128, 8], F32,
                                 kind="ExternalInput").ap()
    for nm, shp in [("xpw_t", [128, 16 * 96]), ("convw_t", [128, 64]),
                    ("convb_t", [128, 16]), ("dtb_t", [128, 16]),
                    ("a_t", [128, 256]), ("d_t", [128, 16])]:
        io[nm] = nc.dram_tensor(nm, shp, F32, kind="ExternalInput").ap()
    io["z_dram"] = nc.dram_tensor("z_dram", [NB, NCHUNK, 128, T], F32).ap()
    io["yg_dram"] = nc.dram_tensor("yg_dram", [NB, NCHUNK, 128, T], BF16).ap()
    io["x1_out"] = nc.dram_tensor("x1_out", [NB, T, D], F32,
                                  kind="ExternalOutput").ap()
    for nm, shp in P2_WNAMES:
        io[nm] = nc.dram_tensor(nm, shp, F32, kind="ExternalInput").ap()
    for nm, shp in [("bq_t", [128, 8]), ("bk_t", [128, 8]),
                    ("bo_t", [128, 8]), ("wr_t", [128, 64])]:
        io[nm] = nc.dram_tensor(nm, shp, F32, kind="ExternalInput").ap()
    io["x1b_out"] = nc.dram_tensor("x1b_out", [NB, T, D], F32,
                                   kind="ExternalOutput").ap()
    io["h2_out"] = nc.dram_tensor("h2_out", [NB, T, D], BF16,
                                  kind="ExternalOutput").ap()
    io["logits_out"] = nc.dram_tensor("logits_out", [NB, T, 8], F32,
                                      kind="ExternalOutput").ap()
    io["cnt_out"] = nc.dram_tensor("cnt_out", [8, 2], F32,
                                   kind="ExternalOutput").ap()
    es = ExitStack()
    with es:
        pb = PB(nc, es, alias=True)
        io["cpool_sb"] = sbt(es, nc, "cpool_shared", [128, 8], F32)
        with ExitStack() as es1:
            emit_p1(pb, es1, nc, io)
        pb.barrier("a1")
        io2 = dict(io)
        io2["x1_in"] = io["x1_out"]
        with ExitStack() as es2:
            emit_p2pre(pb, es2, nc, io2)
        print("LaunchA instructions:", pb.n_instr())
        pb.finalize()
    return nc


def build_launchB():
    nc = bass.Bass("TRN2", target_bir_lowering=False,
                   dynamic_dma_scratch_size=32768)
    io = {}
    io["x1_in"] = nc.dram_tensor("x1_in", [NB, T, D], F32,
                                 kind="ExternalInput").ap()
    io["x1b_in"] = nc.dram_tensor("x1b_in", [NB, T, D], F32,
                                  kind="ExternalInput").ap()
    io["h2_in"] = nc.dram_tensor("h2_in", [NB, T, D], BF16,
                                 kind="ExternalInput").ap()
    io["logits_in"] = nc.dram_tensor("logits_in", [NB, T, 8], F32,
                                     kind="ExternalInput").ap()
    io["cnt_all"] = nc.dram_tensor("cnt_all", [128, 1], F32,
                                   kind="ExternalInput").ap()
    io["L0_ffn_w1"] = nc.dram_tensor("L0_ffn_w1", [8, D, 4096], F32,
                                     kind="ExternalInput").ap()
    io["L0_ffn_w2"] = nc.dram_tensor("L0_ffn_w2", [8, 4096, D], F32,
                                     kind="ExternalInput").ap()
    for nm, shp in [("b1_t", [8, 128, 32]), ("b2_t", [8, 128, 8]),
                    ("Mmask", [128, 8]), ("Mmask2", [128, 8]),
                    ("ident", [128, 128]), ("cpool", [128, 8])]:
        io[nm] = nc.dram_tensor(nm, shp, F32, kind="ExternalInput").ap()
    io["tokmap"] = nc.dram_tensor("tokmap", [128 * 96, 2], U32).ap()
    io["w1bf"] = nc.dram_tensor("w1bf", [8, 8, 128, 4096], BF16).ap()
    io["w2bf"] = nc.dram_tensor("w2bf", [8, 32, 128, 1024], BF16).ap()
    io["yedram"] = nc.dram_tensor("yedram", [8 * KCAP, D], BF16).ap()
    io["x2h"] = nc.dram_tensor("x2h", [NB, T, D], F32).ap()
    io["L1_attn_W_qkv"] = nc.dram_tensor("L1_attn_W_qkv", [D, 3072], F32,
                                         kind="ExternalInput").ap()
    io["L1_attn_W_out"] = nc.dram_tensor("L1_attn_W_out", [D, D], F32,
                                         kind="ExternalInput").ap()
    for nm, shp in [("bq_row", [1, 1024]), ("bk_row", [1, 1024]),
                    ("bv_t", [128, 8]), ("bo_t", [128, 8])]:
        io[nm] = nc.dram_tensor(nm, shp, F32, kind="ExternalInput").ap()
    io["x3h"] = nc.dram_tensor("x3h", [NB, T, D], F32).ap()
    io["L1_ffn_wqkv"] = nc.dram_tensor("L1_ffn_wqkv", [2, D, 1536], F32,
                                       kind="ExternalInput").ap()
    io["L1_ffn_wo"] = nc.dram_tensor("L1_ffn_wo", [2, 512, D], F32,
                                     kind="ExternalInput").ap()
    for nm, shp in [("p4_w1t", [2, 1024, 5632]), ("p4_w2t", [2, 2816, 1024]),
                    ("p4_b1t", [128, 88]), ("p4_b2t", [128, 16]),
                    ("p4_lngb", [128, 32]), ("p4_qks", [1, 256]),
                    ("p4_rot", [4, 2048, 64]), ("p4_masks", [128, 4096])]:
        io[nm] = nc.dram_tensor(nm, shp, F32, kind="ExternalInput").ap()
    io["vT_dram"] = nc.dram_tensor("vT_dram", [2816, 2048], BF16).ap()
    io["x4_out"] = nc.dram_tensor("x4_out", [NB, T, D], F32,
                                  kind="ExternalOutput").ap()
    es = ExitStack()
    with es:
        pb = PB(nc, es, alias=True)
        io["cpool_sb"] = sbt(es, nc, "cpool_shared", [128, 8], F32)
        io2 = dict(io)
        io2["x2_out"] = io["x2h"]
        with ExitStack() as es1:
            emit_p2post(pb, es1, nc, io2)
        pb.barrier("b1")
        io3 = dict(io)
        io3["x2_in"] = io["x2h"]
        io3["x3_out"] = io["x3h"]
        with ExitStack() as es2:
            emit_p3(pb, es2, nc, io3)
        pb.barrier("b2")
        io4 = dict(io)
        io4["x3_in"] = io["x3h"]
        with ExitStack() as es3:
            emit_p4(pb, es3, nc, io4)
        print("LaunchB instructions:", pb.n_instr())
        pb.finalize()
    return nc


def kernel(**inputs):
    d = {k: np.asarray(v) for k, v in inputs.items()}
    x0 = d["L0_attn_x"]
    csts = make_consts()
    CORES = list(range(8))

    ncA = build_launchA()
    sh = {nm: d[nm] for nm, _ in P1_WNAMES}
    sh.update(prep_p1_weights(d))
    sh.update(csts)
    for nm in ["L0_ffn_wq", "L0_ffn_wk", "L0_ffn_wv", "L0_ffn_wo"]:
        sh[nm] = d[nm]
    sh.update(prep_p2_weights(d))
    insA = [dict(sh, x_in=np.ascontiguousarray(x0[2 * c:2 * c + 2]))
            for c in CORES]
    rA = run_bass_kernel_spmd(ncA, insA, core_ids=CORES)

    cnt_all = np.ascontiguousarray(
        np.stack([rA.results[c]["cnt_out"].T for c in CORES])
        .astype(np.float32).reshape(128, 1))

    ncB = build_launchB()
    sh = {"L0_ffn_w1": d["L0_ffn_w1"], "L0_ffn_w2": d["L0_ffn_w2"],
          "ident": csts["ident"], "cpool": csts["cpool"],
          "cnt_all": cnt_all,
          "L1_attn_W_qkv": d["L1_attn_W_qkv"],
          "L1_attn_W_out": d["L1_attn_W_out"],
          "L1_ffn_wqkv": d["L1_ffn_wqkv"], "L1_ffn_wo": d["L1_ffn_wo"]}
    sh.update(prep_p3_weights(d))
    sh.update(prep_p4_weights(d))
    sh.update(make_p4_consts())
    insB = []
    for c in CORES:
        m = dict(sh, x1_in=rA.results[c]["x1_out"],
                 x1b_in=rA.results[c]["x1b_out"],
                 h2_in=rA.results[c]["h2_out"],
                 logits_in=rA.results[c]["logits_out"])
        m.update(prep_p2post_weights(d, core=c))
        insB.append(m)
    rB = run_bass_kernel_spmd(ncB, insB, core_ids=CORES)
    out = np.concatenate([rB.results[c]["x4_out"] for c in CORES], axis=0)
    return np.ascontiguousarray(out.astype(np.float32))


# version 09 (final)
# speedup vs baseline: 1.1690x; 1.1690x over previous
import os
import numpy as np
import ml_dtypes
from contextlib import ExitStack

import concourse.bass as bass
import concourse.mybir as mybir
from concourse.alu_op_type import AluOpType as OP
from concourse.bass_utils import run_bass_kernel_spmd
from concourse.bass import IndirectOffsetOnAxis

F32 = mybir.dt.float32
BF16 = mybir.dt.bfloat16
I32 = mybir.dt.int32
U32 = mybir.dt.uint32
AF = mybir.ActivationFunctionType
AX = mybir.AxisListType
BFNP = ml_dtypes.bfloat16

NB = int(os.environ.get("P1NB", "2"))
P1ST = int(os.environ.get("P1ST", "5"))
P1SA = int(os.environ.get("P1SA", "4"))
T = 2048
D = 1024
DI = 2048
DS = 16
DTR = 64
NCHUNK = DI // 128
KCH = D // 128
TSUB = 1024
NSUB = T // TSUB
LN_EPS = 1e-5


class PB:

    ENGS = ("sync", "tensor", "vector", "scalar", "gpsimd")

    def __init__(self, nc, es, alias=False):
        self.nc = nc
        self.es = es
        self.recs = {e: [] for e in self.ENGS}
        self.sems = {}
        self.cnt = {}
        self.alias = alias
        self.lbl2phys = {}

    def _phys(self, label, default):
        if not self.alias:
            return label
        if label not in self.lbl2phys:
            self.lbl2phys[label] = default
        assert self.lbl2phys[label] == default, (label, default)
        return self.lbl2phys[label]

    def sem(self, name):
        if name not in self.sems:
            self.sems[name] = self.es.enter_context(self.nc.semaphore(name))
            self.cnt[name] = 0
        return self.sems[name]

    CHAIN_ENGS = ("vector", "scalar", "gpsimd")

    def _mark(self, label, sname, value):
        if not hasattr(self, "hist"):
            self.hist = {}
        self.hist.setdefault(label, []).append((sname, value))

    def wait(self, eng, label, val=None):
        if not hasattr(self, "hist"):
            self.hist = {}
        ents = self.hist.get(label, [])
        k = len(ents) if val is None else val
        if k <= 0 or not ents:
            return
        assert k <= len(ents), (label, k, len(ents))
        if hasattr(self, "qeng") and label in self.qeng:
            ename, ev = self._dma_cut(label)
            if eng != self.qeng[label]:
                e = self.sems[ename]
                self.recs[eng].append(
                    lambda E, e=e, ev=ev: E.wait_ge(e, ev))
            return
        sname, v = ents[k - 1]
        s = self.sem(sname)
        self.recs[eng].append(lambda E, s=s, v=v: E.wait_ge(s, v))

    def op(self, eng, build, sem=None, inc=1, dma16=False):
        if dma16:
            phys = self._phys(sem, "Q_" + eng)
            s = self.sem(phys)
            self.cnt[phys] += 16
            if not hasattr(self, "qeng"):
                self.qeng = {}
                self.echo_done = {}
                self.echo_cnt = {}
            self.qeng[sem] = eng
            self.recs[eng].append(
                lambda E, b=build, s=s: b(E).then_inc(s, 16))
            self._mark(sem, phys, self.cnt[phys])
            return
        if eng in self.CHAIN_ENGS:
            ch = "_ch_" + eng
            s = self.sem(ch)
            if not hasattr(self, "_last"):
                self._last = {}
            last = self._last.get(eng)
            if last is not None and last < self.cnt[ch]:
                pass
            if last is not None:
                self.recs[eng].append(
                    lambda E, s=s, v=last: E.wait_ge(s, v))
            self.cnt[ch] += 1
            v = self.cnt[ch]
            self.recs[eng].append(
                lambda E, b=build, s=s: b(E).then_inc(s, 1))
            self._last[eng] = v
            if sem is not None:
                self._mark(sem, ch, v)
        else:
            if sem is not None:
                phys = self._phys(sem, "T_" + eng)
                s = self.sem(phys)
                self.cnt[phys] += inc
                self.recs[eng].append(
                    lambda E, b=build, s=s, i=inc: b(E).then_inc(s, i))
                self._mark(sem, phys, self.cnt[phys])
            else:
                self.recs[eng].append(lambda E, b=build: b(E))

    def dma(self, eng, out, in_, sem, **kw):
        phys = self._phys(sem, "Q_" + eng)
        s = self.sem(phys)
        self.cnt[phys] += 16
        if not hasattr(self, "qeng"):
            self.qeng = {}
            self.echo_done = {}
            self.echo_cnt = {}
        self.qeng[sem] = eng
        self.recs[eng].append(
            lambda E, o=out, i=in_, s=s, kw=kw:
                E.dma_start(out=o, in_=i, **kw).then_inc(s, 16))
        self._mark(sem, phys, self.cnt[phys])

    def _dma_cut(self, label):
        qe = self.qeng[label]
        phys = self._phys(label, "Q_" + qe)
        total = self.cnt[phys]
        if self.echo_done.get(phys) != total:
            s = self.sem(phys)
            e = self.sem("e_" + phys)
            self.recs[qe].append(lambda E, s=s, v=total: E.wait_ge(s, v))
            self.cnt["e_" + phys] += 1
            ev = self.cnt["e_" + phys]
            self.recs[qe].append(
                lambda E, e=e, ev=ev: E.nop().then_inc(e, 1))
            self.echo_done[phys] = total
            self.echo_cnt[phys] = ev
        return ("e_" + phys, self.echo_cnt[phys])

    def barrier(self, tag):
        if hasattr(self, "qeng"):
            for lab in list(self.qeng):
                self._dma_cut(lab)
        s = self.sem("BAR")
        self.cnt["BAR"] += 5
        v = self.cnt["BAR"]
        for e in self.ENGS:
            self.recs[e].append(lambda E, s=s: E.nop().then_inc(s, 1))
        for e in self.ENGS:
            self.recs[e].append(lambda E, s=s, v=v: E.wait_ge(s, v))

    def c(self, label):
        if not hasattr(self, "hist"):
            self.hist = {}
        return len(self.hist.get(label, []))

    def n_instr(self):
        return sum(len(v) for v in self.recs.values())

    def finalize(self):
        with self.nc.Block() as block:
            for eng_name in self.ENGS:
                recs = self.recs[eng_name]

                def run(E, recs=recs):
                    for r in recs:
                        r(E)

                getattr(block, eng_name)(run)


def sbt(es, nc, name, shape, dtype):
    return es.enter_context(nc.sbuf_tensor(name, shape, dtype))


def pst(es, nc, name, shape, dtype):
    return es.enter_context(nc.psum_tensor(name, shape, dtype))



def emit_p1(pb, es, nc, io):
    U8 = mybir.dt.uint8
    ovl = sbt(es, nc, "p1_ovl", [128, 160 * 1024], U8)

    def view(off_kb, ncols, dtype):
        size = ncols * (4 if dtype == F32 else 2 if dtype in (BF16,) else 1)
        return ovl.ap()[:, off_kb * 1024:off_kb * 1024 + size].bitcast(dtype)

    xbc = view(0, 32768, BF16)
    w_half = view(64, 16384, BF16)
    ln0T = view(96, 16384, BF16)
    bbc_all = view(64, 16384, BF16)
    cbc_all = view(96, 16384, BF16)
    scano = view(128, 16384, BF16)
    wout = view(0, 16384, BF16)
    aT = view(32, 16384, BF16)
    ygld = view(64, 8192, BF16)
    atok = view(80, 1024, BF16)
    lntok = view(128, 1024, BF16)
    xbf = view(130, 2064, F32)

    def bbc(s):
        return bbc_all[:, s * TSUB:(s + 1) * TSUB]

    def cbc(s):
        return cbc_all[:, s * TSUB:(s + 1) * TSUB]

    cpool = (io["cpool_sb"] if "cpool_sb" in io else sbt(es, nc, "cpool_sb", [128, 8], F32))
    ident = sbt(es, nc, "p1_ident", [128, 128], BF16)
    sel16 = sbt(es, nc, "p1_sel16", [48, 16 * 128], BF16)
    xpw = sbt(es, nc, "p1_xpw", [128, 16 * 96], BF16)
    dtw = sbt(es, nc, "p1_dtw", [64, DI], BF16)
    conv_w = sbt(es, nc, "p1_convw", [128, 16 * 4], F32)
    conv_b = sbt(es, nc, "p1_convb", [128, 16], F32)
    dt_b = sbt(es, nc, "p1_dtb", [128, 16], F32)
    a_sb = sbt(es, nc, "p1_a", [128, 16 * 16], F32)
    d_sb = sbt(es, nc, "p1_d", [128, 16], F32)

    x0t = sbt(es, nc, "p1_x0t", [128, D], F32)
    stat = sbt(es, nc, "p1_stat", [128, 12], F32)
    mv = sbt(es, nc, "p1_mv", [128, 4], F32)
    rstd = sbt(es, nc, "p1_rstd", [128, 2], F32)
    zscr = sbt(es, nc, "p1_zscr", [128, TSUB], F32)
    zbf = zscr.ap()[:, 0:TSUB // 2].bitcast(BF16)
    projb = sbt(es, nc, "p1_projb", [64, T], BF16)
    bcprj = sbt(es, nc, "p1_bcprj", [64, T], BF16)
    dtbf = sbt(es, nc, "p1_dtbf", [128, TSUB], BF16)
    dtu = sbt(es, nc, "p1_dtu", [128, TSUB], BF16)
    dabf = sbt(es, nc, "p1_dabf", [128, 2 * TSUB], BF16)
    dbubf = sbt(es, nc, "p1_dbubf", [128, TSUB], BF16)
    ytmp = sbt(es, nc, "p1_ytmp", [128, TSUB], BF16)
    skeep = sbt(es, nc, "p1_skeep", [128, 16 * 16], F32)
    ygt = sbt(es, nc, "p1_ygt", [128, TSUB], F32)
    siluz = sbt(es, nc, "p1_siluz", [128, TSUB], BF16)
    x1t = zscr

    psA = pst(es, nc, "p1_psA", [128, 2048], F32)
    psB = pst(es, nc, "p1_psB", [128, 1024], F32)
    psT = pst(es, nc, "p1_psT", [128, 2048], BF16)

    CVALS = [1e-5, 1.0, -0.5, 0.0, 0.5, 2.0, -1.0, 8.0]
    for i, v in enumerate(CVALS):
        nc.const_aps.register(F32, v, cpool[:, i:i + 1])
    pb.dma("sync", cpool[:], io["cpool"], "qs")
    pb.dma("gpsimd", ident[:], io["ident"], "qgp")
    pb.dma("gpsimd", sel16[:], io["sel16"], "qgp")
    pb.dma("gpsimd", xpw[:], io["xpw_t"], "qgp")
    pb.dma("gpsimd", dtw[0:64, :], io["L0_attn_dt_proj_w"], "qgp")
    pb.dma("sync", conv_w[:], io["convw_t"], "qs")
    pb.dma("sync", conv_b[:], io["convb_t"], "qs")
    pb.dma("sync", dt_b[:], io["dtb_t"], "qs")
    pb.dma("sync", a_sb[:], io["a_t"], "qs")
    pb.dma("sync", d_sb[:], io["d_t"], "qs")
    pb.wait("scalar", "qs")
    pb.wait("tensor", "qgp")
    pb.wait("vector", "qs")
    pb.wait("vector", "qgp")
    pb.op("scalar", lambda E: E.activation(a_sb[:], a_sb[:], AF.Exp), sem="act0")
    pb.wait("vector", "act0")
    pb.op("vector", lambda E: E.tensor_scalar(
        a_sb[:], a_sb[:], -1.0, None, OP.mult), sem="va0")
    for bi in range(NB):
        if P1ST < 1:
            continue
        for tt in range(16):
            x0v = x0t[:, 0:D]
            pb.wait("sync", "va_ln", pb.c("va_ln"))
            pb.dma("sync", x0v, io["x_in"][bi, tt * 128:(tt + 1) * 128, :], "qx")
            pb.wait("vector", "qx")
            pb.wait("vector", "mtr", pb.c("mtr"))
            pb.op("vector", lambda E, x=x0v: E.bn_stats(stat[:, 0:6], x[:, 0:512]))
            pb.op("vector", lambda E, x=x0v: E.bn_stats(stat[:, 6:12], x[:, 512:1024]))
            pb.op("vector", lambda E: E.bn_aggr(mv[:, 0:2], stat[:, 0:12]), sem="vs")
            if P1SA < 2:
                continue
            pb.wait("scalar", "vs")
            pb.op("scalar", lambda E: E.activation(
                rstd[:, 0:1], mv[:, 1:2], AF.Ln, bias=LN_EPS), sem="act")
            pb.wait("scalar", "act")
            pb.op("scalar", lambda E: E.activation(
                rstd[:, 1:2], rstd[:, 0:1], AF.Exp, scale=-0.5), sem="act")
            if P1SA < 3:
                continue
            pb.wait("vector", "act")
            pb.op("vector", lambda E, x=x0v: E.tensor_scalar(
                lntok[:], x, mv[:, 0:1], rstd[:, 1:2], OP.subtract, OP.mult),
                sem="va_ln")
            if P1SA < 4:
                continue
            pb.wait("tensor", "va_ln")
            for c in range(KCH):
                pslot = psT[:, (c % 2) * 1024:(c % 2) * 1024 + 128]
                pb.wait("tensor", "vtr", pb.c("vtr") - 1)
                pb.op("tensor", lambda E, p=pslot, c=c: E.transpose(
                    p, lntok[:, c * 128:(c + 1) * 128], ident[:]), sem="mtr")
                pb.wait("vector", "mtr")
                pb.op("vector", lambda E, p=pslot, c=c, tt=tt: E.tensor_copy(
                    ln0T[:, c * 2048 + tt * 128: c * 2048 + (tt + 1) * 128], p),
                    sem="vtr")

        if P1ST < 2:
            continue
        pb.wait("tensor", "vtr")
        pb.op("vector", lambda E: E.memset(xbf[:, 13:16], 0.0), sem="va0")
        for pj in range(2):
            if bi > 0 or pj > 0:
                pb.wait("gpsimd", "vG", pb.c("vG"))
                pb.wait("gpsimd", "mmA", pb.c("mmA"))
                pb.wait("gpsimd", "mmO", pb.c("mmO"))
                pb.wait("gpsimd", "va_f", pb.c("va_f"))
                pb.wait("gpsimd", "vx1", pb.c("vx1"))
                pb.wait("scalar", "qx1", pb.c("qx1"))
            for kc in range(KCH):
                pb.dma("gpsimd", w_half[:, kc * 2048:(kc + 1) * 2048],
                       io["L0_attn_in_proj_w"].rearrange(
                           "(k p) n -> k p n", p=128)[kc][:, pj * 2048:(pj + 1) * 2048],
                       "qw")
            pb.wait("tensor", "qw")
            for jj in range(16):
                pb.wait("tensor", "actA", pb.c("actA"))
                pb.wait("tensor", "actZ", pb.c("actZ"))
                for ns in range(4):
                    for kc in range(KCH):
                        pb.op("tensor", lambda E, jj=jj, ns=ns, kc=kc: E.matmul(
                            psA[:, ns * 512:(ns + 1) * 512],
                            w_half[:, kc * 2048 + jj * 128: kc * 2048 + (jj + 1) * 128],
                            ln0T[:, kc * 2048 + ns * 512: kc * 2048 + (ns + 1) * 512],
                            start=(kc == 0), stop=(kc == KCH - 1)),
                            sem=("mmA" if (ns == 3 and kc == KCH - 1) else None))
                if pj == 0:
                    c = jj
                    pb.wait("scalar", "mmA")
                    pb.wait("scalar", "vcv", pb.c("vcv"))
                    pb.op("scalar", lambda E: E.activation(
                        xbf[:, 16:16 + T], psA[:], AF.Copy, bias=0.0), sem="actA")
                    pb.wait("vector", "actA")
                    for hh in range(2):
                        o = hh * 1024
                        pb.wait("vector", "actX", pb.c("actX"))
                        pb.op("vector", lambda E, c=c, o=o: E.tensor_scalar(
                            ygt[:], xbf[:, 13 + o:13 + o + 1024],
                            conv_w[:, c * 4:c * 4 + 1], None, OP.mult))
                        pb.op("vector", lambda E, c=c, o=o: E.scalar_tensor_tensor(
                            siluz[:], xbf[:, 14 + o:14 + o + 1024],
                            conv_w[:, c * 4 + 1:c * 4 + 2], ygt[:], OP.mult, OP.add))
                        pb.op("vector", lambda E, c=c, o=o: E.scalar_tensor_tensor(
                            ygt[:], xbf[:, 15 + o:15 + o + 1024],
                            conv_w[:, c * 4 + 2:c * 4 + 3], siluz[:], OP.mult, OP.add))
                        pb.op("vector", lambda E, c=c, o=o: E.scalar_tensor_tensor(
                            siluz[:], xbf[:, 16 + o:16 + o + 1024],
                            conv_w[:, c * 4 + 3:c * 4 + 4], ygt[:], OP.mult, OP.add),
                            sem="vcv")
                        pb.wait("scalar", "vcv")
                        pb.op("scalar", lambda E, c=c, o=o: E.activation(
                            xbc[:, c * 2048 + o:c * 2048 + o + 1024], siluz[:],
                            AF.Silu, bias=conv_b[:, c:c + 1]), sem="actX")
                else:
                    c = jj
                    pb.wait("scalar", "mmA")
                    for hh in range(2):
                        o = hh * 1024
                        pb.wait("scalar", "qz", pb.c("qz"))
                        pb.op("scalar", lambda E, o=o: E.activation(
                            ytmp[:], psA[:, o:o + 1024], AF.Copy, bias=0.0),
                            sem="actZ")
                        pb.wait("sync", "actZ")
                        pb.dma("sync", io["z_dram"][bi, c][:, o:o + 1024],
                               ytmp[:], "qz")

        if P1ST < 3:
            continue
        pb.wait("tensor", "actX")
        pb.wait("tensor", "actZ", pb.c("actZ"))
        for ns in range(4):
            for grp, (lo, m) in enumerate([(0, 64), (64, 16), (80, 16)]):
                tgt = [psB[0:64, 0:512], psB[0:16, 512:1024], psA[0:16, 0:512]][grp]
                for kc in range(NCHUNK):
                    pb.op("tensor", lambda E, ns=ns, kc=kc, lo=lo, m=m, tgt=tgt:
                          E.matmul(
                              tgt, xpw[:, kc * 96 + lo:kc * 96 + lo + m],
                              xbc[:, kc * 2048 + ns * 512:
                                  kc * 2048 + (ns + 1) * 512],
                              start=(kc == 0), stop=(kc == NCHUNK - 1)),
                          sem=("mmP" if kc == NCHUNK - 1 else None))
            pb.wait("scalar", "mmP")
            pb.op("scalar", lambda E, ns=ns: E.activation(
                projb[0:64, ns * 512:(ns + 1) * 512],
                psB[0:64, 0:512], AF.Copy, bias=0.0), sem="actP")
            pb.wait("vector", "mmP")
            pb.op("vector", lambda E, ns=ns: E.tensor_copy(
                bcprj[0:16, ns * 512:(ns + 1) * 512], psB[0:16, 512:1024]), sem="vP")
            pb.op("vector", lambda E, ns=ns: E.tensor_copy(
                bcprj[32:48, ns * 512:(ns + 1) * 512], psA[0:16, 0:512]), sem="vP")
            pb.wait("tensor", "actP")
            pb.wait("tensor", "vP")

        if P1ST < 4:
            continue
        pb.wait("vector", "qz")
        pb.op("vector", lambda E: E.memset(skeep[:], 0.0), sem="vmem")
        for tsub in range(NSUB):
            t0 = tsub * TSUB
            pb.wait("tensor", "actD", pb.c("actD"))
            pb.wait("tensor", "aB", pb.c("aB"))
            pb.wait("tensor", "vB", pb.c("vB"))
            for s in range(16):
                for h in range(2):
                    pb.op("tensor", lambda E, s=s, h=h, t0=t0: E.matmul(
                        psB[:, h * 512:(h + 1) * 512],
                        sel16[0:16, s * 128:(s + 1) * 128],
                        bcprj[0:16, t0 + h * 512:t0 + (h + 1) * 512],
                        start=True, stop=True), sem=None)
                    pb.op("tensor", lambda E, s=s, h=h, t0=t0: E.matmul(
                        psA[:, h * 512:(h + 1) * 512],
                        sel16[32:48, s * 128:(s + 1) * 128],
                        bcprj[32:48, t0 + h * 512:t0 + (h + 1) * 512],
                        start=True, stop=True),
                        sem=("mmB" if h == 1 else None))
                pb.wait("vector", "mmB")
                pb.op("vector", lambda E, s=s: E.tensor_copy(
                    bbc(s), psB[:, 0:TSUB]), sem="vB")
                pb.wait("scalar", "mmB")
                pb.op("scalar", lambda E, s=s: E.activation(
                    cbc(s), psA[:, 0:TSUB], AF.Copy, bias=0.0), sem="aB")
                pb.wait("tensor", "vB")
                pb.wait("tensor", "aB")
            for c in range(NCHUNK):
                for h in range(2):
                    tgt = psB[:, 512:1024] if h == 0 else psB[:, 0:512]
                    pb.op("tensor", lambda E, c=c, h=h, t0=t0, tgt=tgt: E.matmul(
                        tgt, dtw[0:64, c * 128:(c + 1) * 128],
                        projb[0:64, t0 + h * 512:t0 + (h + 1) * 512],
                        start=True, stop=True), sem=("mmD" if h == 1 else None))
                pb.wait("scalar", "mmD")
                pb.op("scalar", lambda E, c=c: E.activation(
                    dtbf[:, 0:512], psB[:, 512:1024], AF.Exp,
                    bias=dt_b[:, c:c + 1]), sem="actD")
                pb.op("scalar", lambda E, c=c: E.activation(
                    dtbf[:, 512:1024], psB[:, 0:512], AF.Exp,
                    bias=dt_b[:, c:c + 1]), sem="actD")
                pb.op("scalar", lambda E: E.activation(
                    dtbf[:], dtbf[:], AF.Ln, bias=1.0), sem="actD2")
                pb.wait("vector", "actD2")
                pb.op("vector", lambda E, c=c, t0=t0: E.tensor_tensor(
                    dtu[:], dtbf[:], xbc[:, c * 2048 + t0:c * 2048 + t0 + TSUB],
                    OP.mult), sem="vD")
                for s in range(16):
                    da_s = dabf[:, (s % 2) * TSUB:(s % 2 + 1) * TSUB]
                    pb.wait("scalar", "vS2", pb.c("vS2") - 1)
                    pb.op("scalar", lambda E, c=c, s=s, da=da_s: E.activation(
                        da, dtbf[:], AF.Exp,
                        scale=a_sb[:, c * 16 + s:c * 16 + s + 1]), sem="actS")
                    pb.op("vector", lambda E, s=s: E.tensor_tensor(
                        dbubf[:], dtu[:], bbc(s), OP.mult), sem="vS1")
                    pb.wait("vector", "actS")
                    pb.op("vector", lambda E, c=c, s=s, da=da_s:
                          E.tensor_tensor_scan(
                              scano[:, s * TSUB:(s + 1) * TSUB], da, dbubf[:],
                              skeep[:, c * 16 + s:c * 16 + s + 1],
                              OP.mult, OP.add), sem="vS2")
                    pb.wait("vector", "mty", pb.c("mty"))
                    if s == 0:
                        pb.wait("vector", "qyg", pb.c("qyg"))
                    pb.op("vector", lambda E, s=s: E.tensor_tensor(
                        ytmp[:], scano[:, s * TSUB:(s + 1) * TSUB], cbc(s),
                        OP.mult), sem="vS3")
                    pb.wait("tensor", "vS3")
                    for h in range(2):
                        pb.op("tensor", lambda E, s=s, h=h: E.matmul(
                            psA[:, 1024 + h * 512:1024 + (h + 1) * 512],
                            ident[:], ytmp[:, h * 512:(h + 1) * 512],
                            start=(s == 0), stop=(s == 15)),
                            sem=("mty" if h == 1 else None))
                pb.wait("scalar", "vS2", pb.c("vS2") - 1)
                pb.op("vector", lambda E, c=c: E.tensor_copy(
                    skeep[:, c * 16:(c + 1) * 16],
                    scano.rearrange("p (s t) -> p s t", t=TSUB)[:, :, TSUB - 1]),
                    sem="vK")
                pb.wait("sync", "qz")
                pb.wait("sync", "actG", pb.c("actG"))
                pb.dma("sync", zbf[:, 0:TSUB],
                       io["z_dram"][bi, c][:, t0:t0 + TSUB], "qzl")
                pb.wait("scalar", "qzl")
                pb.wait("scalar", "vG", pb.c("vG"))
                pb.op("scalar", lambda E: E.activation(
                    siluz[:], zbf[:, 0:TSUB], AF.Silu), sem="actG")
                pb.wait("vector", "mty")
                pb.wait("vector", "actG")
                pb.wait("vector", "qyg", pb.c("qyg"))
                pb.op("vector", lambda E, c=c, t0=t0: E.scalar_tensor_tensor(
                    ygt[:], xbc[:, c * 2048 + t0:c * 2048 + t0 + TSUB],
                    d_sb[:, c:c + 1], psA[:, 1024:2048], OP.mult, OP.add),
                    sem="vG")
                pb.op("vector", lambda E: E.tensor_tensor(
                    ytmp[:], ygt[:], siluz[:], OP.mult), sem="vG")
                pb.wait("sync", "vG")
                pb.dma("sync", io["yg_dram"][bi, c][:, t0:t0 + TSUB],
                       ytmp[:], "qyg")
                pb.wait("tensor", "vG")

        if P1ST < 5:
            continue
        pb.wait("gpsimd", "vG")
        for kc in range(NCHUNK):
            pb.dma("gpsimd", wout[:, kc * 1024:(kc + 1) * 1024],
                   io["L0_attn_out_proj_w"].rearrange(
                       "(k p) n -> k p n", p=128)[kc], "qwo")
        pb.wait("tensor", "qwo")
        pb.wait("sync", "vG")
        for ts4 in range(4):
            pb.wait("sync", "qyg")
            pb.wait("sync", "mmO", pb.c("mmO"))
            for c in range(NCHUNK):
                pb.dma("sync", ygld[:, c * 512:(c + 1) * 512],
                       io["yg_dram"][bi, c][:, ts4 * 512:(ts4 + 1) * 512],
                       "qygl")
            pb.wait("tensor", "qygl")
            for oc in range(8):
                pbank = psA[:, (oc % 2) * 512:(oc % 2) * 512 + 512]
                pb.wait("tensor", "vE", pb.c("vE") - 1)
                for kc in range(NCHUNK):
                    pb.op("tensor",
                          lambda E, oc=oc, kc=kc, p=pbank: E.matmul(
                              p, wout[:, kc * 1024 + oc * 128:
                                      kc * 1024 + (oc + 1) * 128],
                              ygld[:, kc * 512:(kc + 1) * 512],
                              start=(kc == 0), stop=(kc == NCHUNK - 1)),
                          sem=("mmO" if kc == NCHUNK - 1 else None))
                pb.wait("vector", "mmO")
                pb.op("vector",
                      lambda E, oc=oc, ts4=ts4, p=pbank: E.tensor_copy(
                          aT[:, oc * 2048 + ts4 * 512:
                             oc * 2048 + (ts4 + 1) * 512],
                          p), sem="vE")
        pb.wait("tensor", "vE")
        for tt in range(16):
            pb.wait("vector", "va_f", pb.c("va_f"))
            for c in range(8):
                pslot = psT[:, (c % 2) * 1024:(c % 2) * 1024 + 128]
                pb.wait("tensor", "vF", pb.c("vF") - 1)
                pb.op("tensor", lambda E, c=c, tt=tt, p=pslot: E.transpose(
                    p, aT[:, c * 2048 + tt * 128: c * 2048 + (tt + 1) * 128],
                    ident[:]), sem="mtr2")
                pb.wait("vector", "mtr2")
                pb.op("vector", lambda E, c=c, p=pslot: E.tensor_copy(
                    atok[:, c * 128:(c + 1) * 128], p), sem="vF")
            x0v = x0t[:, 0:D]
            pb.wait("sync", "vx1", pb.c("vx1"))
            pb.dma("sync", x0v, io["x_in"][bi, tt * 128:(tt + 1) * 128, :], "qx2")
            pb.op("vector", lambda E: E.bn_stats(stat[:, 0:6], atok[:, 0:512]))
            pb.op("vector", lambda E: E.bn_stats(stat[:, 6:12], atok[:, 512:1024]))
            pb.op("vector", lambda E: E.bn_aggr(mv[:, 2:4], stat[:, 0:12]), sem="vs2")
            pb.wait("scalar", "vs2")
            pb.op("scalar", lambda E: E.activation(
                rstd[:, 0:1], mv[:, 3:4], AF.Ln, bias=LN_EPS), sem="act2")
            pb.wait("scalar", "act2")
            pb.op("scalar", lambda E: E.activation(
                rstd[:, 1:2], rstd[:, 0:1], AF.Exp, scale=-0.5), sem="act2")
            pb.wait("vector", "act2")
            pb.wait("vector", "qx1", pb.c("qx1"))
            pb.op("vector", lambda E: E.tensor_scalar(
                x1t[:], atok[:], mv[:, 2:3], rstd[:, 1:2],
                OP.subtract, OP.mult), sem="va_f")
            pb.wait("vector", "qx2")
            pb.op("vector", lambda E, x=x0v: E.tensor_tensor(
                x1t[:], x1t[:], x, OP.add), sem="vx1")
            pb.wait("sync", "vx1")
            pb.dma("sync", io["x1_out"][bi, tt * 128:(tt + 1) * 128, :],
                   x1t[:], "qx1")

    pb.wait("sync", "qx1")
    pb.wait("sync", "qyg")


def prep_p1_weights(d):
    out = {}
    out["xpw_t"] = np.ascontiguousarray(
        d["L0_attn_x_proj_w"].reshape(16, 128, 96).transpose(1, 0, 2)
        .reshape(128, 16 * 96))
    out["convw_t"] = np.ascontiguousarray(
        d["L0_attn_conv_w"].reshape(16, 128, 4).transpose(1, 0, 2)
        .reshape(128, 64))
    out["convb_t"] = np.ascontiguousarray(
        d["L0_attn_conv_b"].reshape(16, 128).T)
    out["dtb_t"] = np.ascontiguousarray(
        d["L0_attn_dt_proj_b"].reshape(16, 128).T)
    out["a_t"] = np.ascontiguousarray(
        d["L0_attn_A_log"].reshape(16, 128, 16).transpose(1, 0, 2)
        .reshape(128, 256))
    out["d_t"] = np.ascontiguousarray(
        d["L0_attn_D_param"].reshape(16, 128).T)
    return out


def make_consts():
    ident = np.eye(128, dtype=np.float32)
    sel16 = np.zeros((48, 16 * 128), np.float32)
    for s in range(16):
        sel16[s, s * 128:(s + 1) * 128] = 1.0
        sel16[32 + s, s * 128:(s + 1) * 128] = 1.0
    cpool = np.zeros((128, 8), np.float32)
    for i, v in enumerate([1e-5, 1.0, -0.5, 0.0, 0.5, 2.0, -1.0, 8.0]):
        cpool[:, i] = v
    return {"ident": ident, "sel16": sel16, "cpool": cpool}


P1_WNAMES = [
    ("L0_attn_in_proj_w", [D, 2 * DI]), ("L0_attn_conv_w", [DI, 4]),
    ("L0_attn_conv_b", [DI]), ("L0_attn_x_proj_w", [DI, 96]),
    ("L0_attn_dt_proj_w", [DTR, DI]), ("L0_attn_dt_proj_b", [DI]),
    ("L0_attn_A_log", [DI, DS]), ("L0_attn_D_param", [DI]),
    ("L0_attn_out_proj_w", [DI, D]),
]


def build_p1():
    nc = bass.Bass("TRN2", target_bir_lowering=False)
    io = {}
    io["x_in"] = nc.dram_tensor("x_in", [NB, T, D], F32, kind="ExternalInput").ap()
    for nm, shp in P1_WNAMES:
        io[nm] = nc.dram_tensor(nm, shp, F32, kind="ExternalInput").ap()
    io["ident"] = nc.dram_tensor("ident", [128, 128], F32, kind="ExternalInput").ap()
    io["sel16"] = nc.dram_tensor("sel16", [48, 16 * 128], F32,
                                 kind="ExternalInput").ap()
    io["cpool"] = nc.dram_tensor("cpool", [128, 8], F32,
                                 kind="ExternalInput").ap()
    for nm, shp in [("xpw_t", [128, 16 * 96]), ("convw_t", [128, 64]),
                    ("convb_t", [128, 16]), ("dtb_t", [128, 16]),
                    ("a_t", [128, 256]), ("d_t", [128, 16])]:
        io[nm] = nc.dram_tensor(nm, shp, F32, kind="ExternalInput").ap()
    io["z_dram"] = nc.dram_tensor("z_dram", [NB, NCHUNK, 128, T], BF16).ap()
    io["yg_dram"] = nc.dram_tensor("yg_dram", [NB, NCHUNK, 128, T], BF16).ap()
    io["x1_out"] = nc.dram_tensor("x1_out", [NB, T, D], F32,
                                  kind="ExternalOutput").ap()
    es = ExitStack()
    with es:
        pb = PB(nc, es)
        emit_p1(pb, es, nc, io)
        print("P1 instructions:", pb.n_instr())
        pb.finalize()
    return nc




KCAP = 1536
CGLOB = 8192.0


def emit_ln_tile(pb, mv, stat, rstd, src):
    pb.op("vector", lambda E: E.bn_stats(stat[:, 0:6], src[:, 0:512]))
    pb.op("vector", lambda E: E.bn_stats(stat[:, 6:12], src[:, 512:1024]))
    pb.op("vector", lambda E: E.bn_aggr(mv[:, 0:2], stat[:, 0:12]), sem="vs")
    pb.wait("scalar", "vs")
    pb.op("scalar", lambda E: E.activation(
        rstd[:, 0:1], mv[:, 1:2], AF.Ln, bias=LN_EPS), sem="act")
    pb.wait("scalar", "act")
    pb.op("scalar", lambda E: E.activation(
        rstd[:, 1:2], rstd[:, 0:1], AF.Exp, scale=-0.5), sem="act")


def emit_p2pre(pb, es, nc, io):
    U8 = mybir.dt.uint8
    ovl = sbt(es, nc, "p2_ovl", [128, 161 * 1024], U8)

    def view(off_kb, ncols, dtype):
        size = ncols * (4 if dtype == F32 else 2)
        return ovl.ap()[:, off_kb * 1024:off_kb * 1024 + size].bitcast(dtype)

    hT = view(0, 16384, BF16)
    qT = view(32, 16384, BF16)
    aT2 = view(32, 16384, BF16)
    kT = view(64, 16384, BF16)
    vg = view(96, 16640, BF16)
    pTa = view(129, 8192, BF16)
    pTb = view(145, 8192, BF16)
    ttmp = view(131, 1024, F32)
    x1tl = view(145, 1024, F32)
    x1btok = view(149, 1024, F32)
    htok = view(153, 1024, BF16)
    h2tok = view(155, 1024, BF16)
    h2Tp = view(157, 1024, BF16)
    atok2 = view(159, 1024, BF16)
    rowin = view(0, 1024, BF16)[0:1, :]
    m1T = view(64, 4096, F32)
    m2T = view(80, 4096, F32)
    lcum = view(96, 4096, F32)

    cpool = (io["cpool_sb"] if "cpool_sb" in io else sbt(es, nc, "p2_cpool", [128, 8], F32))
    ident = sbt(es, nc, "p2_ident", [128, 128], BF16)
    ones1 = sbt(es, nc, "p2_ones1", [1, 128], BF16)
    g1rep = sbt(es, nc, "p2_g1rep", [128, 1024], BF16)
    b1rep = sbt(es, nc, "p2_b1rep", [128, 1024], BF16)
    g2rep = sbt(es, nc, "p2_g2rep", [128, 1024], BF16)
    b2rep = sbt(es, nc, "p2_b2rep", [128, 1024], BF16)
    bvrep = sbt(es, nc, "p2_bvrep", [128, 1024], BF16)
    wqk = sbt(es, nc, "p2_wqk", [128, 2 * 8192], BF16)
    bqk = sbt(es, nc, "p2_bqk", [128, 16], F32)
    bo_t = sbt(es, nc, "p2_bo", [128, 8], F32)
    wr_sb = sbt(es, nc, "p2_wr", [128, 64], BF16)
    stat = sbt(es, nc, "p2_stat", [128, 12], F32)
    mv = sbt(es, nc, "p2_mv", [128, 4], F32)
    rstd = sbt(es, nc, "p2_rstd", [128, 4], F32)
    obf = sbt(es, nc, "p2_obf", [128, 512], BF16)
    recip = sbt(es, nc, "p2_recip", [1, 512], BF16)
    logit_sb = sbt(es, nc, "p2_logit", [128, 256], F32)
    mrow = sbt(es, nc, "p2_mrow", [128, 2], F32)
    mwork = sbt(es, nc, "p2_mwork", [128, 32], F32)
    mbf = sbt(es, nc, "p2_mbf", [128, 16], BF16)
    iota8 = sbt(es, nc, "p2_iota8", [128, 8], F32)
    cnt_sb = sbt(es, nc, "p2_cnt", [8, 2], F32)

    psA = pst(es, nc, "p2_psA", [128, 2048], F32)
    psB = pst(es, nc, "p2_psB", [128, 1024], F32)
    psT = pst(es, nc, "p2_psT", [128, 2048], BF16)

    CVALS = [1e-5, 1.0, -0.5, 0.0, 0.5, 2.0, -1.0, 8.0]
    for i, v in enumerate(CVALS):
        nc.const_aps.register(F32, v, cpool[:, i:i + 1])
    pb.dma("sync", cpool[:], io["cpool"], "qs")
    pb.wait("scalar", "qs")
    pb.wait("vector", "qs")
    pb.wait("tensor", "qs")
    pb.dma("gpsimd", ident[:], io["ident"], "qgp")
    pb.dma("gpsimd", bqk[:, 0:8], io["bq_t"], "qgp")
    pb.dma("gpsimd", bqk[:, 8:16], io["bk_t"], "qgp")
    pb.dma("gpsimd", bo_t[:], io["bo_t"], "qgp")
    pb.dma("gpsimd", wr_sb[:], io["wr_t"], "qgp")
    pb.wait("gpsimd", "qgp")
    pb.op("gpsimd", lambda E: E.iota(
        iota8[:], [[1, 8]], base=0, channel_multiplier=0,
        allow_small_or_imprecise_dtypes=True), sem="pio")
    pb.op("vector", lambda E: E.memset(ones1[:], 1.0), sem="vone")
    for nm, dst in [("L0_ffn_ln1_g", g1rep), ("L0_ffn_ln1_b", b1rep),
                    ("L0_ffn_ln2_g", g2rep), ("L0_ffn_ln2_b", b2rep),
                    ("L0_ffn_bv", bvrep)]:
        pb.wait("gpsimd", "mbc", pb.c("mbc"))
        pb.dma("gpsimd", rowin[:], io[nm], "qrow")
        pb.wait("tensor", "qrow")
        pb.wait("tensor", "vone")
        pb.wait("tensor", "vbc", pb.c("vbc"))
        for h in range(2):
            pb.op("tensor", lambda E, h=h: E.matmul(
                psB[:, h * 512:(h + 1) * 512], ones1[0:1, 0:128],
                rowin[0:1, h * 512:(h + 1) * 512], start=True, stop=True),
                sem=("mbc" if h else None))
        pb.wait("vector", "mbc")
        pb.op("vector", lambda E, dst=dst: E.tensor_copy(dst[:], psB[:, 0:1024]),
              sem="vbc")

    for bi in range(NB):
        for tt in range(16):
            pb.wait("sync", "vh", pb.c("vh"))
            pb.wait("sync", "vx1b", pb.c("vx1b"))
            pb.dma("sync", x1tl[:], io["x1_in"][bi, tt * 128:(tt + 1) * 128, :],
                   "qx")
            pb.wait("vector", "qx")
            pb.wait("vector", "vhcp", pb.c("vhcp"))
            emit_ln_tile(pb, mv, stat, rstd, x1tl[:])
            pb.wait("vector", "act")
            pb.op("vector", lambda E: E.tensor_tensor(
                rstd[:, 2:3], mv[:, 1:2], rstd[:, 1:2], OP.mult))
            pb.op("vector", lambda E: E.tensor_tensor(
                rstd[:, 2:3], rstd[:, 2:3], rstd[:, 1:2], OP.mult), sem="vv2")
            pb.wait("scalar", "vv2")
            pb.op("scalar", lambda E: E.activation(
                rstd[:, 3:4], rstd[:, 2:3], AF.Ln, bias=LN_EPS), sem="acA")
            pb.wait("scalar", "acA")
            pb.op("scalar", lambda E: E.activation(
                rstd[:, 3:4], rstd[:, 3:4], AF.Exp, scale=-0.5), sem="acB")
            pb.wait("vector", "acB")
            pb.op("vector", lambda E: E.tensor_tensor(
                rstd[:, 3:4], rstd[:, 3:4], rstd[:, 1:2], OP.mult))
            pb.op("vector", lambda E: E.tensor_scalar(
                ttmp[:], x1tl[:], mv[:, 0:1], rstd[:, 3:4],
                OP.subtract, OP.mult))
            pb.op("vector", lambda E: E.tensor_tensor(
                ttmp[:], ttmp[:], g1rep[:], OP.mult))
            pb.op("vector", lambda E: E.tensor_tensor(
                htok[:], ttmp[:], b1rep[:], OP.add), sem="vh")
            pb.wait("tensor", "vh")
            for c in range(KCH):
                pslot = psT[:, (c % 2) * 1024:(c % 2) * 1024 + 128]
                pb.wait("tensor", "vhcp", pb.c("vhcp") - 1)
                pb.op("tensor", lambda E, p=pslot, c=c: E.transpose(
                    p, htok[:, c * 128:(c + 1) * 128], ident[:]), sem="mtA")
                pb.wait("vector", "mtA")
                pb.op("vector", lambda E, p=pslot, c=c, tt=tt: E.tensor_copy(
                    hT[:, c * 2048 + tt * 128: c * 2048 + (tt + 1) * 128], p),
                    sem="vhcp")

        pb.wait("gpsimd", "mq", pb.c("mq"))
        pb.wait("gpsimd", "mvg", pb.c("mvg"))
        pb.wait("gpsimd", "mat", pb.c("mat"))
        pb.wait("gpsimd", "mlg", pb.c("mlg"))
        for w_i, wnm in [(0, "L0_ffn_wq"), (1, "L0_ffn_wk")]:
            for kc in range(KCH):
                pb.dma("gpsimd", wqk[:, w_i * 8192 + kc * 1024:
                                     w_i * 8192 + (kc + 1) * 1024],
                       io[wnm].rearrange("(k p) n -> k p n", p=128)[kc], "qwqk")
        pb.wait("tensor", "qwqk")
        pb.wait("tensor", "vhcp")
        for w_i in range(2):
            dstT = qT if w_i == 0 else kT
            sc = 0.125 if w_i == 0 else 1.0
            for oc in range(KCH):
                for ts4 in range(4):
                    pbank = psA[:, (ts4 % 4) * 512:(ts4 % 4) * 512 + 512]
                    pb.wait("tensor", "vqk", pb.c("vqk") - 3)
                    for kc in range(KCH):
                        pb.op("tensor", lambda E, w_i=w_i, oc=oc, ts4=ts4,
                              kc=kc, p=pbank: E.matmul(
                                  p, wqk[:, w_i * 8192 + kc * 1024 + oc * 128:
                                         w_i * 8192 + kc * 1024 + (oc + 1) * 128],
                                  hT[:, kc * 2048 + ts4 * 512:
                                     kc * 2048 + (ts4 + 1) * 512],
                                  start=(kc == 0), stop=(kc == KCH - 1)),
                              sem=("mq" if kc == KCH - 1 else None))
                    pb.wait("vector", "mq")
                    pb.op("vector", lambda E, w_i=w_i, oc=oc, ts4=ts4, sc=sc,
                          p=pbank, dT=dstT: E.tensor_scalar(
                              dT[:, oc * 2048 + ts4 * 512:
                                 oc * 2048 + (ts4 + 1) * 512],
                              p, bqk[:, w_i * 8 + oc:w_i * 8 + oc + 1], sc,
                              OP.add, OP.mult), sem="vqk")
        pb.wait("gpsimd", "mq", pb.c("mq"))
        pb.wait("gpsimd", "mat", pb.c("mat"))
        for kc in range(KCH):
            pb.dma("gpsimd", wqk[:, kc * 1024:(kc + 1) * 1024],
                   io["L0_ffn_wv"].rearrange("(k p) n -> k p n", p=128)[kc],
                   "qwv")
        vg4 = vg.rearrange("p (t h d) -> p t h d", h=16, d=65)
        pb.op("vector", lambda E: E.memset(vg4[:, :, :, 64:65], 1.0),
              sem="vgone")
        pb.wait("tensor", "qwv")
        pb.wait("tensor", "vgone")
        for tt in range(16):
            for half in range(2):
                pbank = psB[:, (half % 2) * 512:(half % 2) * 512 + 512]
                pb.wait("tensor", "vvg", pb.c("vvg") - 1)
                for kc in range(KCH):
                    pb.op("tensor", lambda E, tt=tt, half=half, kc=kc,
                          p=pbank: E.matmul(
                              p, hT[:, kc * 2048 + tt * 128:
                                    kc * 2048 + (tt + 1) * 128],
                              wqk[:, kc * 1024 + half * 512:
                                  kc * 1024 + (half + 1) * 512],
                              start=(kc == 0), stop=(kc == KCH - 1)),
                          sem=("mvg" if kc == KCH - 1 else None))
                pb.wait("vector", "mvg")
                pb.op("vector", lambda E, half=half, p=pbank: E.tensor_tensor(
                    obf[:, 0:512], p, bvrep[:, half * 512:(half + 1) * 512],
                    OP.add))
                pb.op("vector", lambda E, tt=tt, half=half: E.tensor_copy(
                    vg4[:, tt, half * 8:(half + 1) * 8, 0:64],
                    obf.ap().rearrange("p (h d) -> p h d", d=64)),
                    sem="vvg")

        if bi == 0 and "dbg_hT" in io:
            pb.wait("sync", "vqk")
            pb.dma("sync", io["dbg_hT"], hT, "qdbg")
            pb.dma("sync", io["dbg_qT"], qT, "qdbg")
            pb.dma("sync", io["dbg_kT"], kT, "qdbg")
            pb.wait("sync", "vvg")
            pb.dma("sync", io["dbg_vg"], vg, "qdbg")
            pb.wait("sync", "qdbg")
            for e2 in ("vector", "scalar", "tensor", "gpsimd"):
                pb.wait(e2, "qdbg")
        pb.wait("scalar", "vvg", pb.c("vvg"))
        pb.wait("scalar", "vh", pb.c("vh"))
        pb.wait("scalar", "qx", pb.c("qx"))
        pb.wait("scalar", "qx2", pb.c("qx2"))
        pb.wait("scalar", "vh2", pb.c("vh2"))
        pb.wait("scalar", "vx1b", pb.c("vx1b"))
        pb.wait("scalar", "vat2", pb.c("vat2"))
        pb.wait("scalar", "vlg", pb.c("vlg"))
        pb.wait("scalar", "vtp", pb.c("vtp"))
        pb.wait("scalar", "qx1b", pb.c("qx1b"))
        pb.wait("scalar", "qh2", pb.c("qh2"))
        pb.wait("scalar", "qlg", pb.c("qlg"))
        pb.wait("scalar", "mtp", pb.c("mtp"))
        pb.wait("scalar", "mtt", pb.c("mtt"))
        pb.wait("scalar", "mlg", pb.c("mlg"))
        for hp in range(KCH):
            for qq in range(4):
                for kc in range(16):
                    bA = psA[:, (kc % 2) * 512:(kc % 2) * 512 + 512]
                    bB = psA[:, 1024 + (kc % 2) * 512:
                             1024 + (kc % 2) * 512 + 512]
                    pb.wait("tensor", "aexp", pb.c("aexp") - 2)
                    pb.op("tensor", lambda E, hp=hp, qq=qq, kc=kc, b=bA:
                          E.matmul(b, kT[0:64, hp * 2048 + kc * 128:
                                         hp * 2048 + (kc + 1) * 128],
                                   qT[0:64, hp * 2048 + qq * 512:
                                      hp * 2048 + (qq + 1) * 512],
                                   start=True, stop=True), sem="msc")
                    pb.op("tensor", lambda E, hp=hp, qq=qq, kc=kc, b=bB:
                          E.matmul(b, kT[64:128, hp * 2048 + kc * 128:
                                         hp * 2048 + (kc + 1) * 128],
                                   qT[64:128, hp * 2048 + qq * 512:
                                      hp * 2048 + (qq + 1) * 512],
                                   start=True, stop=True), sem="msc")
                    pb.wait("scalar", "msc")
                    pb.op("scalar", lambda E, kc=kc, b=bA: E.activation(
                        pTa[:, kc * 512:(kc + 1) * 512], b, AF.Exp),
                        sem="aexp")
                    pb.op("scalar", lambda E, kc=kc, b=bB: E.activation(
                        pTb[:, kc * 512:(kc + 1) * 512], b, AF.Exp),
                        sem="aexp")
                for hi in range(2):
                    pT_ = pTa if hi == 0 else pTb
                    h = hp * 2 + hi
                    pob = psB[:, 0:512]
                    pb.wait("tensor", "aexp")
                    pb.wait("tensor", "vsc", pb.c("vsc"))
                    pb.wait("tensor", "vob", pb.c("vob"))
                    pb.wait("tensor", "arc", pb.c("arc"))
                    for kc in range(16):
                        pb.op("tensor", lambda E, h=h, kc=kc, pT_=pT_,
                              pob=pob: E.matmul(
                                  pob[0:65, :], vg4[:, kc, h, :],
                                  pT_[:, kc * 512:(kc + 1) * 512],
                                  start=(kc == 0), stop=(kc == 15)),
                              sem=("mpv" if kc == 15 else None))
                    pb.wait("vector", "mpv")
                    pb.wait("scalar", "mpv")
                    pb.op("scalar", lambda E, pob=pob: E.activation(
                        psB[64:65, 512:1024], pob[64:65, :], AF.Ln))
                    pb.op("scalar", lambda E: E.activation(
                        recip[0:1, :], psB[64:65, 512:1024], AF.Exp,
                        scale=-1.0), sem="arc")
                    pb.op("vector", lambda E, pob=pob: E.tensor_copy(
                        obf[0:64, :], pob[0:64, :]), sem="vob")
                    pb.wait("tensor", "arc")
                    pb.op("tensor", lambda E: E.matmul(
                        psB[0:64, 512:1024], ones1[0:1, 0:64], recip[0:1, :],
                        start=True, stop=True), sem="mrc")
                    pb.wait("vector", "mrc")
                    hof = hi * 64
                    pb.op("vector", lambda E, hof=hof, qq=qq, hp=hp:
                          E.scalar_tensor_tensor(
                              hT[hof:hof + 64,
                                 hp * 2048 + qq * 512:hp * 2048 + (qq + 1) * 512],
                              obf[0:64, :], 1.0, psB[0:64, 512:1024],
                              OP.mult, OP.mult), sem="vsc")

        if bi == 0 and "dbg_hT" in io:
            pb.wait("sync", "vsc")
            pb.dma("sync", io["dbg_oT"], hT, "qdbg")
            pb.wait("sync", "qdbg")
            for e2 in ("vector", "scalar", "tensor", "gpsimd"):
                pb.wait(e2, "qdbg")
        pb.wait("gpsimd", "msc", pb.c("msc"))
        pb.wait("gpsimd", "mpv", pb.c("mpv"))
        pb.wait("sync", "mpv", pb.c("mpv"))
        pb.wait("sync", "vsc", pb.c("vsc"))
        pb.wait("vector", "mpv", pb.c("mpv"))
        pb.wait("vector", "msc", pb.c("msc"))
        pb.wait("scalar", "mpv", pb.c("mpv"))
        for kc in range(KCH):
            pb.dma("gpsimd", wqk[:, 8192 + kc * 1024:8192 + (kc + 1) * 1024],
                   io["L0_ffn_wo"].rearrange("(k p) n -> k p n", p=128)[kc],
                   "qwo")
        pb.wait("tensor", "qwo")
        pb.wait("tensor", "vsc")
        for oc in range(8):
            for ts4 in range(4):
                pbank = psB[:, (ts4 % 2) * 512:(ts4 % 2) * 512 + 512]
                pb.wait("tensor", "vao", pb.c("vao") - 1)
                for kc in range(KCH):
                    pb.op("tensor", lambda E, oc=oc, ts4=ts4, kc=kc, p=pbank:
                          E.matmul(
                              p, wqk[:, 8192 + kc * 1024 + oc * 128:
                                     8192 + kc * 1024 + (oc + 1) * 128],
                              hT[:, kc * 2048 + ts4 * 512:
                                 kc * 2048 + (ts4 + 1) * 512],
                              start=(kc == 0), stop=(kc == KCH - 1)),
                          sem=("mat" if kc == KCH - 1 else None))
                pb.wait("vector", "mat")
                pb.op("vector", lambda E, oc=oc, ts4=ts4, p=pbank:
                      E.tensor_scalar(
                          aT2[:, oc * 2048 + ts4 * 512:
                              oc * 2048 + (ts4 + 1) * 512],
                          p, bo_t[:, oc:oc + 1], None, OP.add), sem="vao")
        if bi == 0 and "dbg_hT" in io:
            pb.wait("sync", "vao")
            pb.dma("sync", io["dbg_aT2"], aT2, "qdbg")
            pb.wait("sync", "qdbg")
            for e2 in ("vector", "scalar", "tensor", "gpsimd"):
                pb.wait(e2, "qdbg")
        pb.wait("tensor", "vao")
        for tt in range(16):
            pb.wait("vector", "vat2", pb.c("vat2"))
            pb.wait("vector", "mtp", pb.c("mtp"))
            pb.wait("tensor", "vtp", pb.c("vtp"))
            for c in range(8):
                pslot = psT[:, (c % 2) * 1024:(c % 2) * 1024 + 128]
                pb.wait("tensor", "vatk", pb.c("vatk") - 1)
                pb.op("tensor", lambda E, c=c, tt=tt, p=pslot: E.transpose(
                    p, aT2[:, c * 2048 + tt * 128:c * 2048 + (tt + 1) * 128],
                    ident[:]), sem="mtt")
                pb.wait("vector", "mtt")
                pb.op("vector", lambda E, c=c, p=pslot: E.tensor_copy(
                    atok2[:, c * 128:(c + 1) * 128], p), sem="vatk")
            pb.wait("sync", "vh2", pb.c("vh2"))
            pb.wait("sync", "vh", pb.c("vh"))
            pb.wait("sync", "vx1b", pb.c("vx1b"))
            pb.dma("sync", x1tl[:], io["x1_in"][bi, tt * 128:(tt + 1) * 128, :],
                   "qx2")
            pb.wait("vector", "qx2")
            pb.wait("vector", "vatk")
            emit_ln_tile(pb, mv, stat, rstd, x1tl[:])
            pb.wait("vector", "act")
            pb.op("vector", lambda E: E.tensor_scalar(
                ttmp[:], x1tl[:], mv[:, 0:1], rstd[:, 1:2],
                OP.subtract, OP.mult))
            pb.wait("vector", "qx1b", pb.c("qx1b"))
            pb.op("vector", lambda E: E.tensor_tensor(
                x1btok[:], ttmp[:], atok2[:], OP.add), sem="vx1b")
            pb.wait("sync", "vx1b")
            pb.dma("sync", io["x1b_out"][bi, tt * 128:(tt + 1) * 128, :],
                   x1btok[:], "qx1b")
            emit_ln_tile(pb, mv, stat, rstd, x1btok[:])
            pb.wait("vector", "act")
            pb.op("vector", lambda E: E.tensor_scalar(
                ttmp[:], x1btok[:], mv[:, 0:1], rstd[:, 1:2],
                OP.subtract, OP.mult))
            pb.op("vector", lambda E: E.tensor_tensor(
                ttmp[:], ttmp[:], g2rep[:], OP.mult))
            pb.wait("vector", "qh2", pb.c("qh2"))
            pb.op("vector", lambda E: E.tensor_tensor(
                h2tok[:], ttmp[:], b2rep[:], OP.add), sem="vh2")
            pb.wait("sync", "vh2")
            pb.dma("sync", io["h2_out"][bi, tt * 128:(tt + 1) * 128, :],
                   h2tok[:], "qh2")
            pb.wait("tensor", "vatk", pb.c("vatk"))
            pb.wait("tensor", "vh2", pb.c("vh2"))
            for c in range(KCH):
                pslot = psT[:, (c % 2) * 1024:(c % 2) * 1024 + 128]
                pb.wait("tensor", "vtp", pb.c("vtp") - 1)
                pb.op("tensor", lambda E, c=c, p=pslot: E.transpose(
                    p, h2tok[:, c * 128:(c + 1) * 128], ident[:]), sem="mtp")
                pb.wait("vector", "mtp")
                pb.op("vector", lambda E, c=c, p=pslot: E.tensor_copy(
                    h2Tp[:, c * 128:(c + 1) * 128], p), sem="vtp")
            pb.wait("tensor", "vtp")
            pb.wait("tensor", "vlg", pb.c("vlg"))
            for c in range(KCH):
                pb.op("tensor", lambda E, c=c: E.matmul(
                    psA[:, 0:8], h2Tp[:, c * 128:(c + 1) * 128],
                    wr_sb[:, c * 8:(c + 1) * 8],
                    start=(c == 0), stop=(c == KCH - 1)),
                    sem=("mlg" if c == KCH - 1 else None))
            pb.wait("vector", "mlg")
            pb.op("vector", lambda E, bi=bi, tt=tt: E.tensor_copy(
                logit_sb[:, (bi * 16 + tt) * 8:(bi * 16 + tt + 1) * 8],
                psA[:, 0:8]), sem="vlg")
            pb.wait("sync", "vlg")
            pb.dma("sync",
                   io["logits_out"][bi, tt * 128:(tt + 1) * 128, :],
                   logit_sb[:, (bi * 16 + tt) * 8:(bi * 16 + tt + 1) * 8],
                   "qlg")

    for lbl in ["msc", "mpv", "mat", "mlg", "vlg", "qlg", "vsc", "vtp"]:
        pb.wait("vector", lbl, pb.c(lbl))
        pb.wait("tensor", lbl, pb.c(lbl))
        pb.wait("scalar", lbl, pb.c(lbl))
    emit_masks_and_cums(pb, es, nc, io, logit_sb, mrow, mwork, mbf,
                        ident, psT, m1T, m2T, lcum, cnt_sb, want_counts=True)
    pb.wait("sync", "qx1b")
    pb.wait("sync", "qh2")
    pb.wait("sync", "qlg")
    pb.wait("sync", "qcnt")


def emit_masks_and_cums(pb, es, nc, io, logit_sb, mrow, mwork, mbf,
                        ident, psT, m1T, m2T, lcum, cnt_sb, want_counts):
    for g in range(32):
        L = logit_sb[:, g * 8:(g + 1) * 8]
        m1 = mwork[:, 0:8]
        m2 = mwork[:, 8:16]
        l2 = mwork[:, 16:24]
        cs = mwork[:, 24:32]
        pb.op("vector", lambda E, L=L: E.tensor_reduce(
            mrow[:, 0:1], L, AX.X, OP.max))
        pb.op("vector", lambda E, L=L, m1=m1: E.tensor_scalar(
            m1, L, mrow[:, 0:1], None, OP.is_ge))
        pb.op("vector", lambda E, m1=m1, cs=cs: E.tensor_tensor_scan(
            cs, m1, m1, 0.0, OP.add, OP.bypass))
        pb.op("vector", lambda E, cs=cs: E.tensor_scalar(
            cs, cs, 1.0, None, OP.is_le))
        pb.op("vector", lambda E, m1=m1, cs=cs: E.tensor_tensor(
            m1, m1, cs, OP.mult))
        pb.op("vector", lambda E, m1=m1, l2=l2, L=L: E.scalar_tensor_tensor(
            l2, m1, -1e30, L, OP.mult, OP.add))
        pb.op("vector", lambda E, l2=l2: E.tensor_reduce(
            mrow[:, 1:2], l2, AX.X, OP.max))
        pb.op("vector", lambda E, l2=l2, m2=m2: E.tensor_scalar(
            m2, l2, mrow[:, 1:2], None, OP.is_ge))
        pb.op("vector", lambda E, m2=m2, cs=cs: E.tensor_tensor_scan(
            cs, m2, m2, 0.0, OP.add, OP.bypass))
        pb.op("vector", lambda E, cs=cs: E.tensor_scalar(
            cs, cs, 1.0, None, OP.is_le))
        pb.op("vector", lambda E, m2=m2, cs=cs: E.tensor_tensor(
            m2, m2, cs, OP.mult))
        pb.op("vector", lambda E: E.tensor_copy(mbf[:], mwork[:, 0:16]),
              sem="vmk")
        for mi in range(2):
            mT = m1T if mi == 0 else m2T
            pslot = psT[:, (mi % 2) * 1024:(mi % 2) * 1024 + 128]
            pb.wait("tensor", "vmk")
            pb.wait("tensor", "vmT", pb.c("vmT") - 1)
            pb.op("tensor", lambda E, mi=mi, p=pslot: E.transpose(
                p[0:8, 0:128], mbf[:, mi * 8:mi * 8 + 8], ident[:]),
                sem="mmT")
            pb.wait("vector", "mmT")
            pb.op("vector", lambda E, g=g, mT=mT, p=pslot: E.tensor_copy(
                mT[0:8, g * 128:(g + 1) * 128], p[0:8, 0:128]), sem="vmT")
    pb.op("vector", lambda E: E.tensor_tensor_scan(
        lcum[0:8, :], m1T[0:8, :], m1T[0:8, :], 0.0, OP.add, OP.bypass),
        sem="vcum")
    if want_counts:
        pb.op("vector", lambda E: E.tensor_copy(
            cnt_sb[0:8, 0:1], lcum[0:8, 4095:4096]), sem="vcnt")
        pb.op("vector", lambda E: E.tensor_tensor_scan(
            lcum[0:8, :], m2T[0:8, :], m2T[0:8, :], 0.0, OP.add, OP.bypass),
            sem="vcum")
        pb.op("vector", lambda E: E.tensor_copy(
            cnt_sb[0:8, 1:2], lcum[0:8, 4095:4096]), sem="vcnt")
        pb.wait("sync", "vcnt")
        pb.dma("sync", io["cnt_out"], cnt_sb[0:8, 0:2], "qcnt")


P2_WNAMES = [
    ("L0_ffn_wq", [D, D]), ("L0_ffn_wk", [D, D]), ("L0_ffn_wv", [D, D]),
    ("L0_ffn_wo", [D, D]), ("L0_ffn_ln1_g", [1, D]), ("L0_ffn_ln1_b", [1, D]),
    ("L0_ffn_ln2_g", [1, D]), ("L0_ffn_ln2_b", [1, D]), ("L0_ffn_bv", [1, D]),
]


def prep_p2_weights(d):
    out = {}
    for nm in ["L0_ffn_ln1_g", "L0_ffn_ln1_b", "L0_ffn_ln2_g", "L0_ffn_ln2_b"]:
        out[nm] = d[nm].reshape(1, D)
    out["L0_ffn_bv"] = d["L0_ffn_bv"].reshape(1, D)
    out["bq_t"] = np.ascontiguousarray(d["L0_ffn_bq"].reshape(8, 128).T)
    out["bk_t"] = np.ascontiguousarray(d["L0_ffn_bk"].reshape(8, 128).T)
    out["bo_t"] = np.ascontiguousarray(d["L0_ffn_bo"].reshape(8, 128).T)
    out["wr_t"] = np.ascontiguousarray(
        d["L0_ffn_wr"].reshape(8, 128, 8).transpose(1, 0, 2).reshape(128, 64))
    return out


def build_p2pre():
    nc = bass.Bass("TRN2", target_bir_lowering=False)
    io = {}
    io["x1_in"] = nc.dram_tensor("x1_in", [NB, T, D], F32,
                                 kind="ExternalInput").ap()
    for nm in ["L0_ffn_wq", "L0_ffn_wk", "L0_ffn_wv", "L0_ffn_wo"]:
        io[nm] = nc.dram_tensor(nm, [D, D], F32, kind="ExternalInput").ap()
    for nm in ["L0_ffn_ln1_g", "L0_ffn_ln1_b", "L0_ffn_ln2_g", "L0_ffn_ln2_b",
               "L0_ffn_bv"]:
        io[nm] = nc.dram_tensor(nm, [1, D], F32, kind="ExternalInput").ap()
    for nm, shp in [("bq_t", [128, 8]), ("bk_t", [128, 8]), ("bo_t", [128, 8]),
                    ("wr_t", [128, 64]), ("ident", [128, 128]),
                    ("cpool", [128, 8])]:
        io[nm] = nc.dram_tensor(nm, shp, F32, kind="ExternalInput").ap()
    io["x1b_out"] = nc.dram_tensor("x1b_out", [NB, T, D], F32,
                                   kind="ExternalOutput").ap()
    io["h2_out"] = nc.dram_tensor("h2_out", [NB, T, D], BF16,
                                  kind="ExternalOutput").ap()
    io["logits_out"] = nc.dram_tensor("logits_out", [NB, T, 8], F32,
                                      kind="ExternalOutput").ap()
    io["cnt_out"] = nc.dram_tensor("cnt_out", [8, 2], F32,
                                   kind="ExternalOutput").ap()
    if os.environ.get("P2DBG", "0") == "1":
        for nm in ["dbg_hT", "dbg_qT", "dbg_kT", "dbg_oT"]:
            io[nm] = nc.dram_tensor(nm, [128, 16384], BF16,
                                    kind="ExternalOutput").ap()
        io["dbg_vg"] = nc.dram_tensor("dbg_vg", [128, 16640], BF16,
                                      kind="ExternalOutput").ap()
        io["dbg_aT2"] = nc.dram_tensor("dbg_aT2", [128, 16384], BF16,
                                       kind="ExternalOutput").ap()
    es = ExitStack()
    with es:
        pb = PB(nc, es)
        emit_p2pre(pb, es, nc, io)
        print("P2pre instructions:", pb.n_instr())
        pb.finalize()
    return nc




def emit_p2post(pb, es, nc, io):
    U8 = mybir.dt.uint8
    ovl = sbt(es, nc, "p3_ovl", [128, 176 * 1024], U8)

    def view(off_kb, ncols, dtype):
        size = ncols * (4 if dtype == F32 else 2)
        return ovl.ap()[:, off_kb * 1024:off_kb * 1024 + size].bitcast(dtype)

    m1T = view(0, 4096, F32)
    m2T = view(16, 4096, F32)
    lcum1 = view(32, 4096, F32)
    lcum2 = view(48, 4096, F32)
    ew1 = view(64, 4096, F32)
    ew2 = view(80, 4096, F32)
    wstgA = view(96, 4096, BF16)
    wstgB = view(104, 4096, BF16)
    w1e = view(0, 32768, BF16)
    w2e = view(64, 32768, BF16)
    h1sb = view(128, 16384, BF16)
    xg = view(160, 4096, BF16)
    xeT = view(168, 4096, BF16)
    yeT = view(168, 4096, BF16)
    yetok = view(160, 4096, BF16)
    ygA = view(0, 1024, BF16)
    ygB = view(2, 1024, BF16)
    cmbf = view(4, 1024, F32)
    x1btl = view(8, 1024, F32)
    x2tl = view(12, 1024, F32)

    cpool = (io["cpool_sb"] if "cpool_sb" in io else sbt(es, nc, "p3_cpool", [128, 8], F32))
    ident = sbt(es, nc, "p3_ident", [128, 128], BF16)
    identf = sbt(es, nc, "p3_identf", [128, 128], F32)
    logit_sb = sbt(es, nc, "p3_logit", [128, 256], F32)
    mrow = sbt(es, nc, "p3_mrow", [128, 2], F32)
    mwork = sbt(es, nc, "p3_mwork", [128, 32], F32)
    mbf = sbt(es, nc, "p3_mbf", [128, 16], BF16)
    cnt_sb = sbt(es, nc, "p3_cnt", [8, 2], F32)
    cntf = sbt(es, nc, "p3_cntf", [128, 1], F32)
    mm1 = sbt(es, nc, "p3_mm1", [128, 8], BF16)
    mm2 = sbt(es, nc, "p3_mm2", [128, 8], BF16)
    off_sb = sbt(es, nc, "p3_off", [8, 4], F32)
    ecol = sbt(es, nc, "p3_ecol", [8, 2], F32)
    pp = sbt(es, nc, "p3_pp", [128, 64], F32)
    vv = sbt(es, nc, "p3_vv", [128, 64], F32)
    ss = sbt(es, nc, "p3_ss", [128, 64], F32)
    su32 = sbt(es, nc, "p3_su32", [128, 64], U32)
    pu32 = sbt(es, nc, "p3_pu32", [128, 64], U32)
    gg = sbt(es, nc, "p3_gg", [128, 64], F32)
    iotat = sbt(es, nc, "p3_iotat", [128, 64], U32)
    tokmap_sb = sbt(es, nc, "p3_tokmap", [128, 192], U32)
    bigv = sbt(es, nc, "p3_bigv", [128, 192], U32)
    b1sb = sbt(es, nc, "p3_b1sb", [128, 32], F32)
    b2sb = sbt(es, nc, "p3_b2sb", [128, 8], F32)
    stat = sbt(es, nc, "p3_stat", [128, 12], F32)
    mv = sbt(es, nc, "p3_mv", [128, 4], F32)
    rstd = sbt(es, nc, "p3_rstd", [128, 4], F32)

    psA = pst(es, nc, "p3_psA", [128, 2048], F32)
    psB = pst(es, nc, "p3_psB", [128, 1024], F32)
    psT = pst(es, nc, "p3_psT", [128, 2048], BF16)

    CVALS = [1e-5, 1.0, -0.5, 0.0, 0.5, 2.0, -1.0, 8.0]
    for i, v in enumerate(CVALS):
        nc.const_aps.register(F32, v, cpool[:, i:i + 1])
    pb.dma("sync", cpool[:], io["cpool"], "qs")
    pb.wait("scalar", "qs")
    pb.wait("vector", "qs")
    pb.wait("tensor", "qs")
    REGS = {}

    def _mkregs(E):
        for nm, v in [("tokmap", 128 * 96 - 1), ("xg", NB * T - 1),
                      ("ye", 8 * KCAP - 1)]:
            r = E.alloc_register()
            E.reg_mov(r, v)
            REGS[nm] = r

    pb.recs["gpsimd"].append(_mkregs)
    pb.dma("gpsimd", ident[:], io["ident"], "qgp")
    pb.dma("sync", identf[:], io["ident"], "qs")
    pb.dma("sync", cntf[:], io["cnt_all"], "qs")
    pb.wait("gpsimd", "qgp")
    pb.op("gpsimd", lambda E: E.iota(
        iotat.ap().rearrange("p (g o) -> p g o", o=2), [[128, 32], [0, 2]],
        base=0, channel_multiplier=1), sem="pio")
    pb.op("gpsimd", lambda E: E.iota(
        bigv[:], [[0, 192]], base=1 << 30, channel_multiplier=0), sem="pio")
    pb.op("gpsimd", lambda E: E.iota(
        ecol[0:8, 0:1], [[1, 1]], base=0, channel_multiplier=KCAP,
        allow_small_or_imprecise_dtypes=True), sem="pio")
    pb.wait("sync", "pio")
    pb.dma("sync", io["tokmap"].rearrange("(p g) o -> p (g o)", g=96),
           bigv[:], "qtm0")

    for bi in range(NB):
        for tt in range(16):
            pb.dma("sync",
                   logit_sb[:, (bi * 16 + tt) * 8:(bi * 16 + tt + 1) * 8],
                   io["logits_in"][bi, tt * 128:(tt + 1) * 128, :], "qlg")
    pb.wait("vector", "qlg")
    pb.wait("tensor", "qlg")
    pb.wait("scalar", "qlg")

    if True:
        emit_masks_and_cums(pb, es, nc, io, logit_sb, mrow, mwork, mbf,
                        ident, psT, m1T, m2T, lcum1, cnt_sb,
                        want_counts=False)
    pb.op("vector", lambda E: E.tensor_tensor_scan(
        lcum2[0:8, :], m2T[0:8, :], m2T[0:8, :], 0.0, OP.add, OP.bypass),
        sem="vcum")
    pb.wait("scalar", "vmk", pb.c("vmk"))
    pb.wait("scalar", "vcum", pb.c("vcum"))
    pb.wait("scalar", "vmT", pb.c("vmT"))
    for g in range(32):
        L = logit_sb[:, g * 8:(g + 1) * 8]
        pb.wait("scalar", "vpp", pb.c("vpp"))
        pb.op("scalar", lambda E, L=L: E.activation(
            mwork[:, 0:8], L, AF.Exp, accum_out=mrow[:, 0:1]), sem="aex")
        pb.wait("vector", "aex")
        pb.op("vector", lambda E: E.reciprocal(mrow[:, 1:2], mrow[:, 0:1]))
        pb.op("vector", lambda E, L=L: E.tensor_reduce(
            mwork[:, 8:9], L, AX.X, OP.max))
        pb.op("vector", lambda E, L=L: E.tensor_scalar(
            mwork[:, 16:24], L, mwork[:, 8:9], None, OP.is_ge))
        pb.op("vector", lambda E: E.tensor_tensor_scan(
            mwork[:, 24:32], mwork[:, 16:24], mwork[:, 16:24], 0.0,
            OP.add, OP.bypass))
        pb.op("vector", lambda E: E.tensor_scalar(
            mwork[:, 24:32], mwork[:, 24:32], 1.0, None, OP.is_le))
        pb.op("vector", lambda E: E.tensor_tensor(
            mwork[:, 16:24], mwork[:, 16:24], mwork[:, 24:32], OP.mult))
        pb.op("vector", lambda E: E.tensor_tensor(
            mwork[:, 8:16], mwork[:, 0:8], mwork[:, 16:24], OP.mult))
        pb.op("vector", lambda E, g=g: E.tensor_scalar(
            mwork[:, 8:16], mwork[:, 8:16], 0.0, 0.0, OP.add, OP.add,
            accum_out=pp[:, 2 * g:2 * g + 1]))
        pb.op("vector", lambda E, L=L: E.scalar_tensor_tensor(
            mwork[:, 8:16], mwork[:, 16:24], -1e30, L, OP.mult, OP.add))
        pb.op("vector", lambda E: E.tensor_reduce(
            mwork[:, 24:25], mwork[:, 8:16], AX.X, OP.max))
        pb.op("vector", lambda E: E.tensor_scalar(
            mwork[:, 8:16], mwork[:, 8:16], mwork[:, 24:25], None, OP.is_ge))
        pb.op("vector", lambda E: E.tensor_tensor_scan(
            mwork[:, 24:32], mwork[:, 8:16], mwork[:, 8:16], 0.0,
            OP.add, OP.bypass))
        pb.op("vector", lambda E: E.tensor_scalar(
            mwork[:, 24:32], mwork[:, 24:32], 1.0, None, OP.is_le))
        pb.op("vector", lambda E: E.tensor_tensor(
            mwork[:, 8:16], mwork[:, 8:16], mwork[:, 24:32], OP.mult))
        pb.op("vector", lambda E: E.tensor_tensor(
            mwork[:, 24:32], mwork[:, 0:8], mwork[:, 8:16], OP.mult))
        pb.op("vector", lambda E, g=g: E.tensor_scalar(
            mwork[:, 24:32], mwork[:, 24:32], 0.0, 0.0, OP.add, OP.add,
            accum_out=pp[:, 2 * g + 1:2 * g + 2]))
        pb.op("vector", lambda E, g=g: E.tensor_scalar(
            pp[:, 2 * g:2 * g + 2], pp[:, 2 * g:2 * g + 2], mrow[:, 1:2],
            None, OP.mult), sem="vpp")

    if True:
        pb.wait("tensor", "qgp2")
        pb.wait("tensor", "qs")
        pb.wait("gpsimd", "mmT", pb.c("mmT"))
        pb.dma("gpsimd", mbf[:, 0:8], io["Mmask"], "qgp2")
        pb.dma("gpsimd", mbf[:, 8:16], io["Mmask2"], "qgp2")
        pb.wait("tensor", "qgp2")
        cntbf = ecol
        pb.wait("vector", "qs")
        pb.op("vector", lambda E: E.tensor_copy(mm1[:, 0:1], cntf[:]), sem="vcf")
        pb.wait("tensor", "vcf")
        pb.op("tensor", lambda E: E.matmul(
            psB[0:8, 0:1], mbf[:, 0:8], mm1[:, 0:1], start=True, stop=True),
            sem=None)
        pb.op("tensor", lambda E: E.matmul(
            psB[0:8, 8:9], mbf[:, 8:16], mm1[:, 0:1], start=True, stop=True),
            sem="moff")
        pb.wait("vector", "moff")
        pb.op("vector", lambda E: E.tensor_copy(off_sb[0:8, 0:2],
                                                psB[0:8, 0:9:8]), sem="voff")

        pb.wait("vector", "vcum")
        pb.op("vector", lambda E: E.tensor_scalar(
            ew1[0:8, :], lcum1[0:8, :], off_sb[0:8, 0:1], -1.0, OP.add, OP.add))
        pb.op("vector", lambda E: E.tensor_scalar(
            ew1[0:8, :], ew1[0:8, :], CGLOB, None, OP.is_lt))
        pb.op("vector", lambda E: E.tensor_tensor(
            ew1[0:8, :], ew1[0:8, :], m1T[0:8, :], OP.mult), sem="vk1")
        pb.op("vector", lambda E: E.tensor_scalar(
            ew2[0:8, :], lcum2[0:8, :], off_sb[0:8, 1:2], -1.0, OP.add, OP.add))
        pb.op("vector", lambda E: E.tensor_scalar(
            ew2[0:8, :], ew2[0:8, :], CGLOB, None, OP.is_lt))
        pb.op("vector", lambda E: E.tensor_tensor(
            ew2[0:8, :], ew2[0:8, :], m2T[0:8, :], OP.mult), sem="vk2")
        pb.wait("tensor", "vk1", pb.c("vk1"))
        pb.wait("tensor", "vk2", pb.c("vk2"))
        for g in range(32):
            for mi in range(2):
                src = (ew1 if mi == 0 else ew2)[0:8, g * 128:(g + 1) * 128]
                pslot = psT[:, (mi % 2) * 1024:(mi % 2) * 1024 + 256].bitcast(F32)
                pb.wait("tensor", "vkv", pb.c("vkv") - 1)
                pb.op("tensor", lambda E, src=src, p=pslot: E.transpose(
                    p[0:128, 0:8], src, identf[0:8, 0:8]), sem="mkv")
                pb.wait("vector", "mkv")
                pb.op("vector", lambda E, g=g, mi=mi, p=pslot: E.tensor_reduce(
                    vv[:, 2 * g + mi:2 * g + mi + 1], p[0:128, 0:8], AX.X,
                    OP.add), sem="vkv")
        pb.wait("vector", "mkv", pb.c("mkv"))
        pb.op("vector", lambda E: E.tensor_scalar(
            ew1[0:8, :], lcum1[0:8, :], ecol[0:8, 0:1], -1.0, OP.add, OP.add))
        pb.op("vector", lambda E: E.tensor_tensor(
            ew1[0:8, :], ew1[0:8, :], m1T[0:8, :], OP.mult), sem="vs1")
        pb.op("vector", lambda E: E.tensor_tensor(
            ecol[0:8, 1:2], ecol[0:8, 0:1], lcum1[0:8, 4095:4096], OP.add))
        pb.op("vector", lambda E: E.tensor_scalar(
            ew2[0:8, :], lcum2[0:8, :], ecol[0:8, 1:2], -1.0, OP.add, OP.add))
        pb.op("vector", lambda E: E.tensor_tensor(
            ew2[0:8, :], ew2[0:8, :], m2T[0:8, :], OP.mult), sem="vs2")
        pb.wait("tensor", "vs1", pb.c("vs1"))
        pb.wait("tensor", "vs2", pb.c("vs2"))
        for g in range(32):
            for mi in range(2):
                src = (ew1 if mi == 0 else ew2)[0:8, g * 128:(g + 1) * 128]
                pslot = psT[:, (mi % 2) * 1024:(mi % 2) * 1024 + 256].bitcast(F32)
                pb.wait("tensor", "vsv", pb.c("vsv") - 1)
                pb.op("tensor", lambda E, src=src, p=pslot: E.transpose(
                    p[0:128, 0:8], src, identf[0:8, 0:8]), sem="msv")
                pb.wait("vector", "msv")
                pb.op("vector", lambda E, g=g, mi=mi, p=pslot: E.tensor_reduce(
                    ss[:, 2 * g + mi:2 * g + mi + 1], p[0:128, 0:8], AX.X,
                    OP.add), sem="vsv")
        pb.op("vector", lambda E: E.tensor_tensor(gg[:], pp[:], vv[:], OP.mult))
        for g in range(32):
            pb.op("vector", lambda E, g=g: E.tensor_tensor(
                mrow[:, 0:1], gg[:, 2 * g:2 * g + 1], gg[:, 2 * g + 1:2 * g + 2],
                OP.add))
            pb.op("vector", lambda E: E.tensor_scalar(
                mrow[:, 0:1], mrow[:, 0:1], 1.1920929e-07, None, OP.max))
            pb.op("vector", lambda E: E.reciprocal(mrow[:, 1:2], mrow[:, 0:1]))
            pb.op("vector", lambda E, g=g: E.tensor_scalar(
                gg[:, 2 * g:2 * g + 2], gg[:, 2 * g:2 * g + 2], mrow[:, 1:2],
                None, OP.mult), sem="vgg")
        pb.op("vector", lambda E: E.tensor_scalar(
            vv[:], ss[:], float(8 * KCAP), None, OP.is_ge))
        pb.op("vector", lambda E: E.scalar_tensor_tensor(
            ss[:], vv[:], 1e9, ss[:], OP.mult, OP.add), sem="vsl")
        pb.op("vector", lambda E: E.tensor_copy(su32[:], ss[:]), sem="vsu")
        pb.op("vector", lambda E: E.tensor_scalar(
            pp[:], ss[:], 1.0 / 128.0, None, OP.mult))
        pb.op("vector", lambda E: E.tensor_copy(pu32[:], pp[:]))
        pb.op("vector", lambda E: E.tensor_copy(pp[:], pu32[:]))
        pb.op("vector", lambda E: E.scalar_tensor_tensor(
            vv[:], pp[:], -128.0, ss[:], OP.mult, OP.add))
        pb.op("vector", lambda E: E.tensor_scalar(
            vv[:], vv[:], 0.0, None, OP.is_lt))
        pb.op("vector", lambda E: E.tensor_tensor(
            pp[:], pp[:], vv[:], OP.subtract))
        pb.op("vector", lambda E: E.tensor_scalar(
            vv[:], ss[:], 96.0, None, OP.mult))
        pb.op("vector", lambda E: E.scalar_tensor_tensor(
            pp[:], pp[:], -(128.0 * 96.0 - 1.0), vv[:], OP.mult, OP.add),
            sem="vpm")
        pb.op("vector", lambda E: E.tensor_copy(pu32[:], pp[:]), sem="vpu")
    pb.wait("gpsimd", "vpu")
    pb.wait("gpsimd", "qtm0")
    for g in range(32):
        for mi in range(2):
            pb.op("gpsimd", lambda E, g=g, mi=mi: E.indirect_dma_start(
                out=io["tokmap"],
                out_offset=IndirectOffsetOnAxis(
                    ap=pu32[:, 2 * g + mi:2 * g + mi + 1], axis=0),
                in_=iotat[:, 2 * g:2 * g + 2], in_offset=None,
                bounds_check=REGS["tokmap"], oob_is_err=False), sem="qsc",
                dma16=True)
    pb.wait("sync", "qsc")
    pb.dma("sync", tokmap_sb[:],
           io["tokmap"].rearrange("(p g) o -> p (g o)", g=96), "qtm")
    pb.wait("gpsimd", "qtm")

    pb.wait("sync", "qstg")
    pb.wait("gpsimd", "qstg")
    pb.wait("vector", "qstg")
    pb.wait("tensor", "qstg")
    pb.op("gpsimd", lambda E: E.memset(xg[:], 0.0), sem="pmz")
    for e in range(8):
        pb.wait("gpsimd", "mh1", pb.c("mh1"))
        pb.wait("gpsimd", "my2", pb.c("my2"))
        if e == 0:
            pb.wait("gpsimd", "vkv", pb.c("vkv"))
            pb.wait("gpsimd", "vsv", pb.c("vsv"))
            pb.wait("gpsimd", "vs1", pb.c("vs1"))
            pb.wait("gpsimd", "vs2", pb.c("vs2"))
            pb.wait("gpsimd", "vcum", pb.c("vcum"))
        for kc in range(8):
            pb.dma("gpsimd", w1e[:, kc * 4096:(kc + 1) * 4096],
                   io["L0_ffn_w1"][e].rearrange(
                       "(k p) n -> k p n", p=128)[kc], "qw1")
        for fc in range(32):
            pb.dma("gpsimd", w2e[:, fc * 1024:(fc + 1) * 1024],
                   io["L0_ffn_w2"][e].rearrange(
                       "(f p) n -> f p n", p=128)[fc], "qw2")
        pb.dma("sync", b1sb[:], io["b1_t"][e], "qb")
        pb.dma("sync", b2sb[:], io["b2_t"][e], "qb")
        pb.wait("tensor", "qw1")
        pb.wait("tensor", "qw2")
        pb.wait("scalar", "qb")
        for grp in range(3):
            pb.wait("gpsimd", "mxT", pb.c("mxT"))
            pb.wait("gpsimd", "vyT", pb.c("vyT"))
            pb.wait("gpsimd", "qye", pb.c("qye"))
            for i4 in range(4):
                gch = e * 12 + grp * 4 + i4
                pb.op("gpsimd", lambda E, gch=gch, i4=i4: E.indirect_dma_start(
                    out=xg[:, i4 * 1024:(i4 + 1) * 1024], out_offset=None,
                    in_=io["h2_in"].rearrange("b t d -> (b t) d"),
                    in_offset=IndirectOffsetOnAxis(
                        ap=tokmap_sb[:, 2 * gch:2 * gch + 1], axis=0),
                    bounds_check=REGS["xg"], oob_is_err=False), sem="qxg",
                    dma16=True)
            pb.wait("tensor", "qxg")
            for i4 in range(4):
                for c in range(KCH):
                    pslot = psT[:, ((i4 * 8 + c) % 2) * 1024:
                                ((i4 * 8 + c) % 2) * 1024 + 128]
                    pb.wait("tensor", "vxT", pb.c("vxT") - 1)
                    pb.op("tensor", lambda E, i4=i4, c=c, p=pslot: E.transpose(
                        p, xg[:, i4 * 1024 + c * 128:i4 * 1024 + (c + 1) * 128],
                        ident[:]), sem="mxT")
                    pb.wait("vector", "mxT")
                    pb.op("vector", lambda E, i4=i4, c=c, p=pslot:
                          E.tensor_copy(
                              xeT[:, c * 512 + i4 * 128:c * 512 + (i4 + 1) * 128],
                              p), sem="vxT")
            pb.wait("tensor", "vxT")
            for fc in range(32):
                pbank = psA[:, (fc % 4) * 512:(fc % 4) * 512 + 512]
                pb.wait("tensor", "ah1", pb.c("ah1") - 3)
                for kc in range(8):
                    pb.op("tensor", lambda E, fc=fc, kc=kc, p=pbank: E.matmul(
                        p, w1e[:, kc * 4096 + fc * 128:kc * 4096 + (fc + 1) * 128],
                        xeT[:, kc * 512:(kc + 1) * 512],
                        start=(kc == 0), stop=(kc == 7)),
                        sem=("mh1" if kc == 7 else None))
                pb.wait("scalar", "mh1")
                pb.op("scalar", lambda E, fc=fc, p=pbank: E.activation(
                    h1sb[:, fc * 512:(fc + 1) * 512], p, AF.Relu,
                    bias=b1sb[:, fc:fc + 1]), sem="ah1")
            pb.wait("tensor", "ah1")
            for oc in range(8):
                pbank = psB[:, (oc % 2) * 512:(oc % 2) * 512 + 512]
                pb.wait("tensor", "vy2", pb.c("vy2") - 1)
                for fc in range(32):
                    pb.op("tensor", lambda E, oc=oc, fc=fc, p=pbank: E.matmul(
                        p, w2e[:, fc * 1024 + oc * 128:fc * 1024 + (oc + 1) * 128],
                        h1sb[:, fc * 512:(fc + 1) * 512],
                        start=(fc == 0), stop=(fc == 31)),
                        sem=("my2" if fc == 31 else None))
                pb.wait("vector", "my2")
                pb.op("vector", lambda E, oc=oc, p=pbank: E.tensor_scalar(
                    yeT[:, oc * 512:(oc + 1) * 512], p, b2sb[:, oc:oc + 1],
                    0.8, OP.add, OP.mult), sem="vy2")
            pb.wait("tensor", "vy2")
            for i4 in range(4):
                pb.wait("vector", "qye", pb.c("qye") - 2)
                for c in range(KCH):
                    pslot = psT[:, ((i4 * 8 + c) % 2) * 1024:
                                ((i4 * 8 + c) % 2) * 1024 + 128]
                    pb.wait("tensor", "vyT", pb.c("vyT") - 1)
                    pb.op("tensor", lambda E, i4=i4, c=c, p=pslot: E.transpose(
                        p, yeT[:, c * 512 + i4 * 128:c * 512 + (i4 + 1) * 128],
                        ident[:]), sem="myT")
                    pb.wait("vector", "myT")
                    pb.op("vector", lambda E, i4=i4, c=c, p=pslot:
                          E.tensor_copy(
                              yetok[:, i4 * 1024 + c * 128:
                                    i4 * 1024 + (c + 1) * 128], p), sem="vyT")
                pb.wait("sync", "vyT")
                row0 = e * KCAP + grp * 512 + i4 * 128
                pb.dma("sync", io["yedram"][row0:row0 + 128, :],
                       yetok[:, i4 * 1024:(i4 + 1) * 1024], "qye")

    pb.wait("gpsimd", "my2", pb.c("my2"))
    pb.wait("gpsimd", "mh1", pb.c("mh1"))
    pb.wait("gpsimd", "qye", pb.c("qye"))
    for g in range(32):
        bi, tt = divmod(g, 16)
        pb.wait("gpsimd", "vcb", pb.c("vcb"))
        pb.op("gpsimd", lambda E: E.memset(ygA[:], 0.0))
        pb.op("gpsimd", lambda E: E.memset(ygB[:], 0.0), sem="pz")
        pb.wait("gpsimd", "pz")
        pb.op("gpsimd", lambda E, g=g: E.indirect_dma_start(
            out=ygA[:], out_offset=None, in_=io["yedram"],
            in_offset=IndirectOffsetOnAxis(ap=su32[:, 2 * g:2 * g + 1], axis=0),
            bounds_check=REGS["ye"], oob_is_err=False), sem="qyg",
            dma16=True)
        pb.op("gpsimd", lambda E, g=g: E.indirect_dma_start(
            out=ygB[:], out_offset=None, in_=io["yedram"],
            in_offset=IndirectOffsetOnAxis(ap=su32[:, 2 * g + 1:2 * g + 2],
                                           axis=0),
            bounds_check=REGS["ye"], oob_is_err=False), sem="qyg",
            dma16=True)
        pb.wait("sync", "vcb", pb.c("vcb"))
        pb.wait("sync", "qx2", pb.c("qx2"))
        pb.dma("sync", x1btl[:], io["x1b_in"][bi, tt * 128:(tt + 1) * 128, :],
               "qxb")
        pb.dma("sync", x2tl[:], io["x1_in"][bi, tt * 128:(tt + 1) * 128, :],
               "qx1")
        pb.wait("vector", "qyg")
        pb.wait("vector", "qxb")
        pb.wait("vector", "qx1")
        pb.op("vector", lambda E, g=g: E.tensor_scalar(
            cmbf[:], ygA[:], gg[:, 2 * g:2 * g + 1], None, OP.mult))
        pb.op("vector", lambda E, g=g: E.scalar_tensor_tensor(
            cmbf[:], ygB[:], gg[:, 2 * g + 1:2 * g + 2], cmbf[:],
            OP.mult, OP.add))
        pb.op("vector", lambda E: E.tensor_tensor(
            cmbf[:], cmbf[:], x1btl[:], OP.add), sem="vfo")
        emit_ln_tile(pb, mv, stat, rstd, cmbf[:])
        pb.wait("vector", "act")
        pb.op("vector", lambda E: E.tensor_scalar(
            cmbf[:], cmbf[:], mv[:, 0:1], rstd[:, 1:2], OP.subtract, OP.mult))
        pb.op("vector", lambda E: E.tensor_tensor(
            x2tl[:], x2tl[:], cmbf[:], OP.add), sem="vcb")
        pb.wait("sync", "vcb")
        pb.dma("sync", io["x2_out"][bi, tt * 128:(tt + 1) * 128, :],
               x2tl[:], "qx2")
    pb.wait("sync", "qx2")


def prep_p2post_weights(d, core):
    out = {}
    out["b1_t"] = np.ascontiguousarray(
        d["L0_ffn_b1"].reshape(8, 32, 128).transpose(0, 2, 1))
    out["b2_t"] = np.ascontiguousarray(
        d["L0_ffn_b2"].reshape(8, 8, 128).transpose(0, 2, 1))
    M1 = np.zeros((128, 8), np.float32)
    M2 = np.zeros((128, 8), np.float32)
    for r in range(8):
        for e in range(8):
            if r < core:
                M1[r * 16 + e, e] = 1.0
                M2[r * 16 + 8 + e, e] = 1.0
            M2[r * 16 + e, e] += 1.0
    out["Mmask"] = M1
    out["Mmask2"] = M2
    return out


def build_p2post():
    nc = bass.Bass("TRN2", target_bir_lowering=False,
                   dynamic_dma_scratch_size=32768)
    io = {}
    io["x1_in"] = nc.dram_tensor("x1_in", [NB, T, D], F32,
                                 kind="ExternalInput").ap()
    io["x1b_in"] = nc.dram_tensor("x1b_in", [NB, T, D], F32,
                                  kind="ExternalInput").ap()
    io["h2_in"] = nc.dram_tensor("h2_in", [NB, T, D], BF16,
                                 kind="ExternalInput").ap()
    io["logits_in"] = nc.dram_tensor("logits_in", [NB, T, 8], F32,
                                     kind="ExternalInput").ap()
    io["cnt_all"] = nc.dram_tensor("cnt_all", [128, 1], F32,
                                   kind="ExternalInput").ap()
    io["L0_ffn_w1"] = nc.dram_tensor("L0_ffn_w1", [8, D, 4096], F32,
                                     kind="ExternalInput").ap()
    io["L0_ffn_w2"] = nc.dram_tensor("L0_ffn_w2", [8, 4096, D], F32,
                                     kind="ExternalInput").ap()
    for nm, shp in [("b1_t", [8, 128, 32]), ("b2_t", [8, 128, 8]),
                    ("Mmask", [128, 8]), ("Mmask2", [128, 8]),
                    ("ident", [128, 128]), ("cpool", [128, 8])]:
        io[nm] = nc.dram_tensor(nm, shp, F32, kind="ExternalInput").ap()
    io["tokmap"] = nc.dram_tensor("tokmap", [128 * 96, 2], U32).ap()
    io["w1bf"] = nc.dram_tensor("w1bf", [8, 8, 128, 4096], BF16).ap()
    io["w2bf"] = nc.dram_tensor("w2bf", [8, 32, 128, 1024], BF16).ap()
    io["yedram"] = nc.dram_tensor("yedram", [8 * KCAP, D], BF16).ap()
    io["x2_out"] = nc.dram_tensor("x2_out", [NB, T, D], F32,
                                  kind="ExternalOutput").ap()
    es = ExitStack()
    with es:
        pb = PB(nc, es)
        emit_p2post(pb, es, nc, io)
        print("P2post instructions:", pb.n_instr())
        pb.finalize()
    return nc




def emit_p3(pb, es, nc, io):
    U8 = mybir.dt.uint8
    ovl = sbt(es, nc, "p4_ovl", [128, 132 * 1024], U8)

    def view(off_kb, ncols, dtype):
        size = ncols * (4 if dtype == F32 else 2)
        return ovl.ap()[:, off_kb * 1024:off_kb * 1024 + size].bitcast(dtype)

    lnT = view(0, 16384, BF16)
    OT = view(0, 16384, BF16)
    Qtok = view(32, 16384, BF16)
    aT = view(32, 16384, BF16)
    Ktok = view(64, 16384, BF16)
    VT = view(96, 16384, BF16)
    wq3 = view(128, 2048, BF16)

    cpool = (io["cpool_sb"] if "cpool_sb" in io else sbt(es, nc, "p4_cpool", [128, 8], F32))
    ident = sbt(es, nc, "p4_ident", [128, 128], BF16)
    ones1 = sbt(es, nc, "p4_ones1", [1, 128], BF16)
    rowbf = sbt(es, nc, "p4_rowbf", [1, 1024], BF16)
    bqrep = sbt(es, nc, "p4_bqrep", [128, 1024], BF16)
    bkrep = sbt(es, nc, "p4_bkrep", [128, 1024], BF16)
    bv_t = sbt(es, nc, "p4_bvt", [128, 8], F32)
    bo_t = sbt(es, nc, "p4_bot", [128, 8], F32)
    wbuf = sbt(es, nc, "p4_wbuf", [128, 8192], BF16)
    x2tl = sbt(es, nc, "p4_x2tl", [128, D], F32)
    stat = sbt(es, nc, "p4_stat", [128, 12], F32)
    mv = sbt(es, nc, "p4_mv", [128, 4], F32)
    rstd = sbt(es, nc, "p4_rstd", [128, 4], F32)
    lntok = sbt(es, nc, "p4_lntok", [128, D], BF16)
    esb = sbt(es, nc, "p4_esb", [64, 64], BF16)
    psums = sbt(es, nc, "p4_psums", [64, 2], F32)
    ptb = sbt(es, nc, "p4_ptb", [128, 64], BF16)
    atok = sbt(es, nc, "p4_atok", [128, D], BF16)
    x3t = sbt(es, nc, "p4_x3t", [128, D], F32)

    psA = pst(es, nc, "p4_psA", [128, 2048], F32)
    psB = pst(es, nc, "p4_psB", [128, 1024], F32)
    psT = pst(es, nc, "p4_psT", [128, 2048], BF16)

    CVALS = [1e-5, 1.0, -0.5, 0.0, 0.5, 2.0, -1.0, 8.0]
    for i, v in enumerate(CVALS):
        nc.const_aps.register(F32, v, cpool[:, i:i + 1])
    pb.dma("sync", cpool[:], io["cpool"], "qs")
    pb.wait("scalar", "qs")
    pb.wait("vector", "qs")
    pb.wait("tensor", "qs")
    pb.dma("gpsimd", ident[:], io["ident"], "qgp")
    pb.dma("gpsimd", bv_t[:], io["bv_t"], "qgp")
    pb.dma("gpsimd", bo_t[:], io["bo_t"], "qgp")
    pb.wait("tensor", "qgp")
    pb.wait("vector", "qgp")
    pb.wait("scalar", "qgp")
    pb.op("vector", lambda E: E.memset(ones1[:], 1.0), sem="vone")
    for nm, dst in [("bq_row", bqrep), ("bk_row", bkrep)]:
        pb.wait("gpsimd", "mbc3", pb.c("mbc3"))
        pb.dma("gpsimd", rowbf[:], io[nm], "qrow")
        pb.wait("tensor", "qrow")
        pb.wait("tensor", "vone")
        pb.wait("tensor", "vbc3", pb.c("vbc3"))
        for h in range(2):
            pb.op("tensor", lambda E, h=h: E.matmul(
                psB[:, h * 512:(h + 1) * 512], ones1[0:1, 0:128],
                rowbf[0:1, h * 512:(h + 1) * 512], start=True, stop=True),
                sem=("mbc3" if h else None))
        pb.wait("vector", "mbc3")
        pb.op("vector", lambda E, dst=dst: E.tensor_copy(dst[:], psB[:, 0:1024]),
              sem="vbc3")

    for bi in range(NB):
        for tt in range(16):
            pb.wait("sync", "vln3", pb.c("vln3"))
            pb.wait("sync", "v3y", pb.c("v3y"))
            pb.dma("sync", x2tl[:], io["x2_in"][bi, tt * 128:(tt + 1) * 128, :],
                   "qx")
            pb.wait("vector", "qx")
            pb.wait("vector", "m3A", pb.c("m3A"))
            emit_ln_tile(pb, mv, stat, rstd, x2tl[:])
            pb.wait("vector", "act")
            pb.op("vector", lambda E: E.tensor_scalar(
                lntok[:], x2tl[:], mv[:, 0:1], rstd[:, 1:2],
                OP.subtract, OP.mult), sem="vln3")
            pb.wait("tensor", "vln3")
            for c in range(KCH):
                pslot = psT[:, (c % 2) * 1024:(c % 2) * 1024 + 128]
                pb.wait("tensor", "v3A", pb.c("v3A") - 1)
                pb.op("tensor", lambda E, p=pslot, c=c: E.transpose(
                    p, lntok[:, c * 128:(c + 1) * 128], ident[:]), sem="m3A")
                pb.wait("vector", "m3A")
                pb.op("vector", lambda E, p=pslot, c=c, tt=tt: E.tensor_copy(
                    lnT[:, c * 2048 + tt * 128: c * 2048 + (tt + 1) * 128], p),
                    sem="v3A")

        for w_i, dstT in ((0, Qtok), (1, Ktok)):
            pb.wait("gpsimd", "v3B", pb.c("v3B"))
            pb.wait("gpsimd", "v3V", pb.c("v3V"))
            pb.wait("gpsimd", "v3D", pb.c("v3D"))
            for kc in range(KCH):
                pb.dma("gpsimd", wbuf[:, kc * 1024:(kc + 1) * 1024],
                       io["L1_attn_W_qkv"].rearrange(
                           "(k p) n -> k p n", p=128)[kc][:, w_i * 1024:
                                                          (w_i + 1) * 1024],
                       "qw3")
            pb.wait("tensor", "qw3")
            pb.wait("tensor", "v3A")
            brep = bqrep if w_i == 0 else bkrep
            for tt in range(16):
                for half in range(2):
                    pbank = psB[:, (half % 2) * 512:(half % 2) * 512 + 512]
                    pb.wait("tensor", "v3B", pb.c("v3B") - 1)
                    for kc in range(KCH):
                        pb.op("tensor", lambda E, tt=tt, half=half, kc=kc,
                              p=pbank: E.matmul(
                                  p, lnT[:, kc * 2048 + tt * 128:
                                        kc * 2048 + (tt + 1) * 128],
                                  wbuf[:, kc * 1024 + half * 512:
                                      kc * 1024 + (half + 1) * 512],
                                  start=(kc == 0), stop=(kc == KCH - 1)),
                              sem=("m3B" if kc == KCH - 1 else None))
                    pb.wait("vector", "m3B")
                    pb.op("vector", lambda E, tt=tt, half=half, p=pbank,
                          dT=dstT, br=brep: E.tensor_tensor(
                              dT[:, tt * 1024 + half * 512:
                                 tt * 1024 + (half + 1) * 512],
                              p, br[:, half * 512:(half + 1) * 512], OP.add),
                          sem="v3B")
        pb.wait("gpsimd", "v3B", pb.c("v3B"))
        for kc in range(KCH):
            pb.dma("gpsimd", wbuf[:, kc * 1024:(kc + 1) * 1024],
                   io["L1_attn_W_qkv"].rearrange(
                       "(k p) n -> k p n", p=128)[kc][:, 2048:3072], "qwv3")
        pb.wait("tensor", "qwv3")
        pb.wait("tensor", "v3B")
        for oc in range(8):
            for ts4 in range(4):
                pbank = psB[:, (ts4 % 2) * 512:(ts4 % 2) * 512 + 512]
                pb.wait("tensor", "v3V", pb.c("v3V") - 1)
                for kc in range(KCH):
                    pb.op("tensor", lambda E, oc=oc, ts4=ts4, kc=kc, p=pbank:
                          E.matmul(
                              p, wbuf[:, kc * 1024 + oc * 128:
                                      kc * 1024 + (oc + 1) * 128],
                              lnT[:, kc * 2048 + ts4 * 512:
                                  kc * 2048 + (ts4 + 1) * 512],
                              start=(kc == 0), stop=(kc == KCH - 1)),
                          sem=("m3V" if kc == KCH - 1 else None))
                pb.wait("vector", "m3V")
                pb.op("vector", lambda E, oc=oc, ts4=ts4, p=pbank:
                      E.tensor_scalar(
                          VT[:, oc * 2048 + ts4 * 512:
                             oc * 2048 + (ts4 + 1) * 512],
                          p, bv_t[:, oc:oc + 1], None, OP.add), sem="v3V")

        pb.wait("tensor", "v3V")
        pb.wait("scalar", "v3B", pb.c("v3B"))
        pb.wait("scalar", "v3V", pb.c("v3V"))
        pb.wait("vector", "v3B", pb.c("v3B"))
        for h in range(16):
            hp = h // 2
            hrow = (h % 2) * 64
            for n in range(8):
                bA = psA[0:64, (n % 2) * 512:(n % 2) * 512 + 64]
                pb.wait("tensor", "a3e", pb.c("a3e") - 1)
                for c2 in range(2):
                    pb.op("tensor", lambda E, h=h, n=n, c2=c2, b=bA: E.matmul(
                        b, Qtok[:, (n + c2) * 1024 + h * 64:
                                (n + c2) * 1024 + (h + 1) * 64],
                        Ktok[:, (n + c2) * 1024 + h * 64:
                             (n + c2) * 1024 + (h + 1) * 64],
                        start=(c2 == 0), stop=(c2 == 1)),
                        sem=("m3w" if c2 else None))
                pb.wait("scalar", "m3w")
                pb.wait("scalar", "v3p", pb.c("v3p"))
                pb.op("scalar", lambda E, b=bA: E.activation(
                    esb[0:64, :], b, AF.Exp, scale=0.125,
                    accum_out=psums[0:64, 0:1]), sem="a3e")
                pb.wait("vector", "a3e")
                pb.op("vector", lambda E: E.reciprocal(
                    psums[0:64, 1:2], psums[0:64, 0:1]))
                pb.op("vector", lambda E: E.tensor_scalar(
                    esb[0:64, :], esb[0:64, :], psums[0:64, 1:2], None,
                    OP.mult), sem="v3p")
                pslot = psT[0:64, (n % 2) * 1024:(n % 2) * 1024 + 64]
                pb.wait("tensor", "v3p")
                pb.wait("tensor", "v3pt", pb.c("v3pt") - 1)
                pb.op("tensor", lambda E, p=pslot: E.transpose(
                    p, esb[0:64, :], ident[0:64, 0:64]), sem="m3pt")
                pb.wait("vector", "m3pt")
                pb.op("vector", lambda E, p=pslot, hrow=hrow: E.tensor_copy(
                    ptb[hrow:hrow + 64, :], p), sem="v3pt")
                pb.wait("tensor", "v3pt")
                for c2 in range(2):
                    ob = psB[:, (c2 % 2) * 512:(c2 % 2) * 512 + 64]
                    pb.wait("tensor", "v3o", pb.c("v3o") - 1)
                    pb.op("tensor", lambda E, h=h, n=n, c2=c2, hp=hp,
                          hrow=hrow, ob=ob: E.matmul(
                              ob, VT[hrow:hrow + 64,
                                     hp * 2048 + n * 128 + c2 * 128:
                                     hp * 2048 + n * 128 + (c2 + 1) * 128],
                              ptb[hrow:hrow + 64, :], start=True, stop=True),
                          sem="m3o")
                    pb.wait("vector", "m3o")
                    for wh in range(2):
                        w_hi = c2 * 2 + wh
                        pb.op("vector", lambda E, hp=hp, hrow=hrow, n=n,
                              w_hi=w_hi, wh=wh, ob=ob: E.tensor_copy(
                                  OT[hrow:hrow + 64,
                                     hp * 2048 + n * 256 + w_hi:
                                     hp * 2048 + n * 256 + w_hi + 253:4],
                                  ob[wh * 64:(wh + 1) * 64, 0:64],
                                  ), sem="v3o")

        pb.wait("gpsimd", "v3V", pb.c("v3V"))
        pb.wait("gpsimd", "v3o", pb.c("v3o"))
        pb.wait("gpsimd", "a3e", pb.c("a3e"))
        for kc in range(KCH):
            pb.dma("gpsimd", wbuf[:, kc * 1024:(kc + 1) * 1024],
                   io["L1_attn_W_out"].rearrange(
                       "(k p) n -> k p n", p=128)[kc], "qwo3")
        pb.wait("tensor", "qwo3")
        pb.wait("tensor", "v3o")
        for oc in range(8):
            for ts4 in range(4):
                pbank = psB[:, (ts4 % 2) * 512:(ts4 % 2) * 512 + 512]
                pb.wait("tensor", "v3D", pb.c("v3D") - 1)
                for kc in range(KCH):
                    pb.op("tensor", lambda E, oc=oc, ts4=ts4, kc=kc, p=pbank:
                          E.matmul(
                              p, wbuf[:, kc * 1024 + oc * 128:
                                      kc * 1024 + (oc + 1) * 128],
                              OT[:, kc * 2048 + ts4 * 512:
                                 kc * 2048 + (ts4 + 1) * 512],
                              start=(kc == 0), stop=(kc == KCH - 1)),
                          sem=("m3D" if kc == KCH - 1 else None))
                pb.wait("vector", "m3D")
                pb.op("vector", lambda E, oc=oc, ts4=ts4, p=pbank:
                      E.tensor_scalar(
                          aT[:, oc * 2048 + ts4 * 512:
                             oc * 2048 + (ts4 + 1) * 512],
                          p, bo_t[:, oc:oc + 1], None, OP.add), sem="v3D")
        pb.wait("tensor", "v3D")
        for tt in range(16):
            pb.wait("vector", "v3x", pb.c("v3x"))
            for c in range(8):
                pslot = psT[:, (c % 2) * 1024:(c % 2) * 1024 + 128]
                pb.wait("tensor", "v3F", pb.c("v3F") - 1)
                pb.op("tensor", lambda E, c=c, tt=tt, p=pslot: E.transpose(
                    p, aT[:, c * 2048 + tt * 128:c * 2048 + (tt + 1) * 128],
                    ident[:]), sem="m3F")
                pb.wait("vector", "m3F")
                pb.op("vector", lambda E, c=c, p=pslot: E.tensor_copy(
                    atok[:, c * 128:(c + 1) * 128], p), sem="v3F")
            pb.wait("sync", "v3y", pb.c("v3y"))
            pb.wait("sync", "vln3", pb.c("vln3"))
            pb.dma("sync", x2tl[:], io["x2_in"][bi, tt * 128:(tt + 1) * 128, :],
                   "qx2")
            pb.wait("vector", "qx2")
            pb.wait("vector", "v3F")
            emit_ln_tile(pb, mv, stat, rstd, atok[:])
            pb.wait("vector", "act")
            pb.wait("vector", "qx3", pb.c("qx3"))
            pb.op("vector", lambda E: E.tensor_scalar(
                x3t[:], atok[:], mv[:, 0:1], rstd[:, 1:2],
                OP.subtract, OP.mult), sem="v3x")
            pb.op("vector", lambda E: E.tensor_tensor(
                x3t[:], x3t[:], x2tl[:], OP.add), sem="v3y")
            pb.wait("sync", "v3y")
            pb.dma("sync", io["x3_out"][bi, tt * 128:(tt + 1) * 128, :],
                   x3t[:], "qx3")
    pb.wait("sync", "qx3")


def prep_p3_weights(d):
    out = {}
    out["bq_row"] = np.ascontiguousarray(d["L1_attn_b_qkv"][0:1024]).reshape(1, 1024)
    out["bk_row"] = np.ascontiguousarray(d["L1_attn_b_qkv"][1024:2048]).reshape(1, 1024)
    out["bv_t"] = np.ascontiguousarray(d["L1_attn_b_qkv"][2048:3072].reshape(8, 128).T)
    out["bo_t"] = np.ascontiguousarray(d["L1_attn_b_out"].reshape(8, 128).T)
    return out


def build_p3():
    nc = bass.Bass("TRN2", target_bir_lowering=False)
    io = {}
    io["x2_in"] = nc.dram_tensor("x2_in", [NB, T, D], F32,
                                 kind="ExternalInput").ap()
    io["L1_attn_W_qkv"] = nc.dram_tensor("L1_attn_W_qkv", [D, 3072], F32,
                                         kind="ExternalInput").ap()
    io["L1_attn_W_out"] = nc.dram_tensor("L1_attn_W_out", [D, D], F32,
                                         kind="ExternalInput").ap()
    for nm, shp in [("bq_row", [1, 1024]), ("bk_row", [1, 1024]),
                    ("bv_t", [128, 8]), ("bo_t", [128, 8]),
                    ("ident", [128, 128]), ("cpool", [128, 8])]:
        io[nm] = nc.dram_tensor(nm, shp, F32, kind="ExternalInput").ap()
    io["x3_out"] = nc.dram_tensor("x3_out", [NB, T, D], F32,
                                  kind="ExternalOutput").ap()
    es = ExitStack()
    with es:
        pb = PB(nc, es)
        emit_p3(pb, es, nc, io)
        print("P3 instructions:", pb.n_instr())
        pb.finalize()
    return nc



P4NSUB = int(os.environ.get("P4NSUB", "2"))
P4OFF = [1, 1, 1, 1, 0, 0, 0, 0]


def emit_p4(pb, es, nc, io):
    U8 = mybir.dt.uint8
    ovl = sbt(es, nc, "p5_ovl", [128, 192768], U8)

    def view(off_b, ncols, dtype):
        size = ncols * (4 if dtype == F32 else 2)
        return ovl.ap()[:, off_b:off_b + size].bitcast(dtype)

    xcT = view(0, 16392, F32)
    wqkvsb = view(67584, 12288, BF16)
    wosb = view(67584, 4096, BF16)
    w1sb = view(67584, 22528, BF16)
    w2sb = view(67584, 22528, BF16)
    lnbf = view(174336, 4096, BF16)
    x3sb = view(135168, 1024, F32)
    x4sb = view(139264, 1024, F32)
    obf_ = view(75776, 512, BF16)
    recipb_ = view(76800, 512, BF16)
    Qt = view(92160, 8192, BF16)
    OT = view(92160, 8192, BF16)
    x3t = view(92160, 1024, F32)
    lntok = view(96256, 1024, BF16)
    Kt = view(108544, 8192, BF16)
    vg = view(124928, 8320, BF16)
    QT = view(141568, 8192, BF16)
    KT = view(157952, 8192, BF16)
    tqC = view(174336, 1024, BF16)
    tqS = view(176384, 1024, BF16)
    tkC = view(178432, 1024, BF16)
    tkS = view(180480, 1024, BF16)
    pTa = view(174336, 4096, BF16)
    masksb = view(182528, 4096, BF16)
    hbf = view(190720, 1024, BF16)
    w1ffin = view(112640, 16384, BF16)
    vsb = view(112640, 11264, BF16)
    sqbf = view(145408, 1024, BF16)
    hbf2 = view(147456, 4096, BF16)
    murow_ = view(155648, 512, BF16)
    rstdrow_ = view(156672, 512, BF16)
    r1_ = view(157696, 512, F32)
    r2_ = view(159744, 512, F32)
    gelb = view(161792, 1024, BF16)
    vbuf = view(163840, 1024, BF16)
    fqn = view(165888, 512, F32)
    murow = murow_[0:1, :]
    rstdrow = rstdrow_[0:1, :]
    r1 = r1_[0:1, :]
    r2 = r2_[0:1, :]

    cpool = (io["cpool_sb"] if "cpool_sb" in io else sbt(es, nc, "p5_cpool", [128, 8], F32))
    ident = sbt(es, nc, "p5_ident", [128, 128], BF16)
    ones1 = sbt(es, nc, "p5_ones1", [1, 128], BF16)
    onesc = sbt(es, nc, "p5_onesc", [128, 1], BF16)
    qsr = sbt(es, nc, "p5_qsr", [128, 256], F32)
    qksbf = sbt(es, nc, "p5_qksbf", [1, 256], BF16)
    ssb = sbt(es, nc, "p5_ssb", [128, 8], F32)
    rstds = sbt(es, nc, "p5_rstds", [128, 8], F32)
    junk = sbt(es, nc, "p5_junk", [128, 64], F32)
    b1t = sbt(es, nc, "p5_b1t", [128, 88], F32)
    b2t = sbt(es, nc, "p5_b2t", [128, 16], F32)
    lngb = sbt(es, nc, "p5_lngb", [128, 32], F32)
    stat = sbt(es, nc, "p5_stat", [128, 12], F32)
    mv = sbt(es, nc, "p5_mv", [128, 4], F32)
    rstd = sbt(es, nc, "p5_rstd", [128, 4], F32)
    recipb = recipb_
    obf = obf_

    psA = pst(es, nc, "p5_psA", [128, 2048], F32)
    psB = pst(es, nc, "p5_psB", [128, 1024], F32)
    psT = pst(es, nc, "p5_psT", [128, 2048], BF16)

    CVALS = [1e-5, 1.0, -0.5, 0.0, 0.5, 2.0, -1.0, 8.0]
    for i, v in enumerate(CVALS):
        nc.const_aps.register(F32, v, cpool[:, i:i + 1])

    vg4 = vg.rearrange("p (t h e) -> p t h e", h=8, e=65)

    pb.dma("sync", cpool[:], io["cpool"], "q5c")
    for e in ("scalar", "vector", "tensor"):
        pb.wait(e, "q5c")
    pb.dma("gpsimd", ident[:], io["ident"], "q5g")
    pb.dma("gpsimd", masksb[:], io["p4_masks"], "q5g")
    pb.dma("gpsimd", b1t[:], io["p4_b1t"], "q5g")
    pb.dma("gpsimd", b2t[:], io["p4_b2t"], "q5g")
    pb.dma("gpsimd", lngb[:], io["p4_lngb"], "q5g")
    pb.dma("gpsimd", qksbf[:], io["p4_qks"], "q5g")
    for e in ("scalar", "vector", "tensor"):
        pb.wait(e, "q5g")
    pb.op("vector", lambda E: E.memset(ones1[:], 1.0))
    pb.op("vector", lambda E: E.memset(onesc[:], 1.0), sem="v5one")
    pb.wait("tensor", "v5one")

    for bi in range(NB):
        pb.wait("sync", "v5vb", pb.c("v5vb"))
        pb.wait("sync", "v5w2", pb.c("v5w2"))
        pb.wait("sync", "v5wo", pb.c("v5wo"))
        pb.wait("tensor", "v5tp", pb.c("v5tp"))
        if pb.c("q5x4"):
            pb.wait("vector", "q5x4")
        def epi_ts(ts):
            if ts == 0:
                pb.wait("tensor", "a5gel", pb.c("a5gel"))
                pb.wait("tensor", "v5vb", pb.c("v5vb"))
                pb.wait("sync", "a5e", pb.c("a5e"))
            pb.wait("vector", "a5q2", pb.c("a5q2"))
            for c in range(8):
                pb.op("vector", lambda E, c=c, ts=ts: E.tensor_copy(
                    hbf2[:, c * 512:(c + 1) * 512],
                    xcT[:, c * 2049 + 1 + ts * 512:
                        c * 2049 + 1 + (ts + 1) * 512]),
                    sem=("v5h2" if c == 7 else None))
            pb.wait("tensor", "v5h2")
            pb.wait("scalar", "v5h2")
            pb.wait("tensor", "v5mu", pb.c("v5mu"))
            for c in range(8):
                sqs = sqbf[:, (c % 2) * 512:(c % 2) * 512 + 512]
                pb.wait("scalar", "m5s2", pb.c("m5s2") - 1)
                pb.op("scalar", lambda E, c=c, s=sqs: E.activation(
                    s, hbf2[:, c * 512:(c + 1) * 512], AF.Square),
                    sem="a5q2")
                pb.op("tensor", lambda E, c=c: E.matmul(
                    psA[0:1, 0:512], onesc[:, 0:1],
                    hbf2[:, c * 512:(c + 1) * 512],
                    start=(c == 0), stop=(c == 7)),
                    sem=("m5s1" if c == 7 else None))
                pb.wait("tensor", "a5q2")
                pb.op("tensor", lambda E, c=c, s=sqs: E.matmul(
                    psA[0:1, 512:1024], onesc[:, 0:1], s,
                    start=(c == 0), stop=(c == 7)), sem="m5s2")
            pb.wait("vector", "m5s1")
            pb.wait("vector", "m5s2")
            pb.op("vector", lambda E: E.tensor_scalar(
                r1[:], psA[0:1, 0:512], 1.0 / 1024.0, None, OP.mult))
            pb.op("vector", lambda E: E.tensor_copy(murow[:], r1[:]),
                  sem="v5mu")
            pb.op("vector", lambda E: E.tensor_scalar(
                r2[:], psA[0:1, 512:1024], 1.0 / 1024.0, None, OP.mult))
            pb.op("vector", lambda E: E.tensor_tensor(
                r1[:], r1[:], r1[:], OP.mult))
            pb.op("vector", lambda E: E.tensor_tensor(
                r2[:], r2[:], r1[:], OP.subtract), sem="v5var")
            pb.wait("scalar", "v5var")
            pb.op("scalar", lambda E: E.activation(
                r1[:], r2[:], AF.Ln, bias=LN_EPS))
            pb.op("scalar", lambda E: E.activation(
                r1[:], r1[:], AF.Exp, scale=-0.5), sem="a5sd")
            pb.wait("vector", "a5sd")
            pb.op("vector", lambda E: E.tensor_copy(rstdrow[:], r1[:]),
                  sem="v5mu")
            pb.wait("tensor", "v5mu")
            pb.wait("tensor", "v5ep", pb.c("v5ep"))
            pb.op("tensor", lambda E: E.matmul(
                psA[:, 1024:1536], ones1[0:1, 0:128], murow[:],
                start=True, stop=True))
            pb.op("tensor", lambda E: E.matmul(
                psA[:, 1536:2048], ones1[0:1, 0:128], rstdrow[:],
                start=True, stop=True), sem="m5bc")
            pb.wait("vector", "m5bc")
            for c in range(8):
                pb.op("vector", lambda E, c=c: E.tensor_tensor(
                    fqn[:], hbf2[:, c * 512:(c + 1) * 512],
                    psA[:, 1024:1536], OP.subtract))
                pb.op("vector", lambda E, c=c: E.tensor_tensor(
                    lnbf[:, c * 512:(c + 1) * 512], fqn[:],
                    psA[:, 1536:2048], OP.mult), sem="v5ep")
            pb.wait("tensor", "v5ep")
            for tl in range(4):
                tg = ts * 4 + tl
                pb.wait("sync", "v5tp", pb.c("v5tp"))
                pb.dma("sync", x3sb[:],
                       io["x3_in"][bi, tg * 128:(tg + 1) * 128, :], "q5x3")
                pb.wait("vector", "q5x3")
                if pb.c("q5x4"):
                    pb.wait("vector", "q5x4")
                for c in range(8):
                    pslot = psT[:, (c % 2) * 1024:(c % 2) * 1024 + 128]
                    pb.wait("tensor", "v5tp", pb.c("v5tp") - 1)
                    pb.op("tensor", lambda E, p=pslot, c=c, tl=tl:
                          E.transpose(
                              p, lnbf[:, c * 512 + tl * 128:
                                      c * 512 + (tl + 1) * 128],
                              ident[:]), sem="m5tp")
                    pb.wait("vector", "m5tp")
                    pb.op("vector", lambda E, p=pslot, c=c: E.tensor_tensor(
                        x4sb[:, c * 128:(c + 1) * 128],
                        x3sb[:, c * 128:(c + 1) * 128], p, OP.add),
                        sem="v5tp")
                pb.wait("sync", "v5tp")
                pb.dma("sync", io["x4_out"][bi, tg * 128:(tg + 1) * 128, :],
                       x4sb[:], "q5x4")

        xcg = xcT.rearrange("p (c n) -> p c n", n=2049)
        pb.op("vector", lambda E: E.memset(xcg[:, :, 0:1], 0.0), sem="v5pro")
        for tt in range(16):
            pb.wait("sync", "v5ln", pb.c("v5ln"))
            pb.dma("sync", x3t[:], io["x3_in"][bi, tt * 128:(tt + 1) * 128, :],
                   "q5x")
            pb.wait("vector", "q5x")
            emit_ln_tile(pb, mv, stat, rstd, x3t[:])
            pb.wait("vector", "act")
            pb.op("vector", lambda E: E.tensor_scalar(
                lntok[:], x3t[:], mv[:, 0:1], rstd[:, 1:2],
                OP.subtract, OP.mult), sem="v5ln")
            pb.wait("tensor", "v5ln")
            for c in range(8):
                pslot = psT[:, (c % 2) * 1024:(c % 2) * 1024 + 128]
                pb.wait("tensor", "v5pro", pb.c("v5pro") - 1)
                pb.op("tensor", lambda E, p=pslot, c=c: E.transpose(
                    p, lntok[:, c * 128:(c + 1) * 128], ident[:]), sem="m5pro")
                pb.wait("vector", "m5pro")
                pb.op("vector", lambda E, p=pslot, c=c, tt=tt: E.tensor_copy(
                    xcT[:, c * 2049 + 1 + tt * 128:c * 2049 + 1 + (tt + 1) * 128],
                    p), sem="v5pro")

        for sub in range(P4NSUB):
            pb.wait("gpsimd", "v5rot", pb.c("v5rot"))
            pb.wait("gpsimd", "v5w2", pb.c("v5w2"))
            pb.wait("gpsimd", "v5vb", pb.c("v5vb"))
            pb.wait("gpsimd", "v5qkv", pb.c("v5qkv"))
            pb.wait("gpsimd", "v5tp", pb.c("v5tp"))
            if pb.c("q5x4"):
                pb.wait("gpsimd", "q5x4")
            for kc in range(8):
                pb.dma("gpsimd", wqkvsb[:, kc * 1536:(kc + 1) * 1536],
                       io["L1_ffn_wqkv"][sub].rearrange(
                           "(k p) n -> k p n", p=128)[kc], "q5w")
            for i, tb in enumerate((tqC, tqS, tkC, tkS)):
                pb.dma("gpsimd", tb.rearrange("p (a d) -> p a d", d=64),
                       io["p4_rot"][i].rearrange("(a p) d -> p a d", p=128),
                       "q5rot")
            pb.wait("tensor", "v5w2", pb.c("v5w2"))
            pb.wait("tensor", "v5qsr", pb.c("v5qsr"))
            pb.op("tensor", lambda E, sub=sub: E.matmul(
                psB[:, 0:64], ones1[0:1, 0:128],
                qksbf[0:1, sub * 128:sub * 128 + 64], start=True, stop=True),
                sem="m5qs")
            pb.op("tensor", lambda E, sub=sub: E.matmul(
                psB[:, 64:128], ones1[0:1, 0:128],
                qksbf[0:1, sub * 128 + 64:sub * 128 + 128],
                start=True, stop=True), sem="m5qs")
            pb.wait("vector", "m5qs")
            for dst, src in [((0, 64), (0, 64)), ((64, 96), (32, 64)),
                             ((96, 128), (0, 32)), ((128, 192), (64, 128)),
                             ((192, 224), (96, 128)), ((224, 256), (64, 96))]:
                pb.op("vector", lambda E, dst=dst, src=src: E.tensor_copy(
                    qsr[:, dst[0]:dst[1]], psB[:, src[0]:src[1]]),
                    sem="v5qsr")
            pb.wait("vector", "q5rot")
            for tb, qof in ((tqC, 0), (tqS, 64), (tkC, 128), (tkS, 192)):
                t3 = tb.rearrange("p (a d) -> p a d", d=64)
                pb.op("vector", lambda E, t3=t3, qof=qof: E.tensor_tensor(
                    t3, t3,
                    qsr[:, qof:qof + 64].unsqueeze(1).broadcast_to(
                        (128, 16, 64)), OP.mult), sem="v5rot")

            pb.wait("vector", "v5pv", pb.c("v5pv"))
            pb.op("vector", lambda E: E.memset(vg[:], 1.0), sem="v5vg")
            pb.wait("tensor", "q5w")
            pb.wait("tensor", "a5e", pb.c("a5e"))
            pb.wait("tensor", "v5ffin", pb.c("v5ffin"))
            pb.wait("tensor", "v5ep", pb.c("v5ep"))
            pb.wait("tensor", "v5pro", pb.c("v5pro"))
            pb.wait("tensor", "v5vg")
            pb.wait("scalar", "v5rot", pb.c("v5rot"))

            def p4_transposes(tt):
                for src_, dstT in ((Qt, QT), (Kt, KT)):
                    for cp in range(4):
                        pslot = psT[:, (cp % 2) * 1024:(cp % 2) * 1024 + 128]
                        pb.wait("tensor", "v5T", pb.c("v5T") - 1)
                        pb.op("tensor", lambda E, p=pslot, src_=src_, tt=tt,
                              cp=cp: E.transpose(
                                  p, src_[:, tt * 512 + cp * 128:
                                          tt * 512 + (cp + 1) * 128],
                                  ident[:]), sem="m5T")
                        pb.wait("vector", "m5T")
                        pb.op("vector", lambda E, p=pslot, dstT=dstT, tt=tt,
                              cp=cp: E.tensor_copy(
                                  dstT[:, cp * 2048 + tt * 128:
                                       cp * 2048 + (tt + 1) * 128], p),
                              sem="v5T")

            for tt in range(16):
                for c in range(8):
                    pb.op("vector", lambda E, c=c, tt=tt: E.tensor_copy(
                        hbf[:, c * 128:(c + 1) * 128],
                        xcT[:, c * 2049 + P4OFF[c] + tt * 128:
                            c * 2049 + P4OFF[c] + (tt + 1) * 128]),
                        sem=("v5hbf" if c == 7 else None))
                pb.wait("tensor", "v5hbf")
                for j in range(3):
                    slot = ((tt * 3 + j) % 4) * 512
                    pbank = psA[:, slot:slot + 512]
                    pb.wait("tensor", "v5qkv", pb.c("v5qkv") - 3)
                    for kc in range(8):
                        pb.op("tensor", lambda E, j=j, kc=kc, p=pbank:
                              E.matmul(
                                  p, hbf[:, kc * 128:(kc + 1) * 128],
                                  wqkvsb[:, kc * 1536 + j * 512:
                                         kc * 1536 + (j + 1) * 512],
                                  start=(kc == 0), stop=(kc == 7)),
                              sem=("m5qkv" if kc == 7 else None))
                    pb.wait("vector", "m5qkv")
                    if j == 0:
                        pb.op("vector", lambda E, tt=tt, p=pbank:
                              E.tensor_copy(
                                  Qt[:, tt * 512:(tt + 1) * 512], p),
                              sem="v5qkv")
                    elif j == 1:
                        pb.op("vector", lambda E, tt=tt, p=pbank:
                              E.tensor_copy(
                                  Kt[:, tt * 512:(tt + 1) * 512], p),
                              sem="v5qkv")
                    else:
                        pb.op("vector", lambda E, tt=tt, p=pbank:
                              E.tensor_copy(
                                  vg4[:, tt, :, 0:64],
                                  p.rearrange("p (h d) -> p h d", d=64)),
                              sem="v5qkv")

                pb.wait("scalar", "v5qkv", pb.c("v5qkv") - 1)
                for src_, tC, tS in ((Qt, tqC, tqS), (Kt, tkC, tkS)):
                    tile = src_[:, tt * 512:(tt + 1) * 512]
                    t3 = tile.rearrange("p (h d) -> p h d", d=64)
                    pb.wait("scalar", "v5rt2", pb.c("v5rt2"))
                    for h in range(8):
                        pb.op("scalar", lambda E, tile=tile, h=h:
                              E.activation(
                                  junk[:], tile[:, h * 64:(h + 1) * 64],
                                  AF.Square, accum_out=ssb[:, h:h + 1]),
                              sem=("a5sq" if h == 7 else None))
                    pb.wait("vector", "a5sq")
                    pb.op("vector", lambda E: E.tensor_scalar(
                        ssb[:], ssb[:], 1e-24, None, OP.max), sem="v5mx")
                    pb.wait("scalar", "v5mx")
                    pb.op("scalar", lambda E: E.activation(
                        rstds[:], ssb[:], AF.Ln))
                    pb.op("scalar", lambda E: E.activation(
                        rstds[:], rstds[:], AF.Exp, scale=-0.5), sem="a5rs")
                    pb.wait("vector", "a5rs")
                    qn3 = psB[:, 0:512].rearrange("p (h d) -> p h d", d=64)
                    qrt23 = psB[:, 512:1024].rearrange(
                        "p (h d) -> p h d", d=64)
                    pb.op("vector", lambda E, t3=t3, qn3=qn3: E.tensor_tensor(
                        qn3, t3, rstds.ap().unsqueeze(2).broadcast_to(
                            (128, 8, 64)), OP.mult))
                    pb.op("vector", lambda E, qn3=qn3, qrt23=qrt23, tS=tS,
                          tt=tt: E.tensor_tensor(
                              qrt23[:, :, 0:32], qn3[:, :, 32:64],
                              tS[:, tt * 64:tt * 64 + 32].unsqueeze(1)
                              .broadcast_to((128, 8, 32)), OP.mult))
                    pb.op("vector", lambda E, qn3=qn3, qrt23=qrt23, tS=tS,
                          tt=tt: E.tensor_tensor(
                              qrt23[:, :, 32:64], qn3[:, :, 0:32],
                              tS[:, tt * 64 + 32:tt * 64 + 64].unsqueeze(1)
                              .broadcast_to((128, 8, 32)), OP.mult))
                    pb.op("vector", lambda E, qn3=qn3, tC=tC,
                          tt=tt: E.tensor_tensor(
                              qn3, qn3,
                              tC[:, tt * 64:(tt + 1) * 64].unsqueeze(1)
                              .broadcast_to((128, 8, 64)), OP.mult),
                          sem="v5rt2")
                    pb.op("vector", lambda E, tile=tile: E.tensor_copy(
                        tile, psB[:, 512:1024]))
                    pb.op("vector", lambda E, tile=tile: E.tensor_tensor(
                        tile, tile, psB[:, 0:512], OP.add), sem="v5rot")

                if tt >= 1:
                    pb.wait("tensor", "v5rot", pb.c("v5rot") - 2)
                    p4_transposes(tt - 1)
            pb.wait("tensor", "v5rot")
            p4_transposes(15)

            if "dbg_q" in io and sub == 0 and bi == 0:
                pb.wait("sync", "v5rot", pb.c("v5rot"))
                pb.dma("sync", io["dbg_q"], Qt, "q5dbg")
                pb.dma("sync", io["dbg_k"], Kt, "q5dbg")
                pb.wait("sync", "q5dbg")
                for e2 in ("vector", "scalar", "tensor", "gpsimd"):
                    pb.wait(e2, "q5dbg")

            pb.wait("tensor", "v5T")
            pb.wait("scalar", "v5T", pb.c("v5T"))
            pb.wait("scalar", "v5msk", pb.c("v5msk"))
            pb.wait("vector", "v5vg", pb.c("v5vg"))
            for h in range(8):
                cp = h // 2
                hrow = (h % 2) * 64
                for w in range(4):
                    kts = list(range(4, 8)) if w == 0 else list(range(8))
                    pb.wait("scalar", "v5o", pb.c("v5o"))
                    for kt in kts:
                        ktg = 4 * (w - 1) + kt
                        slot = psA[:, (kt % 4) * 512:(kt % 4) * 512 + 512]
                        pb.wait("tensor", "a5e", pb.c("a5e") - 3)
                        pb.op("tensor", lambda E, hrow=hrow, cp=cp, ktg=ktg,
                              w=w, s=slot: E.matmul(
                                  s, KT[hrow:hrow + 64,
                                        cp * 2048 + ktg * 128:
                                        cp * 2048 + (ktg + 1) * 128],
                                  QT[hrow:hrow + 64,
                                     cp * 2048 + w * 512:
                                     cp * 2048 + (w + 1) * 512],
                                  start=True, stop=True), sem="m5sc")
                        pb.wait("scalar", "m5sc")
                        pb.op("scalar", lambda E, kt=kt, s=slot:
                              E.activation(
                                  pTa[:, kt * 512:(kt + 1) * 512], s,
                                  AF.Exp, scale=8.0), sem="a5e")
                        pb.wait("vector", "a5e")
                        pb.op("vector", lambda E, kt=kt: E.tensor_tensor(
                            pTa[:, kt * 512:(kt + 1) * 512],
                            pTa[:, kt * 512:(kt + 1) * 512],
                            masksb[:, kt * 512:(kt + 1) * 512], OP.mult),
                            sem="v5msk")
                    pb.wait("tensor", "v5msk")
                    pb.wait("tensor", "v5ob", pb.c("v5ob"))
                    pb.wait("tensor", "a5rc", pb.c("a5rc"))
                    for i, kt in enumerate(kts):
                        ktg = 4 * (w - 1) + kt
                        pb.op("tensor", lambda E, ktg=ktg, h=h, kt=kt,
                              last=(i == len(kts) - 1), first=(i == 0):
                              E.matmul(
                                  psB[0:65, 0:512], vg4[:, ktg, h, :],
                                  pTa[:, kt * 512:(kt + 1) * 512],
                                  start=first, stop=last),
                              sem=("m5pv" if i == len(kts) - 1 else None))
                    pb.wait("vector", "m5pv")
                    pb.wait("scalar", "m5pv")
                    pb.op("scalar", lambda E: E.activation(
                        psB[64:65, 512:1024], psB[64:65, 0:512], AF.Ln))
                    pb.op("scalar", lambda E: E.activation(
                        recipb[0:1, :], psB[64:65, 512:1024], AF.Exp,
                        scale=-1.0), sem="a5rc")
                    pb.op("vector", lambda E: E.tensor_copy(
                        obf[0:64, :], psB[0:64, 0:512]), sem="v5ob")
                    pb.wait("tensor", "a5rc")
                    pb.wait("tensor", "v5o", pb.c("v5o") - 1)
                    pb.op("tensor", lambda E: E.matmul(
                        psB[0:64, 512:1024], ones1[0:1, 0:64], recipb[0:1, :],
                        start=True, stop=True), sem="m5rc")
                    pb.wait("vector", "m5rc")
                    pb.op("vector", lambda E, hrow=hrow, cp=cp, w=w:
                          E.scalar_tensor_tensor(
                              OT[hrow:hrow + 64,
                                 cp * 2048 + w * 512:cp * 2048 + (w + 1) * 512],
                              obf[0:64, :], 1.0, psB[0:64, 512:1024],
                              OP.mult, OP.mult), sem="v5o")

            pb.wait("gpsimd", "v5qkv", pb.c("v5qkv"))
            pb.wait("gpsimd", "v5rot", pb.c("v5rot"))
            for ic in range(4):
                pb.dma("gpsimd", wosb[:, ic * 1024:(ic + 1) * 1024],
                       io["L1_ffn_wo"][sub].rearrange(
                           "(k p) n -> k p n", p=128)[ic], "q5wo")
            pb.wait("tensor", "q5wo")
            pb.wait("tensor", "v5o")
            for oc in range(8):
                for ts4 in range(4):
                    pbank = psB[:, (ts4 % 2) * 512:(ts4 % 2) * 512 + 512]
                    pb.wait("tensor", "v5wo", pb.c("v5wo") - 1)
                    for ic in range(4):
                        pb.op("tensor", lambda E, oc=oc, ts4=ts4, ic=ic,
                              p=pbank: E.matmul(
                                  p, wosb[:, ic * 1024 + oc * 128:
                                          ic * 1024 + (oc + 1) * 128],
                                  OT[:, ic * 2048 + ts4 * 512:
                                     ic * 2048 + (ts4 + 1) * 512],
                                  start=(ic == 0), stop=(ic == 3)),
                              sem=("m5wo" if ic == 3 else None))
                    pb.wait("vector", "m5wo")
                    pb.op("vector", lambda E, oc=oc, ts4=ts4, p=pbank:
                          E.tensor_tensor(
                              xcT[:, oc * 2049 + 1 + ts4 * 512:
                                  oc * 2049 + 1 + (ts4 + 1) * 512],
                              xcT[:, oc * 2049 + 1 + ts4 * 512:
                                  oc * 2049 + 1 + (ts4 + 1) * 512],
                              p, OP.add), sem="v5wo")

            if "dbg_xc" in io and sub == 0 and bi == 0:
                pb.wait("sync", "v5wo", pb.c("v5wo"))
                pb.dma("sync", io["dbg_xc"], xcT, "q5dbg2")
                pb.wait("sync", "q5dbg2")
                for e2 in ("vector", "scalar", "tensor", "gpsimd"):
                    pb.wait(e2, "q5dbg2")

            pb.wait("gpsimd", "v5wo", pb.c("v5wo"))
            for kc in range(8):
                pb.dma("gpsimd", w1sb[:, kc * 2816:(kc + 1) * 2816],
                       io["p4_w1t"][sub].rearrange(
                           "(k p) n -> k p n", p=128)[kc][:, 0:2816], "q5w1")
            pb.wait("tensor", "v5wo")
            pb.wait("tensor", "a5e", pb.c("a5e"))
            pb.wait("scalar", "v5wo", pb.c("v5wo"))
            gfc_box = [0]

            def ffn_stats_ts(ts):
                pb.wait("vector", "a5q2", pb.c("a5q2"))
                for c in range(8):
                    pb.op("vector", lambda E, c=c, ts=ts: E.tensor_copy(
                        hbf2[:, c * 512:(c + 1) * 512],
                        xcT[:, c * 2049 + P4OFF[c] + ts * 512:
                            c * 2049 + P4OFF[c] + (ts + 1) * 512]),
                        sem=("v5h2" if c == 7 else None))
                pb.wait("tensor", "v5h2")
                pb.wait("scalar", "v5h2")
                pb.wait("tensor", "v5mu", pb.c("v5mu"))
                pb.wait("tensor", "v5ffin", pb.c("v5ffin"))
                for c in range(8):
                    sqs = sqbf[:, (c % 2) * 512:(c % 2) * 512 + 512]
                    pb.wait("scalar", "m5s2", pb.c("m5s2") - 1)
                    pb.op("scalar", lambda E, c=c, ts=ts, s=sqs:
                          E.activation(
                              s, hbf2[:, c * 512:(c + 1) * 512], AF.Square),
                          sem="a5q2")
                    pb.op("tensor", lambda E, c=c: E.matmul(
                        psB[0:1, 0:512], onesc[:, 0:1],
                        hbf2[:, c * 512:(c + 1) * 512],
                        start=(c == 0), stop=(c == 7)),
                        sem=("m5s1" if c == 7 else None))
                    pb.wait("tensor", "a5q2")
                    pb.op("tensor", lambda E, c=c, s=sqs: E.matmul(
                        psB[0:1, 512:1024], onesc[:, 0:1], s,
                        start=(c == 0), stop=(c == 7)), sem="m5s2")
                pb.wait("vector", "m5s1")
                pb.wait("vector", "m5s2")
                pb.op("vector", lambda E: E.tensor_scalar(
                    r1[:], psB[0:1, 0:512], 1.0 / 1024.0, None, OP.mult))
                pb.op("vector", lambda E: E.tensor_copy(murow[:], r1[:]),
                      sem="v5mu")
                pb.op("vector", lambda E: E.tensor_scalar(
                    r2[:], psB[0:1, 512:1024], 1.0 / 1024.0, None, OP.mult))
                pb.op("vector", lambda E: E.tensor_tensor(
                    r1[:], r1[:], r1[:], OP.mult))
                pb.op("vector", lambda E: E.tensor_tensor(
                    r2[:], r2[:], r1[:], OP.subtract), sem="v5var")
                pb.wait("scalar", "v5var")
                pb.op("scalar", lambda E: E.activation(
                    r1[:], r2[:], AF.Ln, bias=LN_EPS))
                pb.op("scalar", lambda E: E.activation(
                    r1[:], r1[:], AF.Exp, scale=-0.5), sem="a5sd")
                pb.wait("vector", "a5sd")
                pb.op("vector", lambda E: E.tensor_copy(rstdrow[:], r1[:]),
                      sem="v5mu")
                pb.wait("tensor", "v5mu")
                pb.wait("tensor", "v5ffin", pb.c("v5ffin"))
                pb.op("tensor", lambda E: E.matmul(
                    psB[:, 0:512], ones1[0:1, 0:128], murow[:],
                    start=True, stop=True))
                pb.op("tensor", lambda E: E.matmul(
                    psB[:, 512:1024], ones1[0:1, 0:128], rstdrow[:],
                    start=True, stop=True), sem="m5bc")
                pb.wait("vector", "m5bc")
                for c in range(8):
                    pb.op("vector", lambda E, c=c: E.tensor_tensor(
                        fqn[:], hbf2[:, c * 512:(c + 1) * 512],
                        psB[:, 0:512], OP.subtract))
                    pb.op("vector", lambda E: E.tensor_tensor(
                        fqn[:], fqn[:], psB[:, 512:1024], OP.mult))
                    pb.op("vector", lambda E, c=c, ts=ts, sub=sub:
                          E.tensor_scalar(
                              w1ffin[:, c * 2048 + ts * 512:
                                     c * 2048 + (ts + 1) * 512],
                              fqn[:], lngb[:, sub * 16 + c:sub * 16 + c + 1],
                              lngb[:, sub * 16 + 8 + c:sub * 16 + 8 + c + 1],
                              OP.mult, OP.add), sem="v5ffin")


            def w1_ts(fcs, ts):
                for fc in fcs:
                    fcl = gfc_box[0]
                    gfc_box[0] += 1
                    m = 42 if fc == 21 else 128
                    pcol = (fc - fcs[0]) * 256
                    sA = psA[0:m, (fcl % 2) * 512:(fcl % 2) * 512 + 512]
                    sG = psA[0:m, 1024 + (fcl % 2) * 512:
                             1024 + (fcl % 2) * 512 + 512]
                    pb.wait("tensor", "v5vb", pb.c("v5vb") - 1)
                    pb.wait("tensor", "a5gel", pb.c("a5gel") - 1)
                    for kc in range(8):
                        pb.op("tensor", lambda E, kc=kc, pcol=pcol, m=m,
                              ts=ts, s=sA: E.matmul(
                                  s, w1sb[:, kc * 2816 + pcol:
                                          kc * 2816 + pcol + m],
                                  w1ffin[:, kc * 2048 + ts * 512:
                                         kc * 2048 + (ts + 1) * 512],
                                  start=(kc == 0), stop=(kc == 7)),
                              sem=("m5w1a" if kc == 7 else None))
                    for kc in range(8):
                        pb.op("tensor", lambda E, kc=kc, pcol=pcol, m=m,
                              ts=ts, s=sG: E.matmul(
                                  s, w1sb[:, kc * 2816 + pcol + 128:
                                          kc * 2816 + pcol + 128 + m],
                                  w1ffin[:, kc * 2048 + ts * 512:
                                         kc * 2048 + (ts + 1) * 512],
                                  start=(kc == 0), stop=(kc == 7)),
                              sem=("m5w1g" if kc == 7 else None))
                    gs = gelb[:, (fcl % 2) * 512:(fcl % 2) * 512 + 512]
                    pb.wait("scalar", "m5w1g")
                    pb.wait("scalar", "v5vb", pb.c("v5vb") - 1)
                    pb.op("scalar", lambda E, m=m, fc=fc, sub=sub, s=sG,
                          g=gs: E.activation(
                              g[0:m, :], s, AF.Gelu,
                              bias=b1t[0:m, sub * 44 + 2 * fc + 1:
                                       sub * 44 + 2 * fc + 2]),
                          sem="a5gel")
                    vs = vbuf[:, (fcl % 2) * 512:(fcl % 2) * 512 + 512]
                    pb.wait("vector", "a5gel")
                    pb.wait("vector", "m5w1a")
                    if pb.c("v5vb") >= 2:
                        pb.wait("vector", "q5vT")
                    pb.op("vector", lambda E, m=m, fc=fc, sub=sub, s=sA,
                          g=gs, v=vs: E.scalar_tensor_tensor(
                              v[0:m, :], s,
                              b1t[0:m, sub * 44 + 2 * fc:
                                  sub * 44 + 2 * fc + 1],
                              g[0:m, :], OP.add, OP.mult), sem="v5vb")
                    pb.wait("sync", "v5vb")
                    pb.dma("sync",
                           io["vT_dram"][fc * 128:fc * 128 + m,
                                         ts * 512:(ts + 1) * 512],
                           vs[0:m, :], "q5vT")


            fcs0 = list(range(11))
            for ts in range(4):
                ffn_stats_ts(ts)
                pb.wait("tensor", "v5ffin")
                if ts == 0:
                    pb.wait("tensor", "q5w1")
                w1_ts(fcs0, ts)
            pb.wait("gpsimd", "v5vb", pb.c("v5vb"))
            for kc in range(8):
                pb.dma("gpsimd", w1sb[:, kc * 2816:(kc + 1) * 2816],
                       io["p4_w1t"][sub].rearrange(
                           "(k p) n -> k p n", p=128)[kc][:, 2816:5632],
                       "q5w1")
            pb.wait("tensor", "q5w1")
            fcs1 = list(range(11, 22))
            for ts in range(4):
                w1_ts(fcs1, ts)

            pb.wait("gpsimd", "v5vb", pb.c("v5vb"))
            for fc in range(22):
                pb.dma("gpsimd", w2sb[:, fc * 1024:(fc + 1) * 1024],
                       io["p4_w2t"][sub].rearrange(
                           "(k p) n -> k p n", p=128)[fc], "q5w2")
            pb.wait("sync", "q5vT")
            pb.wait("tensor", "v5ffin", pb.c("v5ffin"))
            for ts in range(4):
                if ts > 0:
                    pb.wait("sync", "v5w2", pb.c("v5w2"))
                for fc in range(22):
                    m = 42 if fc == 21 else 128
                    pb.dma("sync", vsb[0:m, fc * 512:fc * 512 + 512],
                           io["vT_dram"][fc * 128:fc * 128 + m,
                                         ts * 512:(ts + 1) * 512], "q5v")
                pb.wait("tensor", "q5v")
                if ts == 0:
                    pb.wait("tensor", "q5w2")
                for oc in range(8):
                    pbank = psB[:, (oc % 2) * 512:(oc % 2) * 512 + 512]
                    pb.wait("tensor", "v5w2", pb.c("v5w2") - 1)
                    for fc in range(22):
                        m = 42 if fc == 21 else 128
                        pb.op("tensor", lambda E, fc=fc, oc=oc, m=m,
                              p=pbank: E.matmul(
                                  p, w2sb[0:m, fc * 1024 + oc * 128:
                                          fc * 1024 + (oc + 1) * 128],
                                  vsb[0:m, fc * 512:(fc + 1) * 512],
                                  start=(fc == 0), stop=(fc == 21)),
                              sem=("m5w2" if fc == 21 else None))
                    pb.wait("vector", "m5w2")
                    pb.op("vector", lambda E, oc=oc, ts=ts, sub=sub,
                          p=pbank: E.scalar_tensor_tensor(
                              xcT[:, oc * 2049 + 1 + ts * 512:
                                  oc * 2049 + 1 + (ts + 1) * 512],
                              p, b2t[:, sub * 8 + oc:sub * 8 + oc + 1],
                              xcT[:, oc * 2049 + 1 + ts * 512:
                                  oc * 2049 + 1 + (ts + 1) * 512],
                              OP.add, OP.add), sem="v5w2")
                if sub == P4NSUB - 1:
                    epi_ts(ts)

    pb.wait("sync", "q5x4")


def prep_p4_weights(d):
    out = {}
    w1 = d["L1_ffn_w1"]
    w1t = np.zeros((2, 1024, 5632), np.float32)
    b1t = np.zeros((128, 88), np.float32)
    for s in range(2):
        for fc in range(22):
            m = min(128, 2730 - fc * 128)
            w1t[s, :, fc * 256:fc * 256 + m] = w1[s][:, fc * 128:fc * 128 + m]
            w1t[s, :, fc * 256 + 128:fc * 256 + 128 + m] = \
                w1[s][:, 2730 + fc * 128:2730 + fc * 128 + m]
            b1t[0:m, s * 44 + 2 * fc] = d["L1_ffn_b1"][s][fc * 128:
                                                          fc * 128 + m]
            b1t[0:m, s * 44 + 2 * fc + 1] = d["L1_ffn_b1"][s][
                2730 + fc * 128:2730 + fc * 128 + m]
    out["p4_w1t"] = w1t
    out["p4_b1t"] = b1t
    w2t = np.zeros((2, 2816, 1024), np.float32)
    w2t[:, 0:2730] = d["L1_ffn_w2"]
    out["p4_w2t"] = w2t
    b2t = np.zeros((128, 16), np.float32)
    lngb = np.zeros((128, 32), np.float32)
    for s in range(2):
        b2t[:, s * 8:(s + 1) * 8] = d["L1_ffn_b2"][s].reshape(8, 128).T
        lngb[:, s * 16:s * 16 + 8] = d["L1_ffn_ln_g"][s].reshape(8, 128).T
        lngb[:, s * 16 + 8:s * 16 + 16] = \
            d["L1_ffn_ln_b"][s].reshape(8, 128).T
    out["p4_b2t"] = b2t
    out["p4_lngb"] = lngb
    qks = np.zeros((1, 256), np.float32)
    for s in range(2):
        qks[0, s * 128:s * 128 + 64] = d["L1_ffn_q_scale"][s]
        qks[0, s * 128 + 64:s * 128 + 128] = d["L1_ffn_k_scale"][s]
    out["p4_qks"] = qks
    return out


def make_p4_consts():
    half_idx = np.arange(0, 64, 2, dtype=np.float64)
    inv_freq = 1.0 / 10000.0 ** (half_idx / 64.0)
    t = np.arange(2048, dtype=np.float64)
    freqs = t[:, None] * inv_freq[None, :]
    freqs = np.concatenate([freqs, freqs], -1)
    base = (half_idx + 0.4 * 64.0) / (1.4 * 64.0)
    power = (t - 1024.0) / 256.0
    scale = base[None, :] ** power[:, None]
    scale = np.concatenate([scale, scale], -1)
    sign = np.concatenate([-np.ones(32), np.ones(32)])[None, :]
    rot = np.stack([np.cos(freqs) * scale, np.sin(freqs) * scale * sign,
                    np.cos(freqs) / scale,
                    np.sin(freqs) / scale * sign]).astype(np.float32)
    mk = np.zeros((128, 8, 512), np.float32)
    p = np.arange(128)[:, None]
    il = np.arange(512)[None, :]
    for kt in range(4):
        mk[:, kt] = (il <= kt * 128 + p)
    for kt in range(4, 8):
        mk[:, kt] = (il >= (kt - 4) * 128 + p)
    return {"p4_rot": rot, "p4_masks": mk.reshape(128, 4096)}


def build_p4():
    nc = bass.Bass("TRN2", target_bir_lowering=False)
    io = {}
    io["x3_in"] = nc.dram_tensor("x3_in", [NB, T, D], F32,
                                 kind="ExternalInput").ap()
    io["L1_ffn_wqkv"] = nc.dram_tensor("L1_ffn_wqkv", [2, D, 1536], F32,
                                       kind="ExternalInput").ap()
    io["L1_ffn_wo"] = nc.dram_tensor("L1_ffn_wo", [2, 512, D], F32,
                                     kind="ExternalInput").ap()
    for nm, shp in [("p4_w1t", [2, 1024, 5632]), ("p4_w2t", [2, 2816, 1024]),
                    ("p4_b1t", [128, 88]), ("p4_b2t", [128, 16]),
                    ("p4_lngb", [128, 32]), ("p4_qks", [1, 256]),
                    ("p4_rot", [4, 2048, 64]), ("p4_masks", [128, 4096]),
                    ("ident", [128, 128]), ("cpool", [128, 8])]:
        io[nm] = nc.dram_tensor(nm, shp, F32, kind="ExternalInput").ap()
    io["vT_dram"] = nc.dram_tensor("vT_dram", [2816, 2048], BF16).ap()
    io["x4_out"] = nc.dram_tensor("x4_out", [NB, T, D], F32,
                                  kind="ExternalOutput").ap()
    if os.environ.get("P4DBG", "0") == "1":
        io["dbg_q"] = nc.dram_tensor("dbg_q", [128, 8192], BF16,
                                     kind="ExternalOutput").ap()
        io["dbg_k"] = nc.dram_tensor("dbg_k", [128, 8192], BF16,
                                     kind="ExternalOutput").ap()
        io["dbg_xc"] = nc.dram_tensor("dbg_xc", [128, 16392], F32,
                                      kind="ExternalOutput").ap()
    es = ExitStack()
    with es:
        pb = PB(nc, es)
        emit_p4(pb, es, nc, io)
        print("P4 instructions:", pb.n_instr())
        pb.finalize()
    return nc


def build_launchA():
    nc = bass.Bass("TRN2", target_bir_lowering=False)
    io = {}
    io["x_in"] = nc.dram_tensor("x_in", [NB, T, D], F32,
                                kind="ExternalInput").ap()
    for nm, shp in P1_WNAMES:
        io[nm] = nc.dram_tensor(nm, shp, F32, kind="ExternalInput").ap()
    io["ident"] = nc.dram_tensor("ident", [128, 128], F32,
                                 kind="ExternalInput").ap()
    io["sel16"] = nc.dram_tensor("sel16", [48, 16 * 128], F32,
                                 kind="ExternalInput").ap()
    io["cpool"] = nc.dram_tensor("cpool", [128, 8], F32,
                                 kind="ExternalInput").ap()
    for nm, shp in [("xpw_t", [128, 16 * 96]), ("convw_t", [128, 64]),
                    ("convb_t", [128, 16]), ("dtb_t", [128, 16]),
                    ("a_t", [128, 256]), ("d_t", [128, 16])]:
        io[nm] = nc.dram_tensor(nm, shp, F32, kind="ExternalInput").ap()
    io["z_dram"] = nc.dram_tensor("z_dram", [NB, NCHUNK, 128, T], BF16).ap()
    io["yg_dram"] = nc.dram_tensor("yg_dram", [NB, NCHUNK, 128, T], BF16).ap()
    io["x1_out"] = nc.dram_tensor("x1_out", [NB, T, D], F32,
                                  kind="ExternalOutput").ap()
    for nm, shp in P2_WNAMES:
        io[nm] = nc.dram_tensor(nm, shp, F32, kind="ExternalInput").ap()
    for nm, shp in [("bq_t", [128, 8]), ("bk_t", [128, 8]),
                    ("bo_t", [128, 8]), ("wr_t", [128, 64])]:
        io[nm] = nc.dram_tensor(nm, shp, F32, kind="ExternalInput").ap()
    io["x1b_out"] = nc.dram_tensor("x1b_out", [NB, T, D], F32,
                                   kind="ExternalOutput").ap()
    io["h2_out"] = nc.dram_tensor("h2_out", [NB, T, D], BF16,
                                  kind="ExternalOutput").ap()
    io["logits_out"] = nc.dram_tensor("logits_out", [NB, T, 8], F32,
                                      kind="ExternalOutput").ap()
    io["cnt_out"] = nc.dram_tensor("cnt_out", [8, 2], F32,
                                   kind="ExternalOutput").ap()
    es = ExitStack()
    with es:
        pb = PB(nc, es, alias=True)
        io["cpool_sb"] = sbt(es, nc, "cpool_shared", [128, 8], F32)
        with ExitStack() as es1:
            emit_p1(pb, es1, nc, io)
        pb.barrier("a1")
        io2 = dict(io)
        io2["x1_in"] = io["x1_out"]
        with ExitStack() as es2:
            emit_p2pre(pb, es2, nc, io2)
        print("LaunchA instructions:", pb.n_instr())
        pb.finalize()
    return nc


def build_launchB():
    nc = bass.Bass("TRN2", target_bir_lowering=False,
                   dynamic_dma_scratch_size=32768)
    io = {}
    io["x1_in"] = nc.dram_tensor("x1_in", [NB, T, D], F32,
                                 kind="ExternalInput").ap()
    io["x1b_in"] = nc.dram_tensor("x1b_in", [NB, T, D], F32,
                                  kind="ExternalInput").ap()
    io["h2_in"] = nc.dram_tensor("h2_in", [NB, T, D], BF16,
                                 kind="ExternalInput").ap()
    io["logits_in"] = nc.dram_tensor("logits_in", [NB, T, 8], F32,
                                     kind="ExternalInput").ap()
    io["cnt_all"] = nc.dram_tensor("cnt_all", [128, 1], F32,
                                   kind="ExternalInput").ap()
    io["L0_ffn_w1"] = nc.dram_tensor("L0_ffn_w1", [8, D, 4096], F32,
                                     kind="ExternalInput").ap()
    io["L0_ffn_w2"] = nc.dram_tensor("L0_ffn_w2", [8, 4096, D], F32,
                                     kind="ExternalInput").ap()
    for nm, shp in [("b1_t", [8, 128, 32]), ("b2_t", [8, 128, 8]),
                    ("Mmask", [128, 8]), ("Mmask2", [128, 8]),
                    ("ident", [128, 128]), ("cpool", [128, 8])]:
        io[nm] = nc.dram_tensor(nm, shp, F32, kind="ExternalInput").ap()
    io["tokmap"] = nc.dram_tensor("tokmap", [128 * 96, 2], U32).ap()
    io["w1bf"] = nc.dram_tensor("w1bf", [8, 8, 128, 4096], BF16).ap()
    io["w2bf"] = nc.dram_tensor("w2bf", [8, 32, 128, 1024], BF16).ap()
    io["yedram"] = nc.dram_tensor("yedram", [8 * KCAP, D], BF16).ap()
    io["x2h"] = nc.dram_tensor("x2h", [NB, T, D], F32).ap()
    io["L1_attn_W_qkv"] = nc.dram_tensor("L1_attn_W_qkv", [D, 3072], F32,
                                         kind="ExternalInput").ap()
    io["L1_attn_W_out"] = nc.dram_tensor("L1_attn_W_out", [D, D], F32,
                                         kind="ExternalInput").ap()
    for nm, shp in [("bq_row", [1, 1024]), ("bk_row", [1, 1024]),
                    ("bv_t", [128, 8]), ("bo_t", [128, 8])]:
        io[nm] = nc.dram_tensor(nm, shp, F32, kind="ExternalInput").ap()
    io["x3h"] = nc.dram_tensor("x3h", [NB, T, D], F32).ap()
    io["L1_ffn_wqkv"] = nc.dram_tensor("L1_ffn_wqkv", [2, D, 1536], F32,
                                       kind="ExternalInput").ap()
    io["L1_ffn_wo"] = nc.dram_tensor("L1_ffn_wo", [2, 512, D], F32,
                                     kind="ExternalInput").ap()
    for nm, shp in [("p4_w1t", [2, 1024, 5632]), ("p4_w2t", [2, 2816, 1024]),
                    ("p4_b1t", [128, 88]), ("p4_b2t", [128, 16]),
                    ("p4_lngb", [128, 32]), ("p4_qks", [1, 256]),
                    ("p4_rot", [4, 2048, 64]), ("p4_masks", [128, 4096])]:
        io[nm] = nc.dram_tensor(nm, shp, F32, kind="ExternalInput").ap()
    io["vT_dram"] = nc.dram_tensor("vT_dram", [2816, 2048], BF16).ap()
    io["x4_out"] = nc.dram_tensor("x4_out", [NB, T, D], F32,
                                  kind="ExternalOutput").ap()
    es = ExitStack()
    with es:
        pb = PB(nc, es, alias=True)
        io["cpool_sb"] = sbt(es, nc, "cpool_shared", [128, 8], F32)
        io2 = dict(io)
        io2["x2_out"] = io["x2h"]
        with ExitStack() as es1:
            emit_p2post(pb, es1, nc, io2)
        pb.barrier("b1")
        io3 = dict(io)
        io3["x2_in"] = io["x2h"]
        io3["x3_out"] = io["x3h"]
        with ExitStack() as es2:
            emit_p3(pb, es2, nc, io3)
        pb.barrier("b2")
        io4 = dict(io)
        io4["x3_in"] = io["x3h"]
        with ExitStack() as es3:
            emit_p4(pb, es3, nc, io4)
        print("LaunchB instructions:", pb.n_instr())
        pb.finalize()
    return nc


def kernel(**inputs):
    d = {k: np.asarray(v) for k, v in inputs.items()}
    x0 = d["L0_attn_x"]
    csts = make_consts()
    CORES = list(range(8))

    ncA = build_launchA()
    sh = {nm: d[nm] for nm, _ in P1_WNAMES}
    sh.update(prep_p1_weights(d))
    sh.update(csts)
    for nm in ["L0_ffn_wq", "L0_ffn_wk", "L0_ffn_wv", "L0_ffn_wo"]:
        sh[nm] = d[nm]
    sh.update(prep_p2_weights(d))
    insA = [dict(sh, x_in=np.ascontiguousarray(x0[2 * c:2 * c + 2]))
            for c in CORES]
    rA = run_bass_kernel_spmd(ncA, insA, core_ids=CORES)

    cnt_all = np.ascontiguousarray(
        np.stack([rA.results[c]["cnt_out"].T for c in CORES])
        .astype(np.float32).reshape(128, 1))

    ncB = build_launchB()
    sh = {"L0_ffn_w1": d["L0_ffn_w1"], "L0_ffn_w2": d["L0_ffn_w2"],
          "ident": csts["ident"], "cpool": csts["cpool"],
          "cnt_all": cnt_all,
          "L1_attn_W_qkv": d["L1_attn_W_qkv"],
          "L1_attn_W_out": d["L1_attn_W_out"],
          "L1_ffn_wqkv": d["L1_ffn_wqkv"], "L1_ffn_wo": d["L1_ffn_wo"]}
    sh.update(prep_p3_weights(d))
    sh.update(prep_p4_weights(d))
    sh.update(make_p4_consts())
    insB = []
    for c in CORES:
        m = dict(sh, x1_in=rA.results[c]["x1_out"],
                 x1b_in=rA.results[c]["x1b_out"],
                 h2_in=rA.results[c]["h2_out"],
                 logits_in=rA.results[c]["logits_out"])
        m.update(prep_p2post_weights(d, core=c))
        insB.append(m)
    rB = run_bass_kernel_spmd(ncB, insB, core_ids=CORES)
    out = np.concatenate([rB.results[c]["x4_out"] for c in CORES], axis=0)
    return np.ascontiguousarray(out.astype(np.float32))
